# version 11
# baseline (speedup 1.0000x reference)
import numpy as np
import ml_dtypes
import concourse.bass as bass
import concourse.mybir as mybir
from concourse.bass_utils import run_bass_kernel_spmd

F32 = mybir.dt.float32
BF16 = mybir.dt.bfloat16
U8 = mybir.dt.uint8
ALU = mybir.AluOpType
AF = mybir.ActivationFunctionType
AX = mybir.AxisListType

NCORES = 8
B_LOC = 2
T = 2048
TT = 512
NTPS = T // TT
NTILE = B_LOC * NTPS
D = 1024
NCH = 8
NH = 4
DK = 128
DV = 256
C = 64
NCT = TT // C
EPS = 1e-5
MAGIC = 12582912.0

ENGS = ("sy", "ve", "sc", "te", "gp")
PARANOID = [False]
AUDIT = [False]


class Prog:

    def __init__(self, nc):
        self.nc = nc
        self.ops = {e: [] for e in ENGS}
        self.nop = {e: 0 for e in ENGS}
        self.last_w = {}
        self.readers = {}
        self.waited = {}
        self.audited = set()

    def op(self, eng, fn, reads=(), writes=(), dma=False, tail=False,
           sreads=(), eager=False):
        idx = self.nop[eng]
        waits = set()

        def dep(peng, pidx, ptail=False, force=False):
            if peng == eng and not (ptail or force):
                return
            if self.waited.get((eng, peng), -1) >= pidx:
                return
            if peng == eng and pidx == idx:
                return
            waits.add((peng, pidx))

        para = PARANOID[0]
        if AUDIT[0]:
            for k in list(reads) + list(sreads):
                if k not in self.last_w and k not in self.audited:
                    self.audited.add(k)
                    print(f"AUDIT read-before-write: {k} at {eng}#{idx}")
        for k in sreads:
            if k in self.last_w:
                dep(*self.last_w[k], force=True)
        for k in reads:
            if k in self.last_w:
                dep(*self.last_w[k], force=eager or para)
        for k in writes:
            if k in self.last_w:
                dep(*self.last_w[k][:2], force=para)
            for (re_, ri) in self.readers.get(k, ()):
                dep(re_, ri, force=para)
        for (peng, pidx) in waits:
            if pidx > self.waited.get((eng, peng), -1):
                self.waited[(eng, peng)] = pidx
        for k in list(reads) + list(sreads):
            self.readers.setdefault(k, []).append((eng, idx))
        for k in writes:
            self.last_w[k] = (eng, idx, tail)
            self.readers[k] = []
        self.ops[eng].append([fn, waits, dma])
        self.nop[eng] += 1

    def emit(self):
        nc = self.nc
        needed = {e: set() for e in ENGS}
        for e in ENGS:
            for rec in self.ops[e]:
                for (pe, pi) in rec[1]:
                    needed[pe].add(pi)
        sems = {e: nc.alloc_semaphore("sem_" + e) for e in ENGS}
        NPOOL = 12
        pools = {e: [nc.alloc_semaphore(f"dp_{e}{i}") for i in range(NPOOL)]
                 for e in ("sy", "gp")}
        pool_val = {e: [0] * NPOOL for e in ("sy", "gp")}
        pool_i = {e: 0 for e in ("sy", "gp")}
        mile = {}
        cnt = {e: 0 for e in ENGS}
        prewait = {}
        for e in ENGS:
            for i, rec in enumerate(self.ops[e]):
                fn, waits, dma = rec
                if fn is None:
                    continue
                if dma:
                    j = pool_i[e] % NPOOL
                    pool_i[e] += 1
                    s = pools[e][j]
                    if pool_val[e][j] > 0:
                        prewait[(e, i)] = (s, pool_val[e][j])
                    pool_val[e][j] += 16
                    mile[(e, i)] = (s, pool_val[e][j])
                elif i in needed[e]:
                    cnt[e] += 1
                    mile[(e, i)] = (sems[e], cnt[e])
                else:
                    mile[(e, i)] = None

        with nc.Block() as blk:
            def body_for(e):
                def body(eng):
                    for i, rec in enumerate(self.ops[e]):
                        fn, waits, dma = rec
                        for (pe, pi) in sorted(waits):
                            ms = mile[(pe, pi)]
                            eng.wait_ge(ms[0], ms[1])
                        if fn is None:
                            continue
                        if (e, i) in prewait:
                            pw = prewait[(e, i)]
                            eng.wait_ge(pw[0], pw[1])
                        inst = fn(eng)
                        ms = mile[(e, i)]
                        if ms is not None:
                            inst.then_inc(ms[0], 16 if dma else 1)
                return body
            blk.sync(body_for("sy"))
            blk.vector(body_for("ve"))
            blk.scalar(body_for("sc"))
            blk.tensor(body_for("te"))
            blk.gpsimd(body_for("gp"))
        return sems

    def finish(self):
        waits = set()
        for pe in ENGS:
            if self.nop[pe] == 0:
                continue
            if pe != "sy":
                waits.add((pe, self.nop[pe] - 1))
        for e in ("sy", "gp"):
            for i in range(len(self.ops[e]) - 1, -1, -1):
                if self.ops[e][i][2]:
                    waits.add((e, i))
                    break
        self.ops["sy"].append([None, waits, False])


def flat(t):
    ap = t[:]
    if len(ap.shape) == 3:
        return ap.rearrange("p a b -> p (a b)")
    return ap


class G:
    pass


def alloc_generic(nc, g):
    g.PS = [nc.alloc_psum_tensor(f"ps{i}", [128, 512], F32) for i in range(7)]
    g.PS_KT = nc.alloc_psum_tensor("pskt", [128, 512], BF16)
    g.ones_b = nc.alloc_sbuf_tensor("ones_b", [128, 1], BF16)
    g.onesf = nc.alloc_sbuf_tensor("onesf", [1, 128], F32)
    g.zeros = nc.alloc_sbuf_tensor("zeros", [128, 512], BF16)
    g.row_s = nc.alloc_sbuf_tensor("row_s", [1, 512], F32)
    g.row_ms = nc.alloc_sbuf_tensor("row_ms", [1, 512], F32)
    g.rowb1 = nc.alloc_sbuf_tensor("rowb1", [1, 512], BF16)
    g.rowwork = nc.alloc_sbuf_tensor("rowwork", [1, 1536], F32)
    g.scal = nc.alloc_sbuf_tensor("scal", [1, 16], F32)
    g.cols = nc.alloc_sbuf_tensor("colsb", [128, 96], F32)
    g.sbc = nc.alloc_sbuf_tensor("sbc", [128, 512], F32)
    g.msbc = nc.alloc_sbuf_tensor("msbc", [128, 512], F32)
    g.u1 = nc.alloc_sbuf_tensor("u1", [128, 512], F32)
    g.u2 = nc.alloc_sbuf_tensor("u2", [128, 512], F32)


def init_consts(p, g):
    p.op("ve", lambda e: e.memset(g.ones_b[:], 1.0), writes=["ones_b"])
    p.op("ve", lambda e: e.memset(g.onesf[:], 1.0), writes=["onesf"])
    p.op("ve", lambda e: e.memset(g.zeros[:], 0.0), writes=["zeros"])


def ln_stats_mm(p, g, xb_key, xb_ap, sq_key, sq_ap):
    for k in range(NCH):
        p.op("te", lambda e, k=k: e.matmul(
            g.PS[4][0:1, :], g.ones_b[:], xb_ap(k),
            start=(k == 0), stop=(k == NCH - 1)),
            reads=[xb_key, "ones_b"], writes=["ps4"])
    for k in range(NCH):
        p.op("te", lambda e, k=k: e.matmul(
            g.PS[5][0:1, :], g.ones_b[:], sq_ap(k),
            start=(k == 0), stop=(k == NCH - 1)),
            reads=[sq_key, "ones_b"], writes=["ps5"])


def ln_rows(p, g, composite, rows_dram, tt):
    a = (1.0 + EPS) if composite else 1.0
    e2 = (EPS * EPS) if composite else EPS
    rw = g.rowwork
    p.op("sc", lambda e: e.activation(rw[0:1, 0:512], g.PS[4][0:1, :],
                                      AF.Square, scale=1.0 / D),
         reads=["ps4"], writes=["rw0"])
    p.op("ve", lambda e: e.tensor_scalar(rw[0:1, 512:1024], g.PS[5][0:1, :],
                                         a / D, e2, ALU.mult, ALU.add),
         reads=["ps5"], writes=["rw1"])
    p.op("ve", lambda e: e.scalar_tensor_tensor(
        rw[0:1, 1024:1536], rw[0:1, 0:512], -a, rw[0:1, 512:1024],
        ALU.mult, ALU.add), reads=["rw0", "rw1"], writes=["rw2"])
    p.op("sc", lambda e: e.activation(rw[0:1, 0:512], rw[0:1, 1024:1536],
                                      AF.Ln),
         reads=["rw2"], writes=["rw0"])
    p.op("sc", lambda e: e.activation(g.row_s[:], rw[0:1, 0:512], AF.Exp,
                                      scale=-0.5),
         reads=["rw0"], writes=["row_s"])
    p.op("ve", lambda e: e.scalar_tensor_tensor(
        g.row_ms[:], g.row_s[:], 1.0 / D, g.PS[4][0:1, :],
        ALU.mult, ALU.mult),
        reads=["row_s", "ps4"], writes=["row_ms"])
    if rows_dram is not None:
        sl = slice(tt * TT, (tt + 1) * TT)
        p.op("sy", lambda e, sl=sl: e.dma_start(rows_dram[0:1, sl],
                                                g.row_s[:]),
             reads=["row_s"], writes=["rows_dram_st"], dma=True)
        p.op("sy", lambda e, sl=sl: e.dma_start(rows_dram[1:2, sl],
                                                g.row_ms[:]),
             reads=["row_ms"], writes=["rows_dram_st"], dma=True)


def bcast_rows(p, g):
    p.op("te", lambda e: e.matmul(g.PS[5][:, 0:512], g.onesf[:], g.row_s[:],
                                  start=True, stop=True),
         reads=["row_s", "onesf"], writes=["ps5"])
    p.op("te", lambda e: e.matmul(g.PS[6][:, 0:512], g.onesf[:], g.row_ms[:],
                                  start=True, stop=True),
         reads=["row_ms", "onesf"], writes=["ps6"])
    p.op("sc", lambda e: e.activation(g.sbc[:], g.PS[5][:, 0:512], AF.Copy),
         reads=["ps5"], writes=["sbc"])
    p.op("sc", lambda e: e.activation(g.msbc[:], g.PS[6][:, 0:512], AF.Copy),
         reads=["ps6"], writes=["msbc"])


def reduce_scalar_max(p, g, mx_ap, mx_key, id128f, out_dram):
    p.op("ve", lambda e: e.memset(g.cols[:], 0.0), writes=["colsb"])
    p.op("ve", lambda e: e.reduce_max(g.cols[:, 0:1], mx_ap, axis=AX.X),
         reads=[mx_key], writes=["colsb"], tail=True)
    p.op("te", lambda e: e.transpose(g.PS[0][0:96, 0:128], g.cols[:, 0:96],
                                     id128f[:]),
         reads=["colsb", "id128f"], writes=["ps0"])
    p.op("ve", lambda e: e.reduce_max(g.scal[0:1, 0:1], g.PS[0][0:1, 0:128],
                                      axis=AX.X),
         reads=["ps0"], writes=["gmax"], tail=True)
    p.op("sy", lambda e: e.dma_start(out_dram, g.scal[0:1, 0:1]),
         reads=["gmax"], writes=["lmax_st"], dma=True)


def stats_tile(p, g, xt, xb, sqb, kx, kb, composite, rows_dram, tt,
               ksq="sqr"):
    p.op("sc", lambda e: e.activation(flat(sqb), flat(xt), AF.Square),
         reads=[kx], writes=[ksq])
    ln_stats_mm(p, g, kb, lambda k: xb[:, k, :], ksq,
                lambda k: sqb[:, k, :])
    ln_rows(p, g, composite, rows_dram, tt)


def xn_max_tile(p, g, xt, kx, gcol, bcol, xnb, mxc, tt):
    bcast_rows(p, g)
    for k in range(NCH):
        p.op("ve", lambda e, k=k: e.tensor_tensor(
            g.u1[:], xt[:, k, :], g.sbc[:], ALU.mult),
            reads=[kx, "sbc"], writes=["u1"])
        p.op("ve", lambda e: e.tensor_tensor(g.u2[:], g.u1[:], g.msbc[:],
                                             ALU.subtract),
             reads=["u1", "msbc"], writes=["u2"])
        p.op("sc", lambda e, k=k: e.activation(
            xnb[:], g.u2[:], AF.Identity, scale=gcol[:, k:k + 1],
            bias=bcol[:, k:k + 1]),
            reads=["u2", "gbcols"], writes=["xnb"])
        p.op("ve", lambda e, tt=tt, k=k: e.reduce_max(
            mxc[:, tt * NCH + k:tt * NCH + k + 1], xnb[:], axis=AX.X,
            apply_absolute_value=True),
            reads=["xnb"], writes=["mxc"], tail=True)


def emit_stats_pass(p, g, xsrc, rows_dram, gcol, bcol, xt, xb, sqb, xnb, mxc,
                    id128f, lmax_o, composite=True, src_key="x0d",
                    kx="fxt", kb="gxb", ksq="sqr"):
    p.op("ve", lambda e: e.memset(mxc[:], 0.0), writes=["mxc"])
    for tt in range(NTILE):
        b, tau = divmod(tt, NTPS)
        sl = slice(tau * TT, (tau + 1) * TT)
        xv = xsrc[b].rearrange("(c p) t -> p c t", p=128)[:, :, sl]
        p.op("sy", lambda e, xv=xv: e.dma_start(xt[:], xv),
             reads=[src_key], writes=[kx], dma=True)
        p.op("gp", lambda e, xv=xv: e.dma_start(xb[:], xv),
             reads=[src_key], writes=[kb], dma=True)
        stats_tile(p, g, xt, xb, sqb, kx, kb, composite, rows_dram, tt,
                   ksq=ksq)
        xn_max_tile(p, g, xt, kx, gcol, bcol, xnb, mxc, tt)
    reduce_scalar_max(p, g, mxc[:], "mxc", id128f, lmax_o)


def emit_quant_prologue(p, g, lmax_all, alpha_in, lm_sb, al_sb, invsc_col,
                        epi_col):
    p.op("sy", lambda e: e.dma_start(lm_sb[:], lmax_all),
         writes=["lm_sb"], dma=True)
    p.op("sy", lambda e: e.dma_start(al_sb[:], alpha_in),
         writes=["al_sb"], dma=True)
    p.op("ve", lambda e: e.reduce_max(g.scal[0:1, 8:9], lm_sb[:], axis=AX.X),
         reads=["lm_sb"], writes=["q0"], tail=True)
    p.op("ve", lambda e: e.tensor_scalar(g.scal[0:1, 9:10], g.scal[0:1, 8:9],
                                         1e-8, None, ALU.max),
         reads=["q0"], writes=["q1"])
    p.op("ve", lambda e: e.reciprocal(g.scal[0:1, 10:11], g.scal[0:1, 9:10]),
         reads=["q1"], writes=["q2"], tail=True, eager=True)
    p.op("ve", lambda e: e.tensor_scalar(g.scal[0:1, 11:12],
                                         g.scal[0:1, 10:11], 127.0, None,
                                         ALU.mult),
         reads=["q2"], writes=["q3"])
    p.op("ve", lambda e: e.scalar_tensor_tensor(
        g.scal[0:1, 12:13], g.scal[0:1, 9:10], 0.5 / 127.0, al_sb[0:1, 0:1],
        ALU.mult, ALU.mult), reads=["q1", "al_sb"], writes=["q4"])
    p.op("te", lambda e: e.matmul(g.PS[5][:, 0:2], g.onesf[:],
                                  g.scal[0:1, 11:13], start=True, stop=True),
         reads=["q3", "q4", "onesf"], writes=["ps5"])
    p.op("ve", lambda e: e.tensor_copy(invsc_col[:], g.PS[5][:, 0:1]),
         reads=["ps5"], writes=["invsc"])
    p.op("ve", lambda e: e.tensor_copy(epi_col[:], g.PS[5][:, 1:2]),
         reads=["ps5"], writes=["epi"])


def emit_ffnc(p, g, xsrc, xdst, rows_in, tern_sb, gcol, bcol, invsc_col,
              epi_col, xt, qb, with_stats=False, rows2_dram=None,
              xb=None, sqb=None, src_key="x0d", dst_key="x1d",
              kx="fxt", kq="xiq", kb="gxb", ksq="sqr"):
    for tt in range(NTILE):
        b, tau = divmod(tt, NTPS)
        sl = slice(tau * TT, (tau + 1) * TT)
        xv = xsrc[b].rearrange("(c p) t -> p c t", p=128)[:, :, sl]
        p.op("sy", lambda e, xv=xv: e.dma_start(xt[:], xv),
             reads=[src_key], writes=[kx], dma=True)
        p.op("sy", lambda e, tt=tt: e.dma_start(
            g.row_s[:], rows_in[0:1, tt * TT:(tt + 1) * TT]),
            writes=["row_s"], dma=True)
        p.op("sy", lambda e, tt=tt: e.dma_start(
            g.row_ms[:], rows_in[1:2, tt * TT:(tt + 1) * TT]),
            writes=["row_ms"], dma=True)
        bcast_rows(p, g)
        for k in range(NCH):
            p.op("ve", lambda e, k=k: e.tensor_tensor(
                g.u1[:], xt[:, k, :], g.sbc[:], ALU.mult),
                reads=[kx, "sbc"], writes=["u1"])
            p.op("ve", lambda e: e.tensor_tensor(g.u2[:], g.u1[:], g.msbc[:],
                                                 ALU.subtract),
                 reads=["u1", "msbc"], writes=["u2"])
            p.op("sc", lambda e, k=k: e.activation(
                g.u1[:], g.u2[:], AF.Identity, scale=gcol[:, k:k + 1],
                bias=bcol[:, k:k + 1]),
                reads=["u2", "gbcols"], writes=["u1"])
            p.op("ve", lambda e: e.tensor_scalar(
                g.u2[:], g.u1[:], invsc_col[:, 0:1], MAGIC, ALU.mult,
                ALU.add),
                reads=["u1"], sreads=["invsc"], writes=["u2"])
            p.op("ve", lambda e, k=k: e.tensor_scalar(
                qb[:, k, :], g.u2[:], MAGIC, None, ALU.subtract),
                reads=["u2"], writes=[f"{kq}{k}"])
        for m in range(NCH):
            pb = g.PS[m % 2]
            pk = f"ps{m % 2}"
            for k in range(NCH):
                p.op("te", lambda e, m=m, k=k, pb=pb: e.matmul(
                    pb[:, 0:512], tern_sb[:, k, m * 128:(m + 1) * 128],
                    qb[:, k, :], start=(k == 0), stop=(k == NCH - 1)),
                    reads=[f"{kq}{k}", "tern_res"], writes=[pk])
            p.op("sc", lambda e, pb=pb: e.activation(
                g.u1[:], pb[:, 0:512], AF.Identity, scale=epi_col[:, 0:1]),
                reads=[pk, "epi"], writes=["u1"])
            p.op("ve", lambda e, m=m: e.tensor_tensor(
                xt[:, m, :], g.u1[:], xt[:, m, :], ALU.add),
                reads=["u1", kx], writes=[kx])
        p.op("sy", lambda e, b=b, sl=sl: e.dma_start(
            xdst[b].rearrange("(c p) t -> p c t", p=128)[:, :, sl], xt[:]),
            reads=[kx], writes=[dst_key], dma=True)
        if with_stats:
            p.op("ve", lambda e: e.tensor_copy(flat(xb), flat(xt)),
                 reads=[kx], writes=[kb])
            stats_tile(p, g, xt, xb, sqb, kx, kb, False, rows2_dram, tt,
                       ksq=ksq)


def build_L1():
    nc = bass.Bass("TRN2", target_bir_lowering=False, debug=False)
    g = G()
    x0 = nc.dram_tensor("x0", [B_LOC, D, T], F32, kind="ExternalInput").ap()
    wffnT = nc.dram_tensor("wffnT", [D, D], F32, kind="ExternalInput").ap()
    gpc = nc.dram_tensor("gammapc", [128, NCH], F32, kind="ExternalInput").ap()
    bpc = nc.dram_tensor("betapc", [128, NCH], F32, kind="ExternalInput").ap()
    identf = nc.dram_tensor("identf", [128, 128], F32, kind="ExternalInput").ap()

    tern_o = nc.dram_tensor("tern", [D, D], BF16, kind="ExternalOutput").ap()
    alpha_o = nc.dram_tensor("alpha", [1, 1], F32, kind="ExternalOutput").ap()
    rows_o = nc.dram_tensor("rows1", [2, NTILE * TT], F32,
                            kind="ExternalOutput").ap()
    lmax_o = nc.dram_tensor("lmax", [1, 1], F32, kind="ExternalOutput").ap()

    alloc_generic(nc, g)
    wf = nc.alloc_sbuf_tensor("wf", [128, NCH, D], F32)
    absw = nc.alloc_sbuf_tensor("absw", [128, NCH * D], F32)
    m01 = nc.alloc_sbuf_tensor("m01", [128, NCH * D], BF16)
    sgn = nc.alloc_sbuf_tensor("sgn", [128, NCH * D], BF16)
    tern = nc.alloc_sbuf_tensor("tern_sb", [128, NCH, D], BF16)
    id128f = nc.alloc_sbuf_tensor("id128f", [128, 128], F32)
    onecol = nc.alloc_sbuf_tensor("onecol", [128, 1], F32)
    gcol = nc.alloc_sbuf_tensor("gcol", [128, NCH], F32)
    bcol = nc.alloc_sbuf_tensor("bcol", [128, NCH], F32)
    xt = nc.alloc_sbuf_tensor("sxt", [128, NCH, TT], F32)
    xb = nc.alloc_sbuf_tensor("sxb", [128, NCH, TT], BF16)
    sqb = nc.alloc_sbuf_tensor("sqb", [128, NCH, TT], BF16)
    xnb = nc.alloc_sbuf_tensor("xnb", [128, TT], F32)
    mxc = nc.alloc_sbuf_tensor("mxc", [128, NCH * NTILE], F32)

    p = Prog(nc)
    init_consts(p, g)
    p.op("sy", lambda e: e.dma_start(
        wf[:], wffnT.rearrange("(c p) o -> p c o", p=128)),
        writes=["wf"], dma=True)
    p.op("sy", lambda e: e.dma_start(id128f[:], identf),
         writes=["id128f"], dma=True)
    p.op("gp", lambda e: e.dma_start(gcol[:], gpc), writes=["gbcols"], dma=True)
    p.op("gp", lambda e: e.dma_start(bcol[:], bpc), writes=["gbcols"], dma=True)

    wfv = flat(wf)
    p.op("sc", lambda e: e.activation(absw[:], wfv, AF.Abs),
         reads=["wf"], writes=["absw"])
    p.op("sc", lambda e: e.activation(sgn[:], wfv, AF.Sign),
         reads=["wf"], writes=["sgn"])
    p.op("ve", lambda e: e.memset(g.cols[:], 0.0), writes=["colsb"])
    p.op("ve", lambda e: e.reduce_sum(g.cols[:, 0:1], absw[:], axis=AX.X),
         reads=["absw"], writes=["colsb"], tail=True)
    p.op("te", lambda e: e.transpose(g.PS[0][0:96, 0:128], g.cols[:, 0:96],
                                     id128f[:]),
         reads=["colsb", "id128f"], writes=["ps0"])
    p.op("ve", lambda e: e.reduce_sum(g.scal[0:1, 0:1], g.PS[0][0:1, 0:128],
                                      axis=AX.X), reads=["ps0"], writes=["s0"], tail=True)
    p.op("ve", lambda e: e.tensor_scalar(g.scal[0:1, 1:2], g.scal[0:1, 0:1],
                                         0.7 / (D * D), None, ALU.mult),
         reads=["s0"], writes=["s1"])
    p.op("te", lambda e: e.matmul(g.PS[1][:, 0:1], g.onesf[:],
                                  g.scal[0:1, 1:2], start=True, stop=True),
         reads=["s1", "onesf"], writes=["ps1"])
    p.op("ve", lambda e: e.tensor_copy(onecol[:], g.PS[1][:, 0:1]),
         reads=["ps1"], writes=["onecol"])
    p.op("ve", lambda e: e.tensor_scalar(m01[:], absw[:], onecol[:, 0:1],
                                         None, ALU.is_gt),
         reads=["absw"], sreads=["onecol"], writes=["m01"])
    p.op("ve", lambda e: e.tensor_tensor(absw[:], absw[:], m01[:], ALU.mult),
         reads=["absw", "m01"], writes=["absw"])
    p.op("ve", lambda e: e.reduce_sum(g.cols[:, 32:33], absw[:], axis=AX.X),
         reads=["absw"], writes=["colsb"], tail=True)
    p.op("ve", lambda e: e.reduce_sum(g.cols[:, 64:65], m01[:], axis=AX.X),
         reads=["m01"], writes=["colsb"], tail=True)
    p.op("te", lambda e: e.transpose(g.PS[0][0:96, 0:128], g.cols[:, 0:96],
                                     id128f[:]),
         reads=["colsb", "id128f"], writes=["ps0"])
    p.op("ve", lambda e: e.reduce_sum(g.scal[0:1, 2:3], g.PS[0][32:33, 0:128],
                                      axis=AX.X), reads=["ps0"], writes=["s2"], tail=True)
    p.op("ve", lambda e: e.reduce_sum(g.scal[0:1, 3:4], g.PS[0][64:65, 0:128],
                                      axis=AX.X), reads=["ps0"], writes=["s3"], tail=True)
    p.op("ve", lambda e: e.tensor_scalar(g.scal[0:1, 4:5], g.scal[0:1, 3:4],
                                         1.0, None, ALU.max),
         reads=["s3"], writes=["s4"])
    p.op("ve", lambda e: e.reciprocal(g.scal[0:1, 5:6], g.scal[0:1, 4:5]),
         reads=["s4"], writes=["s5"], tail=True, eager=True)
    p.op("ve", lambda e: e.tensor_tensor(g.scal[0:1, 6:7], g.scal[0:1, 2:3],
                                         g.scal[0:1, 5:6], ALU.mult),
         reads=["s2", "s5"], writes=["s6"])
    p.op("ve", lambda e: e.tensor_tensor(flat(tern), sgn[:], m01[:],
                                         ALU.mult),
         reads=["sgn", "m01"], writes=["tern"])
    p.op("sy", lambda e: e.dma_start(
        tern_o.rearrange("(c p) o -> p c o", p=128), tern[:]),
        reads=["tern"], writes=["tern_st"], dma=True)
    p.op("sy", lambda e: e.dma_start(alpha_o, g.scal[0:1, 6:7]),
         reads=["s6"], writes=["alpha_st"], dma=True)

    emit_stats_pass(p, g, x0, rows_o, gcol, bcol, xt, xb, sqb, xnb, mxc,
                    id128f, lmax_o, composite=True,
                    kx="sxt", kb="sxb", ksq="sqb")
    p.finish()
    p.emit()
    return nc


def build_L3():
    nc = bass.Bass("TRN2", target_bir_lowering=False, debug=False)
    g = G()
    x2 = nc.dram_tensor("x2", [B_LOC, D, T], F32, kind="ExternalInput").ap()
    tern_i = nc.dram_tensor("tern", [D, D], BF16, kind="ExternalInput").ap()
    alpha_i = nc.dram_tensor("alpha", [1, 1], F32, kind="ExternalInput").ap()
    rows_i = nc.dram_tensor("rows3", [2, NTILE * TT], F32,
                            kind="ExternalInput").ap()
    lmax_all = nc.dram_tensor("lmax_all", [1, 8], F32,
                              kind="ExternalInput").ap()
    gpc = nc.dram_tensor("gammapc", [128, NCH], F32, kind="ExternalInput").ap()
    bpc = nc.dram_tensor("betapc", [128, NCH], F32, kind="ExternalInput").ap()
    out = nc.dram_tensor("out", [B_LOC, D, T], F32, kind="ExternalOutput").ap()

    alloc_generic(nc, g)
    tern_sb = nc.alloc_sbuf_tensor("tern_res", [128, NCH, D], BF16)
    gcol = nc.alloc_sbuf_tensor("gcol", [128, NCH], F32)
    bcol = nc.alloc_sbuf_tensor("bcol", [128, NCH], F32)
    lm_sb = nc.alloc_sbuf_tensor("lm_sb", [1, 8], F32)
    al_sb = nc.alloc_sbuf_tensor("al_sb", [1, 1], F32)
    invsc_col = nc.alloc_sbuf_tensor("invsc_col", [128, 1], F32)
    epi_col = nc.alloc_sbuf_tensor("epi_col", [128, 1], F32)
    xt = nc.alloc_sbuf_tensor("fxt", [128, NCH, TT], F32)
    qb = nc.alloc_sbuf_tensor("qb", [128, NCH, TT], BF16)

    p = Prog(nc)
    init_consts(p, g)
    p.op("gp", lambda e: e.dma_start(
        tern_sb[:], tern_i.rearrange("(c p) o -> p c o", p=128)),
        writes=["tern_res"], dma=True)
    p.op("gp", lambda e: e.dma_start(gcol[:], gpc), writes=["gbcols"], dma=True)
    p.op("gp", lambda e: e.dma_start(bcol[:], bpc), writes=["gbcols"], dma=True)
    emit_quant_prologue(p, g, lmax_all, alpha_i, lm_sb, al_sb, invsc_col,
                        epi_col)
    emit_ffnc(p, g, x2, out, rows_i, tern_sb, gcol, bcol, invsc_col, epi_col,
              xt, qb, src_key="x2d", dst_key="outd")
    p.finish()
    p.emit()
    return nc


def build_L2():
    nc = bass.Bass("TRN2", target_bir_lowering=False, debug=False)
    g = G()

    def AP(n, s, d=F32, kk="ExternalInput"):
        return nc.dram_tensor(n, s, d, kind=kk).ap()

    x0 = AP("x0", [B_LOC, D, T])
    tern_i = AP("tern", [D, D], BF16)
    alpha_i = AP("alpha", [1, 1])
    rows1 = AP("rows1", [2, NTILE * TT])
    lmax_all = AP("lmax_all", [1, 8])
    gpc, bpc = AP("gammapc", [128, NCH]), AP("betapc", [128, NCH])
    identf = AP("identf", [128, 128])
    ident64d = AP("ident64d", [128, 64])
    masku = AP("masku", [128, 64], U8)
    wx0T = AP("wx0T", [D, 160])
    wx2T = AP("wx2T", [160, D])
    wrT = AP("wrT", [D, 512])
    ww1T = AP("ww1T", [D, 64])
    ww2T = AP("ww2T", [64, 512])
    wkT = AP("wkT", [D, 512])
    wvT = AP("wvT", [D, D])
    wgT = AP("wgT", [D, D])
    woT = AP("woT", [D, D])
    mu0pc = AP("mu0pc", [128, NCH])
    xbiaspc = AP("xbiaspc", [128, 5, NCH])
    bw2pc = AP("bw2pc", [128, 4])
    bonusT = AP("bonusT", [128, 4])
    gnwrow = AP("gnwrow", [2, D])
    gncolw = AP("gncolw", [128, NCH])
    x2_o = AP("x2", [B_LOC, D, T], F32, "ExternalOutput")
    rows3_o = AP("rows3", [2, NTILE * TT], F32, "ExternalOutput")
    lmax2_o = AP("lmax2", [1, 1], F32, "ExternalOutput")
    x1d = nc.dram_tensor("x1scr", [B_LOC, D, T], F32, kind="Internal").ap()
    rows2d = nc.dram_tensor("rows2scr", [2, NTILE * TT], F32,
                            kind="Internal").ap()

    alloc_generic(nc, g)
    tern_sb = nc.alloc_sbuf_tensor("tern_res", [128, NCH, D], BF16)
    wx0 = nc.alloc_sbuf_tensor("wx0", [128, NCH, 160], BF16)
    wx2 = nc.alloc_sbuf_tensor("wx2", [128, 2, D], BF16)
    wr = nc.alloc_sbuf_tensor("wr", [128, NCH, 512], BF16)
    ww1 = nc.alloc_sbuf_tensor("ww1", [128, NCH, 64], BF16)
    ww2 = nc.alloc_sbuf_tensor("ww2", [64, 512], BF16)
    wk = nc.alloc_sbuf_tensor("wk", [128, NCH, 512], BF16)
    wv = nc.alloc_sbuf_tensor("wv", [128, NCH, D], BF16)
    wg = nc.alloc_sbuf_tensor("wg", [128, NCH, D], BF16)
    wo = nc.alloc_sbuf_tensor("wo", [128, NCH, D], BF16)
    id128f = nc.alloc_sbuf_tensor("id128f", [128, 128], F32)
    id128b = nc.alloc_sbuf_tensor("id128b", [128, 128], BF16)
    id64b = nc.alloc_sbuf_tensor("id64b", [128, 64], BF16)
    msku = nc.alloc_sbuf_tensor("msku", [128, 64], U8)
    gcol = nc.alloc_sbuf_tensor("gcol", [128, NCH], F32)
    bcol = nc.alloc_sbuf_tensor("bcol", [128, NCH], F32)
    mu0c = nc.alloc_sbuf_tensor("mu0c", [128, NCH], F32)
    xbc = nc.alloc_sbuf_tensor("xbc", [128, 5, NCH], F32)
    bw2c = nc.alloc_sbuf_tensor("bw2c", [128, 4], F32)
    ubon = nc.alloc_sbuf_tensor("ubon", [128, 4], BF16)
    gnwr = nc.alloc_sbuf_tensor("gnwr", [2, D], BF16)
    gncol = nc.alloc_sbuf_tensor("gncol", [128, NCH], F32)
    lm_sb = nc.alloc_sbuf_tensor("lm_sb", [1, 8], F32)
    al_sb = nc.alloc_sbuf_tensor("al_sb", [1, 1], F32)
    invsc_col = nc.alloc_sbuf_tensor("invsc_col", [128, 1], F32)
    epi_col = nc.alloc_sbuf_tensor("epi_col", [128, 1], F32)
    xt = nc.alloc_sbuf_tensor("fxt", [128, NCH, TT], F32)
    xiq = nc.alloc_sbuf_tensor("xiq", [128, NCH, TT], BF16)
    gxb = nc.alloc_sbuf_tensor("gxb", [128, NCH, TT], BF16)
    sqr = nc.alloc_sbuf_tensor("sqr", [128, NCH, TT], BF16)
    dob = nc.alloc_sbuf_tensor("dob", [128, NCH, TT], BF16)
    xnb = nc.alloc_sbuf_tensor("xnb", [128, TT], F32)
    mxc = nc.alloc_sbuf_tensor("mxc", [128, NCH * NTILE], F32)
    xln = nc.alloc_sbuf_tensor("xln", [128, NCH, TT + 1], BF16)
    xp = nc.alloc_sbuf_tensor("xp", [128, 2, TT], BF16)
    rb = nc.alloc_sbuf_tensor("rb", [128, NH, TT], BF16)
    kbuf = nc.alloc_sbuf_tensor("kbuf", [128, NH, TT], BF16)
    w1b = nc.alloc_sbuf_tensor("w1b", [64, TT], BF16)
    posf = nc.alloc_sbuf_tensor("posf", [128, NH, TT], F32)
    vtm = nc.alloc_sbuf_tensor("vtm", [128, NTPS, D], BF16)
    tmpb = nc.alloc_sbuf_tensor("tmpb", [128, TT], BF16)
    enb2 = [nc.alloc_sbuf_tensor(f"enb{i}", [128, TT + 1], F32)
            for i in range(2)]
    encar = nc.alloc_sbuf_tensor("encar", [128, NH], F32)
    scar = nc.alloc_sbuf_tensor("scar", [128, NH, DV], BF16)
    spp = [nc.alloc_sbuf_tensor(f"spp{i}", [128, DV], BF16) for i in range(2)]
    dcolb = nc.alloc_sbuf_tensor("dcolb", [128, 4], F32)
    fcall2 = [nc.alloc_sbuf_tensor(f"fcall{i}", [128, 24], F32)
              for i in range(2)]
    grow = nc.alloc_sbuf_tensor("grow", [2, TT], BF16)
    rab2 = xp[:, 0, :]
    ksb2 = xp[:, 1, :]
    ktbq = tmpb
    u2bf = g.u2.bitcast(BF16)
    apb = u2bf[:, 0:256]
    dtb = u2bf[:, 256:512]

    p = Prog(nc)
    init_consts(p, g)
    p.op("ve", lambda e: e.memset(grow[:], -1.0), writes=["grow_c"])

    def gload(dst, src, key):
        p.op("gp", lambda e, dst=dst, src=src: e.dma_start(dst, src),
             writes=[key], dma=True)

    gload(tern_sb[:], tern_i.rearrange("(c p) o -> p c o", p=128), "tern_res")
    gload(wx0[:], wx0T.rearrange("(c p) o -> p c o", p=128), "wx0")
    gload(wx2[0:96, 0, :], wx2T[0:96, :], "wx2")
    gload(wx2[0:64, 1, :], wx2T[96:160, :], "wx2")
    gload(wr[:], wrT.rearrange("(c p) o -> p c o", p=128), "wr")
    gload(ww1[:], ww1T.rearrange("(c p) o -> p c o", p=128), "ww1")
    gload(ww2[:], ww2T, "ww2")
    gload(wk[:], wkT.rearrange("(c p) o -> p c o", p=128), "wk")
    gload(wv[:], wvT.rearrange("(c p) o -> p c o", p=128), "wv")
    gload(wg[:], wgT.rearrange("(c p) o -> p c o", p=128), "wg")
    gload(wo[:], woT.rearrange("(c p) o -> p c o", p=128), "wo")
    gload(id128b[:], identf, "id128b")
    gload(id64b[:], ident64d, "id64b")
    gload(ubon[:], bonusT, "ubon")
    gload(gnwr[:], gnwrow, "gnwr")
    p.op("sy", lambda e: e.dma_start(gncol[:], gncolw), writes=["gncol"],
         dma=True)
    p.op("sy", lambda e: e.dma_start(id128f[:], identf), writes=["id128f"],
         dma=True)
    p.op("sy", lambda e: e.dma_start(msku[:], masku), writes=["msku"],
         dma=True)
    p.op("sy", lambda e: e.dma_start(gcol[:], gpc), writes=["gbcols"],
         dma=True)
    p.op("sy", lambda e: e.dma_start(bcol[:], bpc), writes=["gbcols"],
         dma=True)
    p.op("sy", lambda e: e.dma_start(mu0c[:], mu0pc), writes=["mu0c"],
         dma=True)
    p.op("sy", lambda e: e.dma_start(xbc[:], xbiaspc), writes=["xbc"],
         dma=True)
    p.op("sy", lambda e: e.dma_start(bw2c[:], bw2pc), writes=["bw2c"],
         dma=True)

    p.op("ve", lambda e: e.memset(mxc[:], 0.0), writes=["mxc"])
    emit_quant_prologue(p, g, lmax_all, alpha_i, lm_sb, al_sb, invsc_col,
                        epi_col)
    emit_ffnc(p, g, x0, x1d, rows1, tern_sb, gcol, bcol, invsc_col, epi_col,
              xt, xiq, with_stats=True, rows2_dram=rows2d,
              xb=gxb, sqb=sqr, src_key="x0d", dst_key="x1d")

    delta = dob
    xi = xiq
    gsb = gxb
    ob = dob
    rkb = sqr
    xt2 = tern_sb.bitcast(F32)

    def xbuf(t):
        return (xt, "fxt", ["fxt"]) if t % 2 == 0 else                (xt2, "fxt1", ["fxt1", "tern_res"])

    def xload(t):
        bb, uu = divmod(t, NTPS)
        ssl = slice(uu * TT, (uu + 1) * TT)
        xv = x1d[bb].rearrange("(c p) t -> p c t", p=128)[:, :, ssl]
        dst, _, wk = xbuf(t)
        p.op("sy", lambda e, xv=xv, dst=dst: e.dma_start(dst[:], xv),
             reads=["x1d"], writes=wk, dma=True)

    xload(0)
    for tt in range(NTILE):
        b, tau = divmod(tt, NTPS)
        sl = slice(tau * TT, (tau + 1) * TT)
        xcur, kx, _ = xbuf(tt)
        if tt + 1 < NTILE:
            xload(tt + 1)
        p.op("sy", lambda e, tt=tt: e.dma_start(
            g.row_s[:], rows2d[0:1, tt * TT:(tt + 1) * TT]),
            reads=["rows_dram_st"], writes=["row_s"], dma=True)
        p.op("sy", lambda e, tt=tt: e.dma_start(
            g.row_ms[:], rows2d[1:2, tt * TT:(tt + 1) * TT]),
            reads=["rows_dram_st"], writes=["row_ms"], dma=True)
        bcast_rows(p, g)
        if tau == 0:
            p.op("ve", lambda e: e.memset(xln[:, :, 0:1], 0.0),
                 reads=["xln"], writes=["xlncar"])
            p.op("ve", lambda e: e.memset(flat(scar), 0.0),
                 reads=[f"scar{i}" for i in range(NH)],
                 writes=[f"scar{i}" for i in range(NH)])
            p.op("ve", lambda e: e.memset(encar[:], 0.0),
                 reads=["encar0", "encar1"], writes=["encar0", "encar1"])
        else:
            p.op("ve", lambda e: e.tensor_copy(xln[:, :, 0:1],
                                               xln[:, :, TT:TT + 1]),
                 reads=["xln"], writes=["xlncar"])
        for k in range(NCH):
            p.op("ve", lambda e, k=k, xcur=xcur: e.tensor_tensor(
                g.u1[:], xcur[:, k, :], g.sbc[:], ALU.mult),
                reads=[kx, "sbc"], writes=["u1"])
            p.op("ve", lambda e, k=k: e.tensor_tensor(
                xln[:, k, 1:TT + 1], g.u1[:], g.msbc[:], ALU.subtract),
                reads=["u1", "msbc"], writes=["xln"])
        p.op("ve", lambda e: e.tensor_tensor(
            delta[:], xln[:, :, 0:TT], xln[:, :, 1:TT + 1], ALU.subtract),
            reads=["xln", "xlncar"], writes=[f"dob{i}" for i in range(NCH)])
        for k in range(NCH):
            p.op("ve", lambda e, k=k: e.scalar_tensor_tensor(
                xi[:, k, :], delta[:, k, :], mu0c[:, k:k + 1],
                xln[:, k, 1:TT + 1], ALU.mult, ALU.add),
                reads=[f"dob{k}", "mu0c", "xln"], writes=[f"xiq{k}"])
        for mg, (mlo, mhi) in enumerate([(0, 96), (96, 160)]):
            pb, pk = g.PS[mg], f"ps{mg}"
            for k in range(NCH):
                p.op("te", lambda e, k=k, mlo=mlo, mhi=mhi, pb=pb: e.matmul(
                    pb[0:mhi - mlo, 0:512], wx0[:, k, mlo:mhi], xi[:, k, :],
                    start=(k == 0), stop=(k == NCH - 1)),
                    reads=[f"xiq{k}", "wx0"], writes=[pk])
            p.op("sc", lambda e, mg=mg, mlo=mlo, mhi=mhi, pb=pb: e.activation(
                xp[0:mhi - mlo, mg, :], pb[0:mhi - mlo, 0:512], AF.Tanh),
                reads=[pk], writes=["xp", "xpr", "xpk"])
        tmpb2 = xnb.bitcast(BF16)[:, 0:512]

        def emit_mu1(n, m):
            xpc, xpo = ((0, n * 32) if n < 3 else (1, (n - 3) * 32))
            pb, pk = g.PS[2 + m % 4], f"ps{2 + m % 4}"
            p.op("te", lambda e, m=m, xpc=xpc, xpo=xpo, pb=pb: e.matmul(
                pb[:, 0:512],
                wx2[xpo:xpo + 32, xpc, m * 128:(m + 1) * 128],
                xp[xpo:xpo + 32, xpc, :], start=True, stop=True),
                reads=["xp", "wx2"], writes=[pk])

        def emit_wave(n):
            for m in range(NCH):
                pb, pk = g.PS[2 + m % 4], f"ps{2 + m % 4}"
                t_b, tk = ((tmpb[:], "tmpb") if m % 2 == 0 else
                           (tmpb2, "xnb"))
                p.op("ve", lambda e, n=n, m=m, pb=pb, t_b=t_b:
                     e.scalar_tensor_tensor(
                         t_b, pb[:, 0:512], xbc[:, n, m:m + 1],
                         delta[:, m, :], ALU.add, ALU.mult),
                     reads=[pk, f"dob{m}"], sreads=["xbc"],
                     writes=[tk])
                p.op("gp" if m % 2 == 0 else "ve",
                     lambda e, m=m, t_b=t_b: e.tensor_tensor(
                         xi[:, m, :], t_b, xln[:, m, 1:TT + 1], ALU.add),
                     reads=[tk, "xln"], writes=[f"xiq{m}"])
                if m < 4:
                    emit_mu1(n, m + 4)
                elif n < 4:
                    emit_mu1(n + 1, m - 4)

        for m in range(4):
            emit_mu1(0, m)
        for n in range(5):
            emit_wave(n)
            if n in (0, 2):
                dst, dk = (rb, "rb") if n == 0 else (kbuf, "kbuf")
                wt, wkey = (wr, "wr") if n == 0 else (wk, "wk")
                for m in range(NH):
                    pb, pk = g.PS[m % 2], f"ps{m % 2}"
                    for k in range(NCH):
                        p.op("te", lambda e, m=m, k=k, pb=pb, wt=wt: e.matmul(
                            pb[:, 0:512], wt[:, k, m * 128:(m + 1) * 128],
                            xi[:, k, :], start=(k == 0), stop=(k == NCH - 1)),
                            reads=[f"xiq{k}", wkey], writes=[pk])
                    p.op("sc", lambda e, m=m, pb=pb, dst=dst, dk=dk:
                         e.activation(dst[:, m, :], pb[:, 0:512], AF.Copy),
                         reads=[pk], writes=[f"{dk}{m}"])
            elif n == 1:
                for k in range(NCH):
                    p.op("te", lambda e, k=k: e.matmul(
                        g.PS[0][0:64, 0:512], ww1[:, k, :], xi[:, k, :],
                        start=(k == 0), stop=(k == NCH - 1)),
                        reads=[f"xiq{k}", "ww1"], writes=["ps0"])
                p.op("sc", lambda e: e.activation(
                    w1b[:], g.PS[0][0:64, 0:512], AF.Tanh),
                    reads=["ps0"], writes=["w1b"])
                for m in range(NH):
                    pb, pk = g.PS[m % 2], f"ps{m % 2}"
                    p.op("te", lambda e, m=m, pb=pb: e.matmul(
                        pb[:, 0:512], ww2[:, m * 128:(m + 1) * 128], w1b[:],
                        start=True, stop=True),
                        reads=["w1b", "ww2"], writes=[pk])
                    p.op("sc", lambda e, m=m, pb=pb: e.activation(
                        posf[:, m, :], pb[:, 0:512], AF.Exp,
                        bias=bw2c[:, m:m + 1]),
                        reads=[pk, "bw2c"], writes=[f"posf{m}"])
            elif n == 3:
                for ts in range(4):
                    for nv in range(2):
                        pb, pk = g.PS[nv], f"ps{nv}"
                        for k in range(NCH):
                            p.op("te", lambda e, ts=ts, nv=nv, k=k, pb=pb:
                                 e.matmul(pb[:, 0:512],
                                          xi[:, k, ts * 128:(ts + 1) * 128],
                                          wv[:, k, nv * 512:(nv + 1) * 512],
                                          start=(k == 0),
                                          stop=(k == NCH - 1)),
                                 reads=[f"xiq{k}", "wv"], writes=[pk])
                        p.op("sc", lambda e, ts=ts, nv=nv, pb=pb: e.activation(
                            vtm[:, ts, nv * 512:(nv + 1) * 512],
                            pb[:, 0:512], AF.Copy),
                            reads=[pk], writes=["vtm"])
            else:
                for m in range(NCH):
                    pb, pk = g.PS[m % 2], f"ps{m % 2}"
                    for k in range(NCH):
                        p.op("te", lambda e, m=m, k=k, pb=pb: e.matmul(
                            pb[:, 0:512], wg[:, k, m * 128:(m + 1) * 128],
                            xi[:, k, :], start=(k == 0), stop=(k == NCH - 1)),
                            reads=[f"xiq{k}", "wg"], writes=[pk])
                    p.op("sc", lambda e, m=m, pb=pb: e.activation(
                        gsb[:, m, :], pb[:, 0:512], AF.Silu),
                        reads=[pk], writes=["gxb"])
        p.op("gp", lambda e: e.tensor_tensor(
            rkb[:, 0:NH, :].rearrange("p a b -> p (a b)"),
            flat(rb), flat(kbuf), ALU.mult),
            reads=[f"rb{i}" for i in range(NH)]
                  + [f"kbuf{i}" for i in range(NH)], writes=["sqr"])

        def scan_factors(h):
            hp = h % 2
            en, ek = enb2[hp], f"enb{hp}"
            eck = f"encar{hp}"
            fcall, fck, frk = fcall2[hp], f"fcall{hp}", f"fcraw{hp}"
            if tau == 0:
                p.op("ve", lambda e, en=en: e.memset(en[:, 0:1], 0.0),
                     reads=[ek], writes=[ek])
            else:
                p.op("ve", lambda e, h=h, en=en: e.tensor_copy(
                    en[:, 0:1], encar[:, h:h + 1]),
                    reads=[eck, ek], writes=[ek])
            p.op("ve", lambda e, h=h, tau=tau, en=en: e.tensor_tensor_scan(
                en[:, 1:TT + 1], posf[:, h, :], g.zeros[:],
                (encar[:, h:h + 1] if tau > 0 else 0.0), ALU.add, ALU.add),
                reads=[f"posf{h}", "zeros", eck, ek], writes=[ek], tail=True)
            p.op("ve", lambda e, h=h, en=en: e.tensor_copy(
                encar[:, h:h + 1], en[:, TT:TT + 1]),
                reads=[ek], writes=[eck])
            p.op("ve", lambda e, en=en: e.tensor_tensor(
                xnb[:], en[:, 0:512],
                en[:, 0:512:64].broadcast_to((128, 8, 64)), ALU.subtract),
                reads=[ek], writes=["xnb"])
            p.op("ve", lambda e, en=en, fcall=fcall: e.tensor_tensor(
                fcall[:, 0:8], en[:, 32:513:64], en[:, 0:512:64],
                ALU.subtract), reads=[ek], writes=[frk])
            p.op("ve", lambda e, en=en, fcall=fcall: e.tensor_tensor(
                fcall[:, 8:16], en[:, 32:513:64], en[:, 64:513:64],
                ALU.subtract), reads=[ek], writes=[frk])
            p.op("ve", lambda e, en=en, fcall=fcall: e.tensor_tensor(
                fcall[:, 16:24], en[:, 0:512:64], en[:, 64:513:64],
                ALU.subtract), reads=[ek], writes=[frk])
            p.op("sc", lambda e, fcall=fcall: e.activation(
                fcall[:, 0:24], fcall[:, 0:24], AF.Exp),
                reads=[frk], writes=[fck])
            p.op("sc", lambda e, h=h: e.activation(
                posf[:, h, :], xnb[:], AF.Exp, scale=-1.0),
                reads=["xnb"], writes=[f"posf{h}"])
            p.op("ve", lambda e, h=h: e.tensor_tensor(
                rb[:, h, :], rb[:, h, :], posf[:, h, :], ALU.mult),
                reads=[f"rb{h}", f"posf{h}"], writes=[f"rb{h}"])
            p.op("ve", lambda e, h=h, fcall=fcall: e.tensor_tensor(
                rab2, rb[:, h, :],
                fcall[:, 0:8].broadcast_to((128, 8, 64)), ALU.mult),
                reads=[f"rb{h}", fck], writes=["xpr"])
            p.op("ve", lambda e, en=en: e.tensor_tensor(
                xnb[:], en[:, 1:513],
                en[:, 32:513:64].broadcast_to((128, 8, 64)), ALU.subtract),
                reads=[ek, "xnb"], writes=["xnb"])
            p.op("sc", lambda e, h=h: e.activation(
                posf[:, h, :], xnb[:], AF.Exp),
                reads=["xnb", f"posf{h}"], writes=[f"posf{h}"])
            p.op("ve", lambda e, h=h: e.tensor_tensor(
                kbuf[:, h, :], kbuf[:, h, :], posf[:, h, :], ALU.mult),
                reads=[f"kbuf{h}", f"posf{h}"], writes=[f"kbuf{h}"])
            p.op("ve", lambda e, h=h, fcall=fcall: e.tensor_tensor(
                ksb2, kbuf[:, h, :],
                fcall[:, 8:16].broadcast_to((128, 8, 64)), ALU.mult),
                reads=[f"kbuf{h}", fck], writes=["xpk"])

        def scan_pre(h):
            for c in range(NCT):
                t0 = c * C
                pb_ = 64 * (c % 2)
                q = c // 2
                p.op("te", lambda e, t0=t0, pb_=pb_, q=q: e.transpose(
                    g.PS_KT[pb_:pb_ + 64, 128 * q:128 * q + 128],
                    ksb2[:, t0:t0 + C], id128b[:]),
                    reads=["xpk", "id128b"], writes=[f"pskt{c % 2}"])
            for pp in range(2):
                pb_ = 64 * pp
                p.op("sc", lambda e, pb_=pb_: e.activation(
                    ktbq[pb_:pb_ + 64, :], g.PS_KT[pb_:pb_ + 64, :],
                    AF.Copy),
                    reads=[f"pskt{pp}"], writes=["tmpb"])
            for c in range(NCT):
                t0 = c * C
                pb_ = 64 * (c % 2)
                q = c // 2
                bank, bk = g.PS[2 + c % 2], f"ps{2 + c % 2}"
                p.op("te", lambda e, h=h, t0=t0, pb_=pb_, q=q, bank=bank:
                     e.matmul(bank[pb_:pb_ + 64, 64 * q:64 * q + 64],
                              kbuf[:, h, t0:t0 + C], rab2[:, t0:t0 + C],
                              start=True, stop=True),
                     reads=[f"kbuf{h}", "xpr"], writes=[bk])
                p.op("te", lambda e, h=h, t0=t0, pb_=pb_, q=q, bank=bank:
                     e.matmul(bank[pb_:pb_ + 64, 384 + q:385 + q],
                              rkb[:, h, t0:t0 + C], ubon[:, h:h + 1],
                              start=True, stop=True),
                     reads=["sqr", "ubon"], writes=[bk])
            for pp in range(2):
                pb_ = 64 * pp
                bank, bk = g.PS[2 + pp], f"ps{2 + pp}"
                p.op("ve", lambda e, pb_=pb_, bank=bank: e.tensor_copy(
                    dcolb[pb_:pb_ + 64, 0:4], bank[pb_:pb_ + 64, 384:388]),
                    reads=[bk], writes=[f"dcolb{pp}"])
                for q in range(4):
                    p.op("ve", lambda e, pb_=pb_, q=q: e.tensor_scalar(
                        dtb[pb_:pb_ + 64, 64 * q:64 * q + 64],
                        id64b[pb_:pb_ + 64, :], dcolb[pb_:pb_ + 64, q:q + 1],
                        None, ALU.mult),
                        reads=["id64b"], sreads=[f"dcolb{pp}"],
                        writes=["u2"])
                p.op("ve", lambda e, pb_=pb_, bank=bank: e.select(
                    apb[pb_:pb_ + 64, 0:256],
                    msku[pb_:pb_ + 64, :][:, None, :].broadcast_to(
                        (64, 4, 64)),
                    bank[pb_:pb_ + 64, 0:256], dtb[pb_:pb_ + 64, 0:256]),
                    reads=[bk, "msku", "u2"], writes=["u2"],
                    eager=True)

        def scan_state(h):
            fcall, fck = fcall2[h % 2], f"fcall{h % 2}"

            def m_mm(c):
                pb_ = 64 * (c % 2)
                q = c // 2
                sl_ = 256 * ((c // 2) % 2)
                mbank, mk = g.PS[4 + c % 2], f"ps{4 + c % 2}"
                p.op("te", lambda e, h=h, q=q, pb_=pb_, sl_=sl_,
                     mbank=mbank: e.matmul(
                         mbank[:, sl_:sl_ + 256],
                         ktbq[pb_:pb_ + 64, 128 * q:128 * q + 128],
                         vtm[pb_:pb_ + 64, q, h * 256:(h + 1) * 256],
                         start=True, stop=True),
                     reads=["tmpb", "vtm"], writes=[mk])

            for c in range(4):
                m_mm(c)
            for c in range(NCT):
                t0 = c * C
                pb_ = 64 * (c % 2)
                q = c // 2
                st = c // 2
                sl_ = 256 * ((c // 2) % 2)
                bank, bk = g.PS[2 + c % 2], f"ps{2 + c % 2}"
                mbank, mk = g.PS[4 + c % 2], f"ps{4 + c % 2}"
                sprev_key = f"scar{h}" if c == 0 else f"spp{(c - 1) % 2}"
                sprev = (scar[:, h, :] if c == 0 else spp[(c - 1) % 2][:])
                sout_key = f"scar{h}" if c == NCT - 1 else f"spp{c % 2}"
                sout = (scar[:, h, :] if c == NCT - 1 else spp[c % 2][:])
                p.op("ve", lambda e, sprev=sprev, sout=sout, c=c, sl_=sl_,
                     mbank=mbank, fcall=fcall:
                     e.scalar_tensor_tensor(sout, sprev,
                                            fcall[:, 16 + c:17 + c],
                                            mbank[:, sl_:sl_ + 256],
                                            ALU.mult, ALU.add),
                     reads=[sprev_key, mk], sreads=[fck],
                     writes=[sout_key])
                if c + 4 < NCT:
                    m_mm(c + 4)
                for dvc in range(2):
                    p.op("te", lambda e, h=h, st=st, pb_=pb_, q=q, dvc=dvc,
                         bank=bank: e.matmul(
                             bank[:, 256 + dvc * 64:256 + dvc * 64 + 64],
                             vtm[pb_:pb_ + 64, st,
                                 h * 256 + dvc * 128:h * 256 + (dvc + 1) * 128],
                             apb[pb_:pb_ + 64, 64 * q:64 * q + 64],
                             start=True, stop=False),
                         reads=["vtm", "u2"], writes=[bk])
                    p.op("te", lambda e, sprev=sprev, dvc=dvc, h=h, t0=t0,
                         bank=bank: e.matmul(
                             bank[:, 256 + dvc * 64:256 + dvc * 64 + 64],
                             sprev[:, dvc * 128:(dvc + 1) * 128],
                             rb[:, h, t0:t0 + C], start=False, stop=True),
                         reads=[sprev_key, f"rb{h}"], writes=[bk])
                p.op("sc", lambda e, h=h, t0=t0, bank=bank: e.activation(
                    ob[:, 2 * h:2 * h + 2, t0:t0 + C],
                    bank[:, 256:384], AF.Copy),
                    reads=[bk], writes=[f"dob{2 * h}", f"dob{2 * h + 1}"])

        scan_factors(0)
        for h in range(NH):
            scan_pre(h)
            if h + 1 < NH:
                scan_factors(h + 1)
            scan_state(h)

        p.op("sc", lambda e: e.activation(flat(sqr), flat(ob), AF.Square),
             reads=[f"dob{i}" for i in range(NCH)], writes=["sqr"])
        def gn_banks(h):
            pbS, pkS = ((g.PS[5], "ps5") if h % 2 == 0 else (g.PS[2], "ps2"))
            pbQ, pkQ = ((g.PS[6], "ps6") if h % 2 == 0 else (g.PS[3], "ps3"))
            return pbS, pkS, pbQ, pkQ

        def gn_stats(h):
            pbS, pkS, pbQ, pkQ = gn_banks(h)
            for (src_t, key, pb, pk) in ((ob, "dob", pbS, pkS),
                                         (sqr, "sqr", pbQ, pkQ)):
                for cc in range(2):
                    p.op("te", lambda e, h=h, cc=cc, src_t=src_t, pb=pb,
                         key=key, pk=pk: e.matmul(
                             pb[0:1, 0:512], g.ones_b[:],
                             src_t[:, 2 * h + cc, :],
                             start=(cc == 0), stop=(cc == 1)),
                         reads=[(f"dob{2 * h + cc}" if key == "dob"
                                 else key), "ones_b"], writes=[pk])

        gn_stats(0)
        for h in range(NH):
            pbS, pkS, pbQ, pkQ = gn_banks(h)
            if h + 1 < NH:
                gn_stats(h + 1)
            rw = g.rowwork
            p.op("sc", lambda e, pbS=pbS: e.activation(
                rw[0:1, 0:512], pbS[0:1, 0:512], AF.Square,
                scale=1.0 / DV), reads=[pkS], writes=["rw0"])
            p.op("ve", lambda e, pbQ=pbQ: e.tensor_scalar(
                rw[0:1, 512:1024], pbQ[0:1, 0:512],
                1.0 / DV, EPS, ALU.mult, ALU.add),
                reads=[pkQ], writes=["rw1"])
            p.op("ve", lambda e: e.scalar_tensor_tensor(
                rw[0:1, 1024:1536], rw[0:1, 0:512], -1.0, rw[0:1, 512:1024],
                ALU.mult, ALU.add), reads=["rw0", "rw1"], writes=["rw2"])
            p.op("sc", lambda e: e.activation(rw[0:1, 0:512],
                                              rw[0:1, 1024:1536], AF.Ln),
                 reads=["rw2"], writes=["rw0"])
            p.op("sc", lambda e: e.activation(g.row_s[:], rw[0:1, 0:512],
                                              AF.Exp, scale=-0.5),
                 reads=["rw0"], writes=["row_s"])
            p.op("ve", lambda e, pbS=pbS: e.scalar_tensor_tensor(
                g.row_ms[:], g.row_s[:], 1.0 / DV, pbS[0:1, 0:512],
                ALU.mult, ALU.mult),
                reads=["row_s", pkS], writes=["row_ms"])
            p.op("ve", lambda e: e.tensor_copy(grow[0:1, :], g.row_ms[:]),
                 reads=["row_ms"], writes=["grow"])
            p.op("te", lambda e: e.matmul(
                g.PS[0][:, 0:512], g.onesf[:], g.row_s[:],
                start=True, stop=True),
                reads=["onesf", "row_s"], writes=["ps0"])
            for dvc in range(2):
                ch = 2 * h + dvc
                pbB, pkB = ((g.PS[1], "ps1") if dvc == 0 else
                            (g.PS[4], "ps4"))
                t_b, tk = ((tmpb[:], "tmpb") if dvc == 0 else
                           (posf[:, 2, :], "posf2"))
                x_b, xk = ((xnb[:], "xnb") if dvc == 0 else
                           (posf[:, 3, :], "posf3"))
                p.op("te", lambda e, ch=ch, pbB=pbB: e.matmul(
                    pbB[:, 0:512], gnwr[0:2, ch * 128:(ch + 1) * 128],
                    grow[:], start=True, stop=True),
                    reads=["gnwr", "grow", "grow_c"], writes=[pkB])
                p.op("ve", lambda e, ch=ch, t_b=t_b: e.tensor_tensor(
                    t_b, ob[:, ch, :], g.PS[0][:, 0:512], ALU.mult),
                    reads=[f"dob{ch}", "ps0"], writes=[tk])
                p.op("ve", lambda e, ch=ch, pbB=pbB, t_b=t_b, x_b=x_b:
                     e.scalar_tensor_tensor(
                         x_b, t_b, gncol[:, ch:ch + 1], pbB[:, 0:512],
                         ALU.mult, ALU.subtract),
                    reads=[tk, pkB], sreads=["gncol"], writes=[xk])
                p.op("gp", lambda e, ch=ch, x_b=x_b: e.tensor_tensor(
                    ob[:, ch, :], x_b, gsb[:, ch, :], ALU.mult),
                    reads=[xk, "gxb"], writes=[f"dob{ch}"])
        for m in range(NCH):
            pb, pk = g.PS[m % 2], f"ps{m % 2}"
            for k in range(NCH):
                p.op("te", lambda e, m=m, k=k, pb=pb: e.matmul(
                    pb[:, 0:512], wo[:, k, m * 128:(m + 1) * 128],
                    ob[:, k, :], start=(k == 0), stop=(k == NCH - 1)),
                    reads=[f"dob{k}", "wo"], writes=[pk])
            p.op("ve", lambda e, m=m, pb=pb, xcur=xcur: e.tensor_tensor(
                xcur[:, m, :], pb[:, 0:512], xcur[:, m, :], ALU.add),
                reads=[pk, kx], writes=[kx])
        p.op("sy", lambda e, b=b, sl=sl, xcur=xcur: e.dma_start(
            x2_o[b].rearrange("(c p) t -> p c t", p=128)[:, :, sl],
            xcur[:]),
            reads=[kx], writes=["x2d"], dma=True)
        p.op("gp", lambda e, xcur=xcur: e.tensor_copy(flat(gxb),
                                                      flat(xcur)),
             reads=[kx], writes=["gxb"])
        stats_tile(p, g, xcur, gxb, sqr, kx, "gxb", True, rows3_o, tt)
        xn_max_tile(p, g, xcur, kx, gcol, bcol, xnb, mxc, tt)

    reduce_scalar_max(p, g, mxc[:], "mxc", id128f, lmax2_o)
    p.finish()
    p.emit()
    return nc


_NC_CACHE = {}


def _get_nc(which):
    if which not in _NC_CACHE:
        _NC_CACHE[which] = {"L1": build_L1, "L2": build_L2, "L3": build_L3}[which]()
    return _NC_CACHE[which]


def _pc(v, nch=NCH):
    return np.ascontiguousarray(np.asarray(v, np.float32).reshape(nch, 128).T)


def kernel(L0_attn_hidden_states, L0_attn_mu0, L0_attn_W_x0, L0_attn_W_x2,
           L0_attn_x_bias, L0_attn_W_r, L0_attn_W_w1, L0_attn_W_w2,
           L0_attn_b_w2, L0_attn_W_k, L0_attn_W_v, L0_attn_W_g,
           L0_attn_bonus, L0_attn_gn_weight, L0_attn_gn_bias, L0_attn_W_o,
           L0_ffn_weight, L0_ffn_gamma, L0_ffn_beta):
    f32 = np.float32
    x = np.asarray(L0_attn_hidden_states, f32)
    xfm = np.ascontiguousarray(x.transpose(0, 2, 1))
    cores = list(range(NCORES))

    identf = np.eye(128, dtype=f32)
    ident64d = np.ascontiguousarray(
        np.concatenate([np.eye(64, dtype=f32)] * 2, axis=0))
    jj = np.arange(128)[:, None] % 64
    ii = np.arange(64)[None, :]
    masku = (jj < ii).astype(np.uint8)

    wffnT = np.ascontiguousarray(np.asarray(L0_ffn_weight, f32).T)
    gammapc = _pc(L0_ffn_gamma)
    betapc = _pc(L0_ffn_beta)

    nc1 = _get_nc("L1")
    in1 = []
    for c in cores:
        in1.append(dict(
            x0=np.ascontiguousarray(xfm[2 * c:2 * c + 2]),
            wffnT=wffnT, gammapc=gammapc, betapc=betapc, identf=identf))
    r1 = run_bass_kernel_spmd(nc1, in1, core_ids=cores).results
    lmax1_all = np.ascontiguousarray(
        np.stack([r1[c]["lmax"][0] for c in cores], axis=1))

    wx2 = np.asarray(L0_attn_W_x2, f32).reshape(D, 5, 32)
    wx2T = np.ascontiguousarray(wx2.transpose(1, 2, 0).reshape(160, D))
    xbias = np.asarray(L0_attn_x_bias, f32)
    xbiaspc = np.ascontiguousarray(
        xbias.reshape(5, NCH, 128).transpose(2, 0, 1))
    nc2 = _get_nc("L2")
    in2 = []
    for c in cores:
        in2.append(dict(
            x0=in1[c]["x0"],
            tern=r1[c]["tern"], alpha=r1[c]["alpha"], rows1=r1[c]["rows1"],
            lmax_all=lmax1_all,
            gammapc=gammapc, betapc=betapc, identf=identf,
            ident64d=ident64d, masku=masku,
            wx0T=np.ascontiguousarray(np.asarray(L0_attn_W_x0, f32).T),
            wx2T=wx2T,
            wrT=np.ascontiguousarray(np.asarray(L0_attn_W_r, f32).T),
            ww1T=np.ascontiguousarray(np.asarray(L0_attn_W_w1, f32).T),
            ww2T=np.ascontiguousarray(np.asarray(L0_attn_W_w2, f32).T),
            wkT=np.ascontiguousarray(np.asarray(L0_attn_W_k, f32).T),
            wvT=np.ascontiguousarray(np.asarray(L0_attn_W_v, f32).T),
            wgT=np.ascontiguousarray(np.asarray(L0_attn_W_g, f32).T),
            woT=np.ascontiguousarray(np.asarray(L0_attn_W_o, f32).T),
            mu0pc=_pc(L0_attn_mu0),
            xbiaspc=xbiaspc,
            bw2pc=_pc(L0_attn_b_w2, 4),
            bonusT=np.ascontiguousarray(np.asarray(L0_attn_bonus, f32).T),
            gnwrow=np.ascontiguousarray(np.stack(
                [np.asarray(L0_attn_gn_weight, f32),
                 np.asarray(L0_attn_gn_bias, f32)])),
            gncolw=_pc(L0_attn_gn_weight),
        ))
    r2 = run_bass_kernel_spmd(nc2, in2, core_ids=cores).results
    lmax2_all = np.ascontiguousarray(
        np.stack([r2[c]["lmax2"][0] for c in cores], axis=1))

    nc3 = _get_nc("L3")
    in3 = []
    for c in cores:
        in3.append(dict(
            x2=r2[c]["x2"], tern=r1[c]["tern"], alpha=r1[c]["alpha"],
            rows3=r2[c]["rows3"], lmax_all=lmax2_all,
            gammapc=gammapc, betapc=betapc))
    r3 = run_bass_kernel_spmd(nc3, in3, core_ids=cores).results

    out_fm = np.concatenate([r3[c]["out"] for c in cores], axis=0)
    return np.ascontiguousarray(out_fm.transpose(0, 2, 1))


# version 12
# speedup vs baseline: 1.0193x; 1.0193x over previous
import numpy as np
import ml_dtypes
import concourse.bass as bass
import concourse.mybir as mybir
from concourse.bass_utils import run_bass_kernel_spmd

F32 = mybir.dt.float32
BF16 = mybir.dt.bfloat16
U8 = mybir.dt.uint8
ALU = mybir.AluOpType
AF = mybir.ActivationFunctionType
AX = mybir.AxisListType

NCORES = 8
B_LOC = 2
T = 2048
TT = 512
NTPS = T // TT
NTILE = B_LOC * NTPS
D = 1024
NCH = 8
NH = 4
DK = 128
DV = 256
C = 64
NCT = TT // C
EPS = 1e-5
MAGIC = 12582912.0

ENGS = ("sy", "ve", "sc", "te", "gp")
PARANOID = [False]
AUDIT = [False]


class Prog:

    def __init__(self, nc):
        self.nc = nc
        self.ops = {e: [] for e in ENGS}
        self.nop = {e: 0 for e in ENGS}
        self.last_w = {}
        self.readers = {}
        self.waited = {}
        self.audited = set()

    def op(self, eng, fn, reads=(), writes=(), dma=False, tail=False,
           sreads=(), eager=False):
        idx = self.nop[eng]
        waits = set()

        def dep(peng, pidx, ptail=False, force=False):
            if peng == eng and not (ptail or force):
                return
            if self.waited.get((eng, peng), -1) >= pidx:
                return
            if peng == eng and pidx == idx:
                return
            waits.add((peng, pidx))

        para = PARANOID[0]
        if AUDIT[0]:
            for k in list(reads) + list(sreads):
                if k not in self.last_w and k not in self.audited:
                    self.audited.add(k)
                    print(f"AUDIT read-before-write: {k} at {eng}#{idx}")
        for k in sreads:
            if k in self.last_w:
                dep(*self.last_w[k], force=True)
        for k in reads:
            if k in self.last_w:
                dep(*self.last_w[k], force=eager or para)
        for k in writes:
            if k in self.last_w:
                dep(*self.last_w[k][:2], force=para)
            for (re_, ri) in self.readers.get(k, ()):
                dep(re_, ri, force=para)
        for (peng, pidx) in waits:
            if pidx > self.waited.get((eng, peng), -1):
                self.waited[(eng, peng)] = pidx
        for k in list(reads) + list(sreads):
            self.readers.setdefault(k, []).append((eng, idx))
        for k in writes:
            self.last_w[k] = (eng, idx, tail)
            self.readers[k] = []
        self.ops[eng].append([fn, waits, dma])
        self.nop[eng] += 1

    def emit(self):
        nc = self.nc
        needed = {e: set() for e in ENGS}
        for e in ENGS:
            for rec in self.ops[e]:
                for (pe, pi) in rec[1]:
                    needed[pe].add(pi)
        sems = {e: nc.alloc_semaphore("sem_" + e) for e in ENGS}
        NPOOL = 12
        pools = {e: [nc.alloc_semaphore(f"dp_{e}{i}") for i in range(NPOOL)]
                 for e in ("sy", "gp")}
        pool_val = {e: [0] * NPOOL for e in ("sy", "gp")}
        pool_i = {e: 0 for e in ("sy", "gp")}
        mile = {}
        cnt = {e: 0 for e in ENGS}
        prewait = {}
        for e in ENGS:
            for i, rec in enumerate(self.ops[e]):
                fn, waits, dma = rec
                if fn is None:
                    continue
                if dma:
                    j = pool_i[e] % NPOOL
                    pool_i[e] += 1
                    s = pools[e][j]
                    if pool_val[e][j] > 0:
                        prewait[(e, i)] = (s, pool_val[e][j])
                    pool_val[e][j] += 16
                    mile[(e, i)] = (s, pool_val[e][j])
                elif i in needed[e]:
                    cnt[e] += 1
                    mile[(e, i)] = (sems[e], cnt[e])
                else:
                    mile[(e, i)] = None

        with nc.Block() as blk:
            def body_for(e):
                def body(eng):
                    for i, rec in enumerate(self.ops[e]):
                        fn, waits, dma = rec
                        for (pe, pi) in sorted(waits):
                            ms = mile[(pe, pi)]
                            eng.wait_ge(ms[0], ms[1])
                        if fn is None:
                            continue
                        if (e, i) in prewait:
                            pw = prewait[(e, i)]
                            eng.wait_ge(pw[0], pw[1])
                        inst = fn(eng)
                        ms = mile[(e, i)]
                        if ms is not None:
                            inst.then_inc(ms[0], 16 if dma else 1)
                return body
            blk.sync(body_for("sy"))
            blk.vector(body_for("ve"))
            blk.scalar(body_for("sc"))
            blk.tensor(body_for("te"))
            blk.gpsimd(body_for("gp"))
        return sems

    def finish(self):
        waits = set()
        for pe in ENGS:
            if self.nop[pe] == 0:
                continue
            if pe != "sy":
                waits.add((pe, self.nop[pe] - 1))
        for e in ("sy", "gp"):
            for i in range(len(self.ops[e]) - 1, -1, -1):
                if self.ops[e][i][2]:
                    waits.add((e, i))
                    break
        self.ops["sy"].append([None, waits, False])


def flat(t):
    ap = t[:]
    if len(ap.shape) == 3:
        return ap.rearrange("p a b -> p (a b)")
    return ap


class G:
    pass


def alloc_generic(nc, g):
    g.PS = [nc.alloc_psum_tensor(f"ps{i}", [128, 512], F32) for i in range(7)]
    g.PS_KT = nc.alloc_psum_tensor("pskt", [128, 512], BF16)
    g.ones_b = nc.alloc_sbuf_tensor("ones_b", [128, 1], BF16)
    g.onesf = nc.alloc_sbuf_tensor("onesf", [1, 128], F32)
    g.zeros = nc.alloc_sbuf_tensor("zeros", [128, 512], BF16)
    g.row_s = nc.alloc_sbuf_tensor("row_s", [1, 512], F32)
    g.row_ms = nc.alloc_sbuf_tensor("row_ms", [1, 512], F32)
    g.rowb1 = nc.alloc_sbuf_tensor("rowb1", [1, 512], BF16)
    g.rowwork = nc.alloc_sbuf_tensor("rowwork", [1, 1536], F32)
    g.scal = nc.alloc_sbuf_tensor("scal", [1, 16], F32)
    g.cols = nc.alloc_sbuf_tensor("colsb", [128, 96], F32)
    g.sbc = nc.alloc_sbuf_tensor("sbc", [128, 512], F32)
    g.msbc = nc.alloc_sbuf_tensor("msbc", [128, 512], F32)
    g.u1 = nc.alloc_sbuf_tensor("u1", [128, 512], F32)
    g.u2 = nc.alloc_sbuf_tensor("u2", [128, 512], F32)


def init_consts(p, g):
    p.op("ve", lambda e: e.memset(g.ones_b[:], 1.0), writes=["ones_b"])
    p.op("ve", lambda e: e.memset(g.onesf[:], 1.0), writes=["onesf"])
    p.op("ve", lambda e: e.memset(g.zeros[:], 0.0), writes=["zeros"])


def ln_stats_mm(p, g, xb_key, xb_ap, sq_key, sq_ap):
    for k in range(NCH):
        p.op("te", lambda e, k=k: e.matmul(
            g.PS[4][0:1, :], g.ones_b[:], xb_ap(k),
            start=(k == 0), stop=(k == NCH - 1)),
            reads=[xb_key, "ones_b"], writes=["ps4"])
    for k in range(NCH):
        p.op("te", lambda e, k=k: e.matmul(
            g.PS[5][0:1, :], g.ones_b[:], sq_ap(k),
            start=(k == 0), stop=(k == NCH - 1)),
            reads=[sq_key, "ones_b"], writes=["ps5"])


def ln_rows(p, g, composite, rows_dram, tt):
    a = (1.0 + EPS) if composite else 1.0
    e2 = (EPS * EPS) if composite else EPS
    rw = g.rowwork
    p.op("sc", lambda e: e.activation(rw[0:1, 0:512], g.PS[4][0:1, :],
                                      AF.Square, scale=1.0 / D),
         reads=["ps4"], writes=["rw0"])
    p.op("ve", lambda e: e.tensor_scalar(rw[0:1, 512:1024], g.PS[5][0:1, :],
                                         a / D, e2, ALU.mult, ALU.add),
         reads=["ps5"], writes=["rw1"])
    p.op("ve", lambda e: e.scalar_tensor_tensor(
        rw[0:1, 1024:1536], rw[0:1, 0:512], -a, rw[0:1, 512:1024],
        ALU.mult, ALU.add), reads=["rw0", "rw1"], writes=["rw2"])
    p.op("sc", lambda e: e.activation(rw[0:1, 0:512], rw[0:1, 1024:1536],
                                      AF.Ln),
         reads=["rw2"], writes=["rw0"])
    p.op("sc", lambda e: e.activation(g.row_s[:], rw[0:1, 0:512], AF.Exp,
                                      scale=-0.5),
         reads=["rw0"], writes=["row_s"])
    p.op("ve", lambda e: e.scalar_tensor_tensor(
        g.row_ms[:], g.row_s[:], 1.0 / D, g.PS[4][0:1, :],
        ALU.mult, ALU.mult),
        reads=["row_s", "ps4"], writes=["row_ms"])
    if rows_dram is not None:
        sl = slice(tt * TT, (tt + 1) * TT)
        p.op("sy", lambda e, sl=sl: e.dma_start(rows_dram[0:1, sl],
                                                g.row_s[:]),
             reads=["row_s"], writes=["rows_dram_st"], dma=True)
        p.op("sy", lambda e, sl=sl: e.dma_start(rows_dram[1:2, sl],
                                                g.row_ms[:]),
             reads=["row_ms"], writes=["rows_dram_st"], dma=True)


def bcast_rows(p, g):
    p.op("te", lambda e: e.matmul(g.PS[5][:, 0:512], g.onesf[:], g.row_s[:],
                                  start=True, stop=True),
         reads=["row_s", "onesf"], writes=["ps5"])
    p.op("te", lambda e: e.matmul(g.PS[6][:, 0:512], g.onesf[:], g.row_ms[:],
                                  start=True, stop=True),
         reads=["row_ms", "onesf"], writes=["ps6"])
    p.op("sc", lambda e: e.activation(g.sbc[:], g.PS[5][:, 0:512], AF.Copy),
         reads=["ps5"], writes=["sbc"])
    p.op("sc", lambda e: e.activation(g.msbc[:], g.PS[6][:, 0:512], AF.Copy),
         reads=["ps6"], writes=["msbc"])


def reduce_scalar_max(p, g, mx_ap, mx_key, id128f, out_dram):
    p.op("ve", lambda e: e.memset(g.cols[:], 0.0), writes=["colsb"])
    p.op("ve", lambda e: e.reduce_max(g.cols[:, 0:1], mx_ap, axis=AX.X),
         reads=[mx_key], writes=["colsb"], tail=True)
    p.op("te", lambda e: e.transpose(g.PS[0][0:96, 0:128], g.cols[:, 0:96],
                                     id128f[:]),
         reads=["colsb", "id128f"], writes=["ps0"])
    p.op("ve", lambda e: e.reduce_max(g.scal[0:1, 0:1], g.PS[0][0:1, 0:128],
                                      axis=AX.X),
         reads=["ps0"], writes=["gmax"], tail=True)
    p.op("sy", lambda e: e.dma_start(out_dram, g.scal[0:1, 0:1]),
         reads=["gmax"], writes=["lmax_st"], dma=True)


def stats_tile(p, g, xt, xb, sqb, kx, kb, composite, rows_dram, tt,
               ksq="sqr"):
    p.op("sc", lambda e: e.activation(flat(sqb), flat(xt), AF.Square),
         reads=[kx], writes=[ksq])
    ln_stats_mm(p, g, kb, lambda k: xb[:, k, :], ksq,
                lambda k: sqb[:, k, :])
    ln_rows(p, g, composite, rows_dram, tt)


def xn_max_tile(p, g, xt, kx, gcol, bcol, xnb, mxc, tt):
    bcast_rows(p, g)
    for k in range(NCH):
        p.op("ve", lambda e, k=k: e.tensor_tensor(
            g.u1[:], xt[:, k, :], g.sbc[:], ALU.mult),
            reads=[kx, "sbc"], writes=["u1"])
        p.op("ve", lambda e: e.tensor_tensor(g.u2[:], g.u1[:], g.msbc[:],
                                             ALU.subtract),
             reads=["u1", "msbc"], writes=["u2"])
        p.op("sc", lambda e, k=k: e.activation(
            xnb[:], g.u2[:], AF.Identity, scale=gcol[:, k:k + 1],
            bias=bcol[:, k:k + 1]),
            reads=["u2", "gbcols"], writes=["xnb"])
        p.op("ve", lambda e, tt=tt, k=k: e.reduce_max(
            mxc[:, tt * NCH + k:tt * NCH + k + 1], xnb[:], axis=AX.X,
            apply_absolute_value=True),
            reads=["xnb"], writes=["mxc"], tail=True)


def emit_stats_pass(p, g, xsrc, rows_dram, gcol, bcol, xt, xb, sqb, xnb, mxc,
                    id128f, lmax_o, composite=True, src_key="x0d",
                    kx="fxt", kb="gxb", ksq="sqr"):
    p.op("ve", lambda e: e.memset(mxc[:], 0.0), writes=["mxc"])
    for tt in range(NTILE):
        b, tau = divmod(tt, NTPS)
        sl = slice(tau * TT, (tau + 1) * TT)
        xv = xsrc[b].rearrange("(c p) t -> p c t", p=128)[:, :, sl]
        p.op("sy", lambda e, xv=xv: e.dma_start(xt[:], xv),
             reads=[src_key], writes=[kx], dma=True)
        p.op("gp", lambda e, xv=xv: e.dma_start(xb[:], xv),
             reads=[src_key], writes=[kb], dma=True)
        stats_tile(p, g, xt, xb, sqb, kx, kb, composite, rows_dram, tt,
                   ksq=ksq)
        xn_max_tile(p, g, xt, kx, gcol, bcol, xnb, mxc, tt)
    reduce_scalar_max(p, g, mxc[:], "mxc", id128f, lmax_o)


def emit_quant_prologue(p, g, lmax_all, alpha_in, lm_sb, al_sb, invsc_col,
                        epi_col):
    p.op("sy", lambda e: e.dma_start(lm_sb[:], lmax_all),
         writes=["lm_sb"], dma=True)
    p.op("sy", lambda e: e.dma_start(al_sb[:], alpha_in),
         writes=["al_sb"], dma=True)
    p.op("ve", lambda e: e.reduce_max(g.scal[0:1, 8:9], lm_sb[:], axis=AX.X),
         reads=["lm_sb"], writes=["q0"], tail=True)
    p.op("ve", lambda e: e.tensor_scalar(g.scal[0:1, 9:10], g.scal[0:1, 8:9],
                                         1e-8, None, ALU.max),
         reads=["q0"], writes=["q1"])
    p.op("ve", lambda e: e.reciprocal(g.scal[0:1, 10:11], g.scal[0:1, 9:10]),
         reads=["q1"], writes=["q2"], tail=True, eager=True)
    p.op("ve", lambda e: e.tensor_scalar(g.scal[0:1, 11:12],
                                         g.scal[0:1, 10:11], 127.0, None,
                                         ALU.mult),
         reads=["q2"], writes=["q3"])
    p.op("ve", lambda e: e.scalar_tensor_tensor(
        g.scal[0:1, 12:13], g.scal[0:1, 9:10], 0.5 / 127.0, al_sb[0:1, 0:1],
        ALU.mult, ALU.mult), reads=["q1", "al_sb"], writes=["q4"])
    p.op("te", lambda e: e.matmul(g.PS[5][:, 0:2], g.onesf[:],
                                  g.scal[0:1, 11:13], start=True, stop=True),
         reads=["q3", "q4", "onesf"], writes=["ps5"])
    p.op("ve", lambda e: e.tensor_copy(invsc_col[:], g.PS[5][:, 0:1]),
         reads=["ps5"], writes=["invsc"])
    p.op("ve", lambda e: e.tensor_copy(epi_col[:], g.PS[5][:, 1:2]),
         reads=["ps5"], writes=["epi"])


def emit_ffnc(p, g, xsrc, xdst, rows_in, tern_sb, gcol, bcol, invsc_col,
              epi_col, xt, qb, with_stats=False, rows2_dram=None,
              xb=None, sqb=None, src_key="x0d", dst_key="x1d",
              kx="fxt", kq="xiq", kb="gxb", ksq="sqr"):
    for tt in range(NTILE):
        b, tau = divmod(tt, NTPS)
        sl = slice(tau * TT, (tau + 1) * TT)
        xv = xsrc[b].rearrange("(c p) t -> p c t", p=128)[:, :, sl]
        p.op("sy", lambda e, xv=xv: e.dma_start(xt[:], xv),
             reads=[src_key], writes=[kx], dma=True)
        p.op("sy", lambda e, tt=tt: e.dma_start(
            g.row_s[:], rows_in[0:1, tt * TT:(tt + 1) * TT]),
            writes=["row_s"], dma=True)
        p.op("sy", lambda e, tt=tt: e.dma_start(
            g.row_ms[:], rows_in[1:2, tt * TT:(tt + 1) * TT]),
            writes=["row_ms"], dma=True)
        bcast_rows(p, g)
        for k in range(NCH):
            p.op("ve", lambda e, k=k: e.tensor_tensor(
                g.u1[:], xt[:, k, :], g.sbc[:], ALU.mult),
                reads=[kx, "sbc"], writes=["u1"])
            p.op("ve", lambda e: e.tensor_tensor(g.u2[:], g.u1[:], g.msbc[:],
                                                 ALU.subtract),
                 reads=["u1", "msbc"], writes=["u2"])
            p.op("sc", lambda e, k=k: e.activation(
                g.u1[:], g.u2[:], AF.Identity, scale=gcol[:, k:k + 1],
                bias=bcol[:, k:k + 1]),
                reads=["u2", "gbcols"], writes=["u1"])
            p.op("ve", lambda e: e.tensor_scalar(
                g.u2[:], g.u1[:], invsc_col[:, 0:1], MAGIC, ALU.mult,
                ALU.add),
                reads=["u1"], sreads=["invsc"], writes=["u2"])
            p.op("ve", lambda e, k=k: e.tensor_scalar(
                qb[:, k, :], g.u2[:], MAGIC, None, ALU.subtract),
                reads=["u2"], writes=[f"{kq}{k}"])
        for m in range(NCH):
            pb = g.PS[m % 2]
            pk = f"ps{m % 2}"
            for k in range(NCH):
                p.op("te", lambda e, m=m, k=k, pb=pb: e.matmul(
                    pb[:, 0:512], tern_sb[:, k, m * 128:(m + 1) * 128],
                    qb[:, k, :], start=(k == 0), stop=(k == NCH - 1)),
                    reads=[f"{kq}{k}", "tern_res"], writes=[pk])
            p.op("sc", lambda e, pb=pb: e.activation(
                g.u1[:], pb[:, 0:512], AF.Identity, scale=epi_col[:, 0:1]),
                reads=[pk, "epi"], writes=["u1"])
            p.op("ve", lambda e, m=m: e.tensor_tensor(
                xt[:, m, :], g.u1[:], xt[:, m, :], ALU.add),
                reads=["u1", kx], writes=[kx])
        p.op("sy", lambda e, b=b, sl=sl: e.dma_start(
            xdst[b].rearrange("(c p) t -> p c t", p=128)[:, :, sl], xt[:]),
            reads=[kx], writes=[dst_key], dma=True)
        if with_stats:
            p.op("ve", lambda e: e.tensor_copy(flat(xb), flat(xt)),
                 reads=[kx], writes=[kb])
            stats_tile(p, g, xt, xb, sqb, kx, kb, False, rows2_dram, tt,
                       ksq=ksq)


def build_L1():
    nc = bass.Bass("TRN2", target_bir_lowering=False, debug=False)
    g = G()
    x0 = nc.dram_tensor("x0", [B_LOC, D, T], F32, kind="ExternalInput").ap()
    wffnT = nc.dram_tensor("wffnT", [D, D], F32, kind="ExternalInput").ap()
    gpc = nc.dram_tensor("gammapc", [128, NCH], F32, kind="ExternalInput").ap()
    bpc = nc.dram_tensor("betapc", [128, NCH], F32, kind="ExternalInput").ap()
    identf = nc.dram_tensor("identf", [128, 128], F32, kind="ExternalInput").ap()

    tern_o = nc.dram_tensor("tern", [D, D], BF16, kind="ExternalOutput").ap()
    alpha_o = nc.dram_tensor("alpha", [1, 1], F32, kind="ExternalOutput").ap()
    rows_o = nc.dram_tensor("rows1", [2, NTILE * TT], F32,
                            kind="ExternalOutput").ap()
    lmax_o = nc.dram_tensor("lmax", [1, 1], F32, kind="ExternalOutput").ap()

    alloc_generic(nc, g)
    wf = nc.alloc_sbuf_tensor("wf", [128, NCH, D], F32)
    absw = nc.alloc_sbuf_tensor("absw", [128, NCH * D], F32)
    m01 = nc.alloc_sbuf_tensor("m01", [128, NCH * D], BF16)
    sgn = nc.alloc_sbuf_tensor("sgn", [128, NCH * D], BF16)
    tern = nc.alloc_sbuf_tensor("tern_sb", [128, NCH, D], BF16)
    id128f = nc.alloc_sbuf_tensor("id128f", [128, 128], F32)
    onecol = nc.alloc_sbuf_tensor("onecol", [128, 1], F32)
    gcol = nc.alloc_sbuf_tensor("gcol", [128, NCH], F32)
    bcol = nc.alloc_sbuf_tensor("bcol", [128, NCH], F32)
    xt = nc.alloc_sbuf_tensor("sxt", [128, NCH, TT], F32)
    xb = nc.alloc_sbuf_tensor("sxb", [128, NCH, TT], BF16)
    sqb = nc.alloc_sbuf_tensor("sqb", [128, NCH, TT], BF16)
    xnb = nc.alloc_sbuf_tensor("xnb", [128, TT], F32)
    mxc = nc.alloc_sbuf_tensor("mxc", [128, NCH * NTILE], F32)

    p = Prog(nc)
    init_consts(p, g)
    p.op("sy", lambda e: e.dma_start(
        wf[:], wffnT.rearrange("(c p) o -> p c o", p=128)),
        writes=["wf"], dma=True)
    p.op("sy", lambda e: e.dma_start(id128f[:], identf),
         writes=["id128f"], dma=True)
    p.op("gp", lambda e: e.dma_start(gcol[:], gpc), writes=["gbcols"], dma=True)
    p.op("gp", lambda e: e.dma_start(bcol[:], bpc), writes=["gbcols"], dma=True)

    wfv = flat(wf)
    p.op("sc", lambda e: e.activation(absw[:], wfv, AF.Abs),
         reads=["wf"], writes=["absw"])
    p.op("sc", lambda e: e.activation(sgn[:], wfv, AF.Sign),
         reads=["wf"], writes=["sgn"])
    p.op("ve", lambda e: e.memset(g.cols[:], 0.0), writes=["colsb"])
    p.op("ve", lambda e: e.reduce_sum(g.cols[:, 0:1], absw[:], axis=AX.X),
         reads=["absw"], writes=["colsb"], tail=True)
    p.op("te", lambda e: e.transpose(g.PS[0][0:96, 0:128], g.cols[:, 0:96],
                                     id128f[:]),
         reads=["colsb", "id128f"], writes=["ps0"])
    p.op("ve", lambda e: e.reduce_sum(g.scal[0:1, 0:1], g.PS[0][0:1, 0:128],
                                      axis=AX.X), reads=["ps0"], writes=["s0"], tail=True)
    p.op("ve", lambda e: e.tensor_scalar(g.scal[0:1, 1:2], g.scal[0:1, 0:1],
                                         0.7 / (D * D), None, ALU.mult),
         reads=["s0"], writes=["s1"])
    p.op("te", lambda e: e.matmul(g.PS[1][:, 0:1], g.onesf[:],
                                  g.scal[0:1, 1:2], start=True, stop=True),
         reads=["s1", "onesf"], writes=["ps1"])
    p.op("ve", lambda e: e.tensor_copy(onecol[:], g.PS[1][:, 0:1]),
         reads=["ps1"], writes=["onecol"])
    p.op("ve", lambda e: e.tensor_scalar(m01[:], absw[:], onecol[:, 0:1],
                                         None, ALU.is_gt),
         reads=["absw"], sreads=["onecol"], writes=["m01"])
    p.op("ve", lambda e: e.tensor_tensor(absw[:], absw[:], m01[:], ALU.mult),
         reads=["absw", "m01"], writes=["absw"])
    p.op("ve", lambda e: e.reduce_sum(g.cols[:, 32:33], absw[:], axis=AX.X),
         reads=["absw"], writes=["colsb"], tail=True)
    p.op("ve", lambda e: e.reduce_sum(g.cols[:, 64:65], m01[:], axis=AX.X),
         reads=["m01"], writes=["colsb"], tail=True)
    p.op("te", lambda e: e.transpose(g.PS[0][0:96, 0:128], g.cols[:, 0:96],
                                     id128f[:]),
         reads=["colsb", "id128f"], writes=["ps0"])
    p.op("ve", lambda e: e.reduce_sum(g.scal[0:1, 2:3], g.PS[0][32:33, 0:128],
                                      axis=AX.X), reads=["ps0"], writes=["s2"], tail=True)
    p.op("ve", lambda e: e.reduce_sum(g.scal[0:1, 3:4], g.PS[0][64:65, 0:128],
                                      axis=AX.X), reads=["ps0"], writes=["s3"], tail=True)
    p.op("ve", lambda e: e.tensor_scalar(g.scal[0:1, 4:5], g.scal[0:1, 3:4],
                                         1.0, None, ALU.max),
         reads=["s3"], writes=["s4"])
    p.op("ve", lambda e: e.reciprocal(g.scal[0:1, 5:6], g.scal[0:1, 4:5]),
         reads=["s4"], writes=["s5"], tail=True, eager=True)
    p.op("ve", lambda e: e.tensor_tensor(g.scal[0:1, 6:7], g.scal[0:1, 2:3],
                                         g.scal[0:1, 5:6], ALU.mult),
         reads=["s2", "s5"], writes=["s6"])
    p.op("ve", lambda e: e.tensor_tensor(flat(tern), sgn[:], m01[:],
                                         ALU.mult),
         reads=["sgn", "m01"], writes=["tern"])
    p.op("sy", lambda e: e.dma_start(
        tern_o.rearrange("(c p) o -> p c o", p=128), tern[:]),
        reads=["tern"], writes=["tern_st"], dma=True)
    p.op("sy", lambda e: e.dma_start(alpha_o, g.scal[0:1, 6:7]),
         reads=["s6"], writes=["alpha_st"], dma=True)

    emit_stats_pass(p, g, x0, rows_o, gcol, bcol, xt, xb, sqb, xnb, mxc,
                    id128f, lmax_o, composite=True,
                    kx="sxt", kb="sxb", ksq="sqb")
    p.finish()
    p.emit()
    return nc


def build_L3():
    nc = bass.Bass("TRN2", target_bir_lowering=False, debug=False)
    g = G()
    x2 = nc.dram_tensor("x2", [B_LOC, D, T], F32, kind="ExternalInput").ap()
    tern_i = nc.dram_tensor("tern", [D, D], BF16, kind="ExternalInput").ap()
    alpha_i = nc.dram_tensor("alpha", [1, 1], F32, kind="ExternalInput").ap()
    rows_i = nc.dram_tensor("rows3", [2, NTILE * TT], F32,
                            kind="ExternalInput").ap()
    lmax_all = nc.dram_tensor("lmax_all", [1, 8], F32,
                              kind="ExternalInput").ap()
    gpc = nc.dram_tensor("gammapc", [128, NCH], F32, kind="ExternalInput").ap()
    bpc = nc.dram_tensor("betapc", [128, NCH], F32, kind="ExternalInput").ap()
    out = nc.dram_tensor("out", [B_LOC, D, T], F32, kind="ExternalOutput").ap()

    alloc_generic(nc, g)
    tern_sb = nc.alloc_sbuf_tensor("tern_res", [128, NCH, D], BF16)
    gcol = nc.alloc_sbuf_tensor("gcol", [128, NCH], F32)
    bcol = nc.alloc_sbuf_tensor("bcol", [128, NCH], F32)
    lm_sb = nc.alloc_sbuf_tensor("lm_sb", [1, 8], F32)
    al_sb = nc.alloc_sbuf_tensor("al_sb", [1, 1], F32)
    invsc_col = nc.alloc_sbuf_tensor("invsc_col", [128, 1], F32)
    epi_col = nc.alloc_sbuf_tensor("epi_col", [128, 1], F32)
    xt = nc.alloc_sbuf_tensor("fxt", [128, NCH, TT], F32)
    qb = nc.alloc_sbuf_tensor("qb", [128, NCH, TT], BF16)

    p = Prog(nc)
    init_consts(p, g)
    p.op("gp", lambda e: e.dma_start(
        tern_sb[:], tern_i.rearrange("(c p) o -> p c o", p=128)),
        writes=["tern_res"], dma=True)
    p.op("gp", lambda e: e.dma_start(gcol[:], gpc), writes=["gbcols"], dma=True)
    p.op("gp", lambda e: e.dma_start(bcol[:], bpc), writes=["gbcols"], dma=True)
    emit_quant_prologue(p, g, lmax_all, alpha_i, lm_sb, al_sb, invsc_col,
                        epi_col)
    emit_ffnc(p, g, x2, out, rows_i, tern_sb, gcol, bcol, invsc_col, epi_col,
              xt, qb, src_key="x2d", dst_key="outd")
    p.finish()
    p.emit()
    return nc


def build_L2():
    nc = bass.Bass("TRN2", target_bir_lowering=False, debug=False)
    g = G()

    def AP(n, s, d=F32, kk="ExternalInput"):
        return nc.dram_tensor(n, s, d, kind=kk).ap()

    x0 = AP("x0", [B_LOC, D, T])
    tern_i = AP("tern", [D, D], BF16)
    alpha_i = AP("alpha", [1, 1])
    rows1 = AP("rows1", [2, NTILE * TT])
    lmax_all = AP("lmax_all", [1, 8])
    gpc, bpc = AP("gammapc", [128, NCH]), AP("betapc", [128, NCH])
    identf = AP("identf", [128, 128])
    ident64d = AP("ident64d", [128, 64])
    masku = AP("masku", [128, 64], U8)
    wx0T = AP("wx0T", [D, 160])
    wx2T = AP("wx2T", [160, D])
    wrT = AP("wrT", [D, 512])
    ww1T = AP("ww1T", [D, 64])
    ww2T = AP("ww2T", [64, 512])
    wkT = AP("wkT", [D, 512])
    wvT = AP("wvT", [D, D])
    wgT = AP("wgT", [D, D])
    woT = AP("woT", [D, D])
    mu0pc = AP("mu0pc", [128, NCH])
    xbiaspc = AP("xbiaspc", [128, 5, NCH])
    bw2pc = AP("bw2pc", [128, 4])
    bonusT = AP("bonusT", [128, 4])
    gnwrow = AP("gnwrow", [2, D])
    gncolw = AP("gncolw", [128, NCH])
    x2_o = AP("x2", [B_LOC, D, T], F32, "ExternalOutput")
    rows3_o = AP("rows3", [2, NTILE * TT], F32, "ExternalOutput")
    lmax2_o = AP("lmax2", [1, 1], F32, "ExternalOutput")
    x1d = nc.dram_tensor("x1scr", [B_LOC, D, T], F32, kind="Internal").ap()
    rows2d = nc.dram_tensor("rows2scr", [2, NTILE * TT], F32,
                            kind="Internal").ap()

    alloc_generic(nc, g)
    tern_sb = nc.alloc_sbuf_tensor("tern_res", [128, NCH, D], BF16)
    wx0 = nc.alloc_sbuf_tensor("wx0", [128, NCH, 160], BF16)
    wx2 = nc.alloc_sbuf_tensor("wx2", [128, 2, D], BF16)
    wr = nc.alloc_sbuf_tensor("wr", [128, NCH, 512], BF16)
    ww1 = nc.alloc_sbuf_tensor("ww1", [128, NCH, 64], BF16)
    ww2 = nc.alloc_sbuf_tensor("ww2", [64, 512], BF16)
    wk = nc.alloc_sbuf_tensor("wk", [128, NCH, 512], BF16)
    wv = nc.alloc_sbuf_tensor("wv", [128, NCH, D], BF16)
    wg = nc.alloc_sbuf_tensor("wg", [128, NCH, D], BF16)
    wo = nc.alloc_sbuf_tensor("wo", [128, NCH, D], BF16)
    id128f = nc.alloc_sbuf_tensor("id128f", [128, 128], F32)
    id128b = nc.alloc_sbuf_tensor("id128b", [128, 128], BF16)
    id64b = nc.alloc_sbuf_tensor("id64b", [128, 64], BF16)
    msku = nc.alloc_sbuf_tensor("msku", [128, 64], U8)
    gcol = nc.alloc_sbuf_tensor("gcol", [128, NCH], F32)
    bcol = nc.alloc_sbuf_tensor("bcol", [128, NCH], F32)
    mu0c = nc.alloc_sbuf_tensor("mu0c", [128, NCH], F32)
    xbc = nc.alloc_sbuf_tensor("xbc", [128, 5, NCH], F32)
    bw2c = nc.alloc_sbuf_tensor("bw2c", [128, 4], F32)
    ubon = nc.alloc_sbuf_tensor("ubon", [128, 4], BF16)
    gnwr = nc.alloc_sbuf_tensor("gnwr", [2, D], BF16)
    gncol = nc.alloc_sbuf_tensor("gncol", [128, NCH], F32)
    lm_sb = nc.alloc_sbuf_tensor("lm_sb", [1, 8], F32)
    al_sb = nc.alloc_sbuf_tensor("al_sb", [1, 1], F32)
    invsc_col = nc.alloc_sbuf_tensor("invsc_col", [128, 1], F32)
    epi_col = nc.alloc_sbuf_tensor("epi_col", [128, 1], F32)
    xt = nc.alloc_sbuf_tensor("fxt", [128, NCH, TT], F32)
    xiq = nc.alloc_sbuf_tensor("xiq", [128, NCH, TT], BF16)
    gxb = nc.alloc_sbuf_tensor("gxb", [128, NCH, TT], BF16)
    sqr = nc.alloc_sbuf_tensor("sqr", [128, NCH, TT], BF16)
    dob = nc.alloc_sbuf_tensor("dob", [128, NCH, TT], BF16)
    xnb = nc.alloc_sbuf_tensor("xnb", [128, TT], F32)
    mxc = nc.alloc_sbuf_tensor("mxc", [128, NCH * NTILE], F32)
    xln = nc.alloc_sbuf_tensor("xln", [128, NCH, TT + 1], BF16)
    xp = nc.alloc_sbuf_tensor("xp", [128, 2, TT], BF16)
    rb = nc.alloc_sbuf_tensor("rb", [128, NH, TT], BF16)
    kbuf = nc.alloc_sbuf_tensor("kbuf", [128, NH, TT], BF16)
    w1b = nc.alloc_sbuf_tensor("w1b", [64, TT], BF16)
    posf = nc.alloc_sbuf_tensor("posf", [128, NH, TT], F32)
    vtm = nc.alloc_sbuf_tensor("vtm", [128, NTPS, D], BF16)
    tmpb = nc.alloc_sbuf_tensor("tmpb", [128, TT], BF16)
    enb2 = [nc.alloc_sbuf_tensor(f"enb{i}", [128, TT + 1], F32)
            for i in range(2)]
    encar = nc.alloc_sbuf_tensor("encar", [128, NH], F32)
    scar = nc.alloc_sbuf_tensor("scar", [128, NH, DV], BF16)
    spp = [nc.alloc_sbuf_tensor(f"spp{i}", [128, DV], BF16) for i in range(2)]
    dcolb = nc.alloc_sbuf_tensor("dcolb", [128, 4], F32)
    fcall2 = [nc.alloc_sbuf_tensor(f"fcall{i}", [128, 24], F32)
              for i in range(2)]
    grow = nc.alloc_sbuf_tensor("grow", [2, TT], BF16)
    rab2 = xp[:, 0, :]
    ksb2 = xp[:, 1, :]
    ktbq = tmpb
    u2bf = g.u2.bitcast(BF16)
    apb = u2bf[:, 0:256]
    dtb = u2bf[:, 256:512]

    p = Prog(nc)
    init_consts(p, g)
    p.op("ve", lambda e: e.memset(grow[:], -1.0), writes=["grow_c"])

    def gload(dst, src, key):
        p.op("gp", lambda e, dst=dst, src=src: e.dma_start(dst, src),
             writes=[key], dma=True)

    gload(tern_sb[:], tern_i.rearrange("(c p) o -> p c o", p=128), "tern_res")
    gload(wx0[:], wx0T.rearrange("(c p) o -> p c o", p=128), "wx0")
    gload(wx2[0:96, 0, :], wx2T[0:96, :], "wx2")
    gload(wx2[0:64, 1, :], wx2T[96:160, :], "wx2")
    gload(wr[:], wrT.rearrange("(c p) o -> p c o", p=128), "wr")
    gload(ww1[:], ww1T.rearrange("(c p) o -> p c o", p=128), "ww1")
    gload(ww2[:], ww2T, "ww2")
    gload(wk[:], wkT.rearrange("(c p) o -> p c o", p=128), "wk")
    gload(wv[:], wvT.rearrange("(c p) o -> p c o", p=128), "wv")
    gload(wg[:], wgT.rearrange("(c p) o -> p c o", p=128), "wg")
    gload(wo[:], woT.rearrange("(c p) o -> p c o", p=128), "wo")
    gload(id128b[:], identf, "id128b")
    gload(id64b[:], ident64d, "id64b")
    gload(ubon[:], bonusT, "ubon")
    gload(gnwr[:], gnwrow, "gnwr")
    p.op("sy", lambda e: e.dma_start(gncol[:], gncolw), writes=["gncol"],
         dma=True)
    p.op("sy", lambda e: e.dma_start(id128f[:], identf), writes=["id128f"],
         dma=True)
    p.op("sy", lambda e: e.dma_start(msku[:], masku), writes=["msku"],
         dma=True)
    p.op("sy", lambda e: e.dma_start(gcol[:], gpc), writes=["gbcols"],
         dma=True)
    p.op("sy", lambda e: e.dma_start(bcol[:], bpc), writes=["gbcols"],
         dma=True)
    p.op("sy", lambda e: e.dma_start(mu0c[:], mu0pc), writes=["mu0c"],
         dma=True)
    p.op("sy", lambda e: e.dma_start(xbc[:], xbiaspc), writes=["xbc"],
         dma=True)
    p.op("sy", lambda e: e.dma_start(bw2c[:], bw2pc), writes=["bw2c"],
         dma=True)

    p.op("ve", lambda e: e.memset(mxc[:], 0.0), writes=["mxc"])
    emit_quant_prologue(p, g, lmax_all, alpha_i, lm_sb, al_sb, invsc_col,
                        epi_col)
    emit_ffnc(p, g, x0, x1d, rows1, tern_sb, gcol, bcol, invsc_col, epi_col,
              xt, xiq, with_stats=True, rows2_dram=rows2d,
              xb=gxb, sqb=sqr, src_key="x0d", dst_key="x1d")

    delta = dob
    xi = xiq
    gsb = gxb
    ob = dob
    rkb = sqr
    xt2 = tern_sb.bitcast(F32)

    def xbuf(t):
        return (xt, "fxt", ["fxt"]) if t % 2 == 0 else                (xt2, "fxt1", ["fxt1", "tern_res"])

    def xload(t):
        bb, uu = divmod(t, NTPS)
        ssl = slice(uu * TT, (uu + 1) * TT)
        xv = x1d[bb].rearrange("(c p) t -> p c t", p=128)[:, :, ssl]
        dst, _, wk = xbuf(t)
        p.op("sy", lambda e, xv=xv, dst=dst: e.dma_start(dst[:], xv),
             reads=["x1d"], writes=wk, dma=True)

    xload(0)
    for tt in range(NTILE):
        b, tau = divmod(tt, NTPS)
        sl = slice(tau * TT, (tau + 1) * TT)
        xcur, kx, _ = xbuf(tt)
        if tt + 1 < NTILE:
            xload(tt + 1)
        p.op("sy", lambda e, tt=tt: e.dma_start(
            g.row_s[:], rows2d[0:1, tt * TT:(tt + 1) * TT]),
            reads=["rows_dram_st"], writes=["row_s"], dma=True)
        p.op("sy", lambda e, tt=tt: e.dma_start(
            g.row_ms[:], rows2d[1:2, tt * TT:(tt + 1) * TT]),
            reads=["rows_dram_st"], writes=["row_ms"], dma=True)
        bcast_rows(p, g)
        if tau == 0:
            p.op("ve", lambda e: e.memset(xln[:, :, 0:1], 0.0),
                 reads=["xln"], writes=["xlncar"])
            p.op("ve", lambda e: e.memset(flat(scar), 0.0),
                 reads=[f"scar{i}" for i in range(NH)],
                 writes=[f"scar{i}" for i in range(NH)])
            p.op("ve", lambda e: e.memset(encar[:], 0.0),
                 reads=["encar0", "encar1"], writes=["encar0", "encar1"])
        else:
            p.op("ve", lambda e: e.tensor_copy(xln[:, :, 0:1],
                                               xln[:, :, TT:TT + 1]),
                 reads=["xln"], writes=["xlncar"])
        for k in range(NCH):
            p.op("ve", lambda e, k=k, xcur=xcur: e.tensor_tensor(
                g.u1[:], xcur[:, k, :], g.sbc[:], ALU.mult),
                reads=[kx, "sbc"], writes=["u1"])
            p.op("ve", lambda e, k=k: e.tensor_tensor(
                xln[:, k, 1:TT + 1], g.u1[:], g.msbc[:], ALU.subtract),
                reads=["u1", "msbc"], writes=["xln"])
        p.op("ve", lambda e: e.tensor_tensor(
            delta[:], xln[:, :, 0:TT], xln[:, :, 1:TT + 1], ALU.subtract),
            reads=["xln", "xlncar"], writes=[f"dob{i}" for i in range(NCH)])
        for k in range(NCH):
            p.op("ve", lambda e, k=k: e.scalar_tensor_tensor(
                xi[:, k, :], delta[:, k, :], mu0c[:, k:k + 1],
                xln[:, k, 1:TT + 1], ALU.mult, ALU.add),
                reads=[f"dob{k}", "mu0c", "xln"], writes=[f"xiq{k}"])
        for mg, (mlo, mhi) in enumerate([(0, 96), (96, 160)]):
            pb, pk = g.PS[mg], f"ps{mg}"
            for k in range(NCH):
                p.op("te", lambda e, k=k, mlo=mlo, mhi=mhi, pb=pb: e.matmul(
                    pb[0:mhi - mlo, 0:512], wx0[:, k, mlo:mhi], xi[:, k, :],
                    start=(k == 0), stop=(k == NCH - 1)),
                    reads=[f"xiq{k}", "wx0"], writes=[pk])
            p.op("sc", lambda e, mg=mg, mlo=mlo, mhi=mhi, pb=pb: e.activation(
                xp[0:mhi - mlo, mg, :], pb[0:mhi - mlo, 0:512], AF.Tanh),
                reads=[pk], writes=["xp", "xpr", "xpk"])
        tmpb2 = xnb.bitcast(BF16)[:, 0:512]

        def emit_mu1(n, m):
            xpc, xpo = ((0, n * 32) if n < 3 else (1, (n - 3) * 32))
            pb, pk = g.PS[2 + m % 4], f"ps{2 + m % 4}"
            p.op("te", lambda e, m=m, xpc=xpc, xpo=xpo, pb=pb: e.matmul(
                pb[:, 0:512],
                wx2[xpo:xpo + 32, xpc, m * 128:(m + 1) * 128],
                xp[xpo:xpo + 32, xpc, :], start=True, stop=True),
                reads=["xp", "wx2"], writes=[pk])

        def emit_wave(n):
            for m in range(NCH):
                pb, pk = g.PS[2 + m % 4], f"ps{2 + m % 4}"
                t_b, tk = ((tmpb[:], "tmpb") if m % 2 == 0 else
                           (tmpb2, "xnb"))
                p.op("ve", lambda e, n=n, m=m, pb=pb, t_b=t_b:
                     e.scalar_tensor_tensor(
                         t_b, pb[:, 0:512], xbc[:, n, m:m + 1],
                         delta[:, m, :], ALU.add, ALU.mult),
                     reads=[pk, f"dob{m}"], sreads=["xbc"],
                     writes=[tk])
                p.op("gp" if m % 2 == 0 else "ve",
                     lambda e, m=m, t_b=t_b: e.tensor_tensor(
                         xi[:, m, :], t_b, xln[:, m, 1:TT + 1], ALU.add),
                     reads=[tk, "xln"], writes=[f"xiq{m}"])
                if m < 4:
                    emit_mu1(n, m + 4)
                elif n < 4:
                    emit_mu1(n + 1, m - 4)

        for m in range(4):
            emit_mu1(0, m)
        for n in range(5):
            emit_wave(n)
            if n in (0, 2):
                dst, dk = (rb, "rb") if n == 0 else (kbuf, "kbuf")
                wt, wkey = (wr, "wr") if n == 0 else (wk, "wk")
                for m in range(NH):
                    pb, pk = g.PS[m % 2], f"ps{m % 2}"
                    for k in range(NCH):
                        p.op("te", lambda e, m=m, k=k, pb=pb, wt=wt: e.matmul(
                            pb[:, 0:512], wt[:, k, m * 128:(m + 1) * 128],
                            xi[:, k, :], start=(k == 0), stop=(k == NCH - 1)),
                            reads=[f"xiq{k}", wkey], writes=[pk])
                    p.op("sc", lambda e, m=m, pb=pb, dst=dst, dk=dk:
                         e.activation(dst[:, m, :], pb[:, 0:512], AF.Copy),
                         reads=[pk], writes=[f"{dk}{m}"])
            elif n == 1:
                for k in range(NCH):
                    p.op("te", lambda e, k=k: e.matmul(
                        g.PS[0][0:64, 0:512], ww1[:, k, :], xi[:, k, :],
                        start=(k == 0), stop=(k == NCH - 1)),
                        reads=[f"xiq{k}", "ww1"], writes=["ps0"])
                p.op("sc", lambda e: e.activation(
                    w1b[:], g.PS[0][0:64, 0:512], AF.Tanh),
                    reads=["ps0"], writes=["w1b"])
                for m in range(NH):
                    pb, pk = g.PS[m % 2], f"ps{m % 2}"
                    p.op("te", lambda e, m=m, pb=pb: e.matmul(
                        pb[:, 0:512], ww2[:, m * 128:(m + 1) * 128], w1b[:],
                        start=True, stop=True),
                        reads=["w1b", "ww2"], writes=[pk])
                    p.op("sc", lambda e, m=m, pb=pb: e.activation(
                        posf[:, m, :], pb[:, 0:512], AF.Exp,
                        bias=bw2c[:, m:m + 1]),
                        reads=[pk, "bw2c"], writes=[f"posf{m}"])
            elif n == 3:
                for ts in range(4):
                    for nv in range(2):
                        pb, pk = g.PS[nv], f"ps{nv}"
                        for k in range(NCH):
                            p.op("te", lambda e, ts=ts, nv=nv, k=k, pb=pb:
                                 e.matmul(pb[:, 0:512],
                                          xi[:, k, ts * 128:(ts + 1) * 128],
                                          wv[:, k, nv * 512:(nv + 1) * 512],
                                          start=(k == 0),
                                          stop=(k == NCH - 1)),
                                 reads=[f"xiq{k}", "wv"], writes=[pk])
                        p.op("sc", lambda e, ts=ts, nv=nv, pb=pb: e.activation(
                            vtm[:, ts, nv * 512:(nv + 1) * 512],
                            pb[:, 0:512], AF.Copy),
                            reads=[pk], writes=["vtm"])
            else:
                for m in range(NCH):
                    pb, pk = g.PS[m % 2], f"ps{m % 2}"
                    for k in range(NCH):
                        p.op("te", lambda e, m=m, k=k, pb=pb: e.matmul(
                            pb[:, 0:512], wg[:, k, m * 128:(m + 1) * 128],
                            xi[:, k, :], start=(k == 0), stop=(k == NCH - 1)),
                            reads=[f"xiq{k}", "wg"], writes=[pk])
                    p.op("sc", lambda e, m=m, pb=pb: e.activation(
                        gsb[:, m, :], pb[:, 0:512], AF.Silu),
                        reads=[pk], writes=["gxb"])
        p.op("gp", lambda e: e.tensor_tensor(
            rkb[:, 0:NH, :].rearrange("p a b -> p (a b)"),
            flat(rb), flat(kbuf), ALU.mult),
            reads=[f"rb{i}" for i in range(NH)]
                  + [f"kbuf{i}" for i in range(NH)], writes=["sqr"])

        def scan_factors(h):
            hp = h % 2
            en, ek = enb2[hp], f"enb{hp}"
            eck = f"encar{hp}"
            fcall, fck, frk = fcall2[hp], f"fcall{hp}", f"fcraw{hp}"
            if tau == 0:
                p.op("ve", lambda e, en=en: e.memset(en[:, 0:1], 0.0),
                     reads=[ek], writes=[ek])
            else:
                p.op("ve", lambda e, h=h, en=en: e.tensor_copy(
                    en[:, 0:1], encar[:, h:h + 1]),
                    reads=[eck, ek], writes=[ek])
            p.op("ve", lambda e, h=h, tau=tau, en=en: e.tensor_tensor_scan(
                en[:, 1:TT + 1], posf[:, h, :], g.zeros[:],
                (encar[:, h:h + 1] if tau > 0 else 0.0), ALU.add, ALU.add),
                reads=[f"posf{h}", "zeros", eck, ek], writes=[ek], tail=True)
            p.op("ve", lambda e, h=h, en=en: e.tensor_copy(
                encar[:, h:h + 1], en[:, TT:TT + 1]),
                reads=[ek], writes=[eck])
            p.op("ve", lambda e, en=en: e.tensor_tensor(
                xnb[:], en[:, 0:512],
                en[:, 0:512:64].broadcast_to((128, 8, 64)), ALU.subtract),
                reads=[ek], writes=["xnb"])
            p.op("ve", lambda e, en=en, fcall=fcall: e.tensor_tensor(
                fcall[:, 0:8], en[:, 32:513:64], en[:, 0:512:64],
                ALU.subtract), reads=[ek], writes=[frk])
            p.op("ve", lambda e, en=en, fcall=fcall: e.tensor_tensor(
                fcall[:, 8:16], en[:, 32:513:64], en[:, 64:513:64],
                ALU.subtract), reads=[ek], writes=[frk])
            p.op("ve", lambda e, en=en, fcall=fcall: e.tensor_tensor(
                fcall[:, 16:24], en[:, 0:512:64], en[:, 64:513:64],
                ALU.subtract), reads=[ek], writes=[frk])
            p.op("sc", lambda e, fcall=fcall: e.activation(
                fcall[:, 0:24], fcall[:, 0:24], AF.Exp),
                reads=[frk], writes=[fck])
            p.op("sc", lambda e, h=h: e.activation(
                posf[:, h, :], xnb[:], AF.Exp, scale=-1.0),
                reads=["xnb"], writes=[f"posf{h}"])
            p.op("ve", lambda e, h=h: e.tensor_tensor(
                rb[:, h, :], rb[:, h, :], posf[:, h, :], ALU.mult),
                reads=[f"rb{h}", f"posf{h}"], writes=[f"rb{h}"])
            p.op("ve", lambda e, h=h, fcall=fcall: e.tensor_tensor(
                rab2, rb[:, h, :],
                fcall[:, 0:8].broadcast_to((128, 8, 64)), ALU.mult),
                reads=[f"rb{h}", fck], writes=["xpr"])
            p.op("ve", lambda e, en=en: e.tensor_tensor(
                xnb[:], en[:, 1:513],
                en[:, 32:513:64].broadcast_to((128, 8, 64)), ALU.subtract),
                reads=[ek, "xnb"], writes=["xnb"])
            p.op("sc", lambda e, h=h: e.activation(
                posf[:, h, :], xnb[:], AF.Exp),
                reads=["xnb", f"posf{h}"], writes=[f"posf{h}"])
            p.op("ve", lambda e, h=h: e.tensor_tensor(
                kbuf[:, h, :], kbuf[:, h, :], posf[:, h, :], ALU.mult),
                reads=[f"kbuf{h}", f"posf{h}"], writes=[f"kbuf{h}"])
            p.op("ve", lambda e, h=h, fcall=fcall: e.tensor_tensor(
                ksb2, kbuf[:, h, :],
                fcall[:, 8:16].broadcast_to((128, 8, 64)), ALU.mult),
                reads=[f"kbuf{h}", fck], writes=["xpk"])

        def scan_pre(h):
            for c in range(NCT):
                t0 = c * C
                pb_ = 64 * (c % 2)
                q = c // 2
                p.op("te", lambda e, t0=t0, pb_=pb_, q=q: e.transpose(
                    g.PS_KT[pb_:pb_ + 64, 128 * q:128 * q + 128],
                    ksb2[:, t0:t0 + C], id128b[:]),
                    reads=["xpk", "id128b"], writes=[f"pskt{c % 2}"])
            for pp in range(2):
                pb_ = 64 * pp
                p.op("sc", lambda e, pb_=pb_: e.activation(
                    ktbq[pb_:pb_ + 64, :], g.PS_KT[pb_:pb_ + 64, :],
                    AF.Copy),
                    reads=[f"pskt{pp}"], writes=["tmpb"])
            for c in range(NCT):
                t0 = c * C
                pb_ = 64 * (c % 2)
                q = c // 2
                bank, bk = g.PS[2 + c % 2], f"ps{2 + c % 2}"
                p.op("te", lambda e, h=h, t0=t0, pb_=pb_, q=q, bank=bank:
                     e.matmul(bank[pb_:pb_ + 64, 64 * q:64 * q + 64],
                              kbuf[:, h, t0:t0 + C], rab2[:, t0:t0 + C],
                              start=True, stop=True),
                     reads=[f"kbuf{h}", "xpr"], writes=[bk])
                p.op("te", lambda e, h=h, t0=t0, pb_=pb_, q=q, bank=bank:
                     e.matmul(bank[pb_:pb_ + 64, 384 + q:385 + q],
                              rkb[:, h, t0:t0 + C], ubon[:, h:h + 1],
                              start=True, stop=True),
                     reads=["sqr", "ubon"], writes=[bk])
            for pp in range(2):
                pb_ = 64 * pp
                bank, bk = g.PS[2 + pp], f"ps{2 + pp}"
                p.op("ve", lambda e, pb_=pb_, bank=bank: e.tensor_copy(
                    dcolb[pb_:pb_ + 64, 0:4], bank[pb_:pb_ + 64, 384:388]),
                    reads=[bk], writes=[f"dcolb{pp}"])
                for q in range(4):
                    p.op("ve", lambda e, pb_=pb_, q=q: e.tensor_scalar(
                        dtb[pb_:pb_ + 64, 64 * q:64 * q + 64],
                        id64b[pb_:pb_ + 64, :], dcolb[pb_:pb_ + 64, q:q + 1],
                        None, ALU.mult),
                        reads=["id64b"], sreads=[f"dcolb{pp}"],
                        writes=["u2"])
                p.op("ve", lambda e, pb_=pb_, bank=bank: e.select(
                    apb[pb_:pb_ + 64, 0:256],
                    msku[pb_:pb_ + 64, :][:, None, :].broadcast_to(
                        (64, 4, 64)),
                    bank[pb_:pb_ + 64, 0:256], dtb[pb_:pb_ + 64, 0:256]),
                    reads=[bk, "msku", "u2"], writes=["u2"],
                    eager=True)

        def scan_state(h):
            fcall, fck = fcall2[h % 2], f"fcall{h % 2}"

            def m_mm(c):
                pb_ = 64 * (c % 2)
                q = c // 2
                sl_ = 256 * ((c // 2) % 2)
                mbank, mk = g.PS[4 + c % 2], f"ps{4 + c % 2}"
                p.op("te", lambda e, h=h, q=q, pb_=pb_, sl_=sl_,
                     mbank=mbank: e.matmul(
                         mbank[:, sl_:sl_ + 256],
                         ktbq[pb_:pb_ + 64, 128 * q:128 * q + 128],
                         vtm[pb_:pb_ + 64, q, h * 256:(h + 1) * 256],
                         start=True, stop=True),
                     reads=["tmpb", "vtm"], writes=[mk])

            for c in range(4):
                m_mm(c)
            for c in range(NCT):
                t0 = c * C
                pb_ = 64 * (c % 2)
                q = c // 2
                st = c // 2
                sl_ = 256 * ((c // 2) % 2)
                bank, bk = g.PS[2 + c % 2], f"ps{2 + c % 2}"
                mbank, mk = g.PS[4 + c % 2], f"ps{4 + c % 2}"
                sprev_key = f"scar{h}" if c == 0 else f"spp{(c - 1) % 2}"
                sprev = (scar[:, h, :] if c == 0 else spp[(c - 1) % 2][:])
                sout_key = f"scar{h}" if c == NCT - 1 else f"spp{c % 2}"
                sout = (scar[:, h, :] if c == NCT - 1 else spp[c % 2][:])
                p.op("ve", lambda e, sprev=sprev, sout=sout, c=c, sl_=sl_,
                     mbank=mbank, fcall=fcall:
                     e.scalar_tensor_tensor(sout, sprev,
                                            fcall[:, 16 + c:17 + c],
                                            mbank[:, sl_:sl_ + 256],
                                            ALU.mult, ALU.add),
                     reads=[sprev_key, mk], sreads=[fck],
                     writes=[sout_key])
                if c + 4 < NCT:
                    m_mm(c + 4)
                for dvc in range(2):
                    p.op("te", lambda e, h=h, st=st, pb_=pb_, q=q, dvc=dvc,
                         bank=bank: e.matmul(
                             bank[:, 256 + dvc * 64:256 + dvc * 64 + 64],
                             vtm[pb_:pb_ + 64, st,
                                 h * 256 + dvc * 128:h * 256 + (dvc + 1) * 128],
                             apb[pb_:pb_ + 64, 64 * q:64 * q + 64],
                             start=True, stop=False),
                         reads=["vtm", "u2"], writes=[bk])
                    p.op("te", lambda e, sprev=sprev, dvc=dvc, h=h, t0=t0,
                         bank=bank: e.matmul(
                             bank[:, 256 + dvc * 64:256 + dvc * 64 + 64],
                             sprev[:, dvc * 128:(dvc + 1) * 128],
                             rb[:, h, t0:t0 + C], start=False, stop=True),
                         reads=[sprev_key, f"rb{h}"], writes=[bk])
                p.op("sc", lambda e, h=h, t0=t0, bank=bank: e.activation(
                    ob[:, 2 * h:2 * h + 2, t0:t0 + C],
                    bank[:, 256:384], AF.Copy),
                    reads=[bk], writes=[f"dob{2 * h}", f"dob{2 * h + 1}"])

        scan_factors(0)
        for h in range(NH):
            scan_pre(h)
            if h + 1 < NH:
                scan_factors(h + 1)
            scan_state(h)

        p.op("sc", lambda e: e.activation(flat(sqr), flat(ob), AF.Square),
             reads=[f"dob{i}" for i in range(NCH)], writes=["sqr"])
        def gn_banks(h):
            pbS, pkS = ((g.PS[5], "ps5") if h % 2 == 0 else (g.PS[2], "ps2"))
            pbQ, pkQ = ((g.PS[6], "ps6") if h % 2 == 0 else (g.PS[3], "ps3"))
            return pbS, pkS, pbQ, pkQ

        def gn_stats(h):
            pbS, pkS, pbQ, pkQ = gn_banks(h)
            for (src_t, key, pb, pk) in ((ob, "dob", pbS, pkS),
                                         (sqr, "sqr", pbQ, pkQ)):
                for cc in range(2):
                    p.op("te", lambda e, h=h, cc=cc, src_t=src_t, pb=pb,
                         key=key, pk=pk: e.matmul(
                             pb[0:1, 0:512], g.ones_b[:],
                             src_t[:, 2 * h + cc, :],
                             start=(cc == 0), stop=(cc == 1)),
                         reads=[(f"dob{2 * h + cc}" if key == "dob"
                                 else key), "ones_b"], writes=[pk])

        gn_stats(0)
        for h in range(NH):
            pbS, pkS, pbQ, pkQ = gn_banks(h)
            if h + 1 < NH:
                gn_stats(h + 1)
            rw = g.rowwork
            p.op("sc", lambda e, pbS=pbS: e.activation(
                rw[0:1, 0:512], pbS[0:1, 0:512], AF.Square,
                scale=1.0 / DV), reads=[pkS], writes=["rw0"])
            p.op("ve", lambda e, pbQ=pbQ: e.tensor_scalar(
                rw[0:1, 512:1024], pbQ[0:1, 0:512],
                1.0 / DV, EPS, ALU.mult, ALU.add),
                reads=[pkQ], writes=["rw1"])
            p.op("ve", lambda e: e.scalar_tensor_tensor(
                rw[0:1, 1024:1536], rw[0:1, 0:512], -1.0, rw[0:1, 512:1024],
                ALU.mult, ALU.add), reads=["rw0", "rw1"], writes=["rw2"])
            p.op("sc", lambda e: e.activation(rw[0:1, 0:512],
                                              rw[0:1, 1024:1536], AF.Ln),
                 reads=["rw2"], writes=["rw0"])
            p.op("sc", lambda e: e.activation(g.row_s[:], rw[0:1, 0:512],
                                              AF.Exp, scale=-0.5),
                 reads=["rw0"], writes=["row_s"])
            p.op("ve", lambda e, pbS=pbS: e.scalar_tensor_tensor(
                g.row_ms[:], g.row_s[:], 1.0 / DV, pbS[0:1, 0:512],
                ALU.mult, ALU.mult),
                reads=["row_s", pkS], writes=["row_ms"])
            p.op("ve", lambda e: e.tensor_copy(grow[0:1, :], g.row_ms[:]),
                 reads=["row_ms"], writes=["grow"])
            p.op("te", lambda e: e.matmul(
                g.PS[0][:, 0:512], g.onesf[:], g.row_s[:],
                start=True, stop=True),
                reads=["onesf", "row_s"], writes=["ps0"])
            for dvc in range(2):
                ch = 2 * h + dvc
                pbB, pkB = ((g.PS[1], "ps1") if dvc == 0 else
                            (g.PS[4], "ps4"))
                t_b, tk = ((tmpb[:], "tmpb") if dvc == 0 else
                           (posf[:, 2, :], "posf2"))
                x_b, xk = ((xnb[:], "xnb") if dvc == 0 else
                           (posf[:, 3, :], "posf3"))
                p.op("te", lambda e, ch=ch, pbB=pbB: e.matmul(
                    pbB[:, 0:512], gnwr[0:2, ch * 128:(ch + 1) * 128],
                    grow[:], start=True, stop=True),
                    reads=["gnwr", "grow", "grow_c"], writes=[pkB])
                p.op("ve", lambda e, ch=ch, t_b=t_b: e.tensor_tensor(
                    t_b, ob[:, ch, :], g.PS[0][:, 0:512], ALU.mult),
                    reads=[f"dob{ch}", "ps0"], writes=[tk])
                p.op("ve", lambda e, ch=ch, pbB=pbB, t_b=t_b, x_b=x_b:
                     e.scalar_tensor_tensor(
                         x_b, t_b, gncol[:, ch:ch + 1], pbB[:, 0:512],
                         ALU.mult, ALU.subtract),
                    reads=[tk, pkB], sreads=["gncol"], writes=[xk])
                p.op("gp", lambda e, ch=ch, x_b=x_b: e.tensor_tensor(
                    ob[:, ch, :], x_b, gsb[:, ch, :], ALU.mult),
                    reads=[xk, "gxb"], writes=[f"dob{ch}"])
        for m in range(NCH):
            pb, pk = g.PS[m % 2], f"ps{m % 2}"
            for k in range(NCH):
                p.op("te", lambda e, m=m, k=k, pb=pb: e.matmul(
                    pb[:, 0:512], wo[:, k, m * 128:(m + 1) * 128],
                    ob[:, k, :], start=(k == 0), stop=(k == NCH - 1)),
                    reads=[f"dob{k}", "wo"], writes=[pk])
            p.op("ve", lambda e, m=m, pb=pb, xcur=xcur: e.tensor_tensor(
                xcur[:, m, :], pb[:, 0:512], xcur[:, m, :], ALU.add),
                reads=[pk, kx], writes=[kx])
        p.op("sy", lambda e, b=b, sl=sl, xcur=xcur: e.dma_start(
            x2_o[b].rearrange("(c p) t -> p c t", p=128)[:, :, sl],
            xcur[:]),
            reads=[kx], writes=["x2d"], dma=True)
        ep_u1, ep_u2 = posf[:, 0, :], posf[:, 1, :]
        ep_sb, ep_msb = posf[:, 2, :], posf[:, 3, :]
        p.op("sc", lambda e, xcur=xcur: e.activation(flat(gxb), flat(xcur),
                                                     AF.Copy),
             reads=[kx], writes=["gxb"])
        p.op("sc", lambda e, xcur=xcur: e.activation(flat(sqr), flat(xcur),
                                                     AF.Square),
             reads=[kx], writes=["sqr"])
        ln_stats_mm(p, g, "gxb", lambda k: gxb[:, k, :], "sqr",
                    lambda k: sqr[:, k, :])
        ln_rows(p, g, True, rows3_o, tt)
        p.op("te", lambda e: e.matmul(g.PS[5][:, 0:512], g.onesf[:],
                                      g.row_s[:], start=True, stop=True),
             reads=["row_s", "onesf"], writes=["ps5"])
        p.op("te", lambda e: e.matmul(g.PS[6][:, 0:512], g.onesf[:],
                                      g.row_ms[:], start=True, stop=True),
             reads=["row_ms", "onesf"], writes=["ps6"])
        p.op("sc", lambda e, ep_sb=ep_sb: e.activation(
            ep_sb, g.PS[5][:, 0:512], AF.Copy),
            reads=["ps5"], writes=["posf2"])
        p.op("sc", lambda e, ep_msb=ep_msb: e.activation(
            ep_msb, g.PS[6][:, 0:512], AF.Copy),
            reads=["ps6"], writes=["posf3"])
        for k in range(NCH):
            p.op("ve", lambda e, k=k, xcur=xcur, ep_u1=ep_u1, ep_sb=ep_sb:
                 e.tensor_tensor(ep_u1, xcur[:, k, :], ep_sb, ALU.mult),
                 reads=[kx, "posf2"], writes=["posf0"])
            p.op("ve", lambda e, ep_u1=ep_u1, ep_u2=ep_u2, ep_msb=ep_msb:
                 e.tensor_tensor(ep_u2, ep_u1, ep_msb, ALU.subtract),
                 reads=["posf0", "posf3"], writes=["posf1"])
            p.op("sc", lambda e, k=k, ep_u2=ep_u2: e.activation(
                xnb[:], ep_u2, AF.Identity, scale=gcol[:, k:k + 1],
                bias=bcol[:, k:k + 1]),
                reads=["posf1", "gbcols"], writes=["xnb"])
            p.op("ve", lambda e, tt=tt, k=k: e.reduce_max(
                mxc[:, tt * NCH + k:tt * NCH + k + 1], xnb[:], axis=AX.X,
                apply_absolute_value=True),
                reads=["xnb"], writes=["mxc"], tail=True)

    reduce_scalar_max(p, g, mxc[:], "mxc", id128f, lmax2_o)
    p.finish()
    p.emit()
    return nc


_NC_CACHE = {}


def _get_nc(which):
    if which not in _NC_CACHE:
        _NC_CACHE[which] = {"L1": build_L1, "L2": build_L2, "L3": build_L3}[which]()
    return _NC_CACHE[which]


def _pc(v, nch=NCH):
    return np.ascontiguousarray(np.asarray(v, np.float32).reshape(nch, 128).T)


def kernel(L0_attn_hidden_states, L0_attn_mu0, L0_attn_W_x0, L0_attn_W_x2,
           L0_attn_x_bias, L0_attn_W_r, L0_attn_W_w1, L0_attn_W_w2,
           L0_attn_b_w2, L0_attn_W_k, L0_attn_W_v, L0_attn_W_g,
           L0_attn_bonus, L0_attn_gn_weight, L0_attn_gn_bias, L0_attn_W_o,
           L0_ffn_weight, L0_ffn_gamma, L0_ffn_beta):
    f32 = np.float32
    x = np.asarray(L0_attn_hidden_states, f32)
    xfm = np.ascontiguousarray(x.transpose(0, 2, 1))
    cores = list(range(NCORES))

    identf = np.eye(128, dtype=f32)
    ident64d = np.ascontiguousarray(
        np.concatenate([np.eye(64, dtype=f32)] * 2, axis=0))
    jj = np.arange(128)[:, None] % 64
    ii = np.arange(64)[None, :]
    masku = (jj < ii).astype(np.uint8)

    wffnT = np.ascontiguousarray(np.asarray(L0_ffn_weight, f32).T)
    gammapc = _pc(L0_ffn_gamma)
    betapc = _pc(L0_ffn_beta)

    nc1 = _get_nc("L1")
    in1 = []
    for c in cores:
        in1.append(dict(
            x0=np.ascontiguousarray(xfm[2 * c:2 * c + 2]),
            wffnT=wffnT, gammapc=gammapc, betapc=betapc, identf=identf))
    r1 = run_bass_kernel_spmd(nc1, in1, core_ids=cores).results
    lmax1_all = np.ascontiguousarray(
        np.stack([r1[c]["lmax"][0] for c in cores], axis=1))

    wx2 = np.asarray(L0_attn_W_x2, f32).reshape(D, 5, 32)
    wx2T = np.ascontiguousarray(wx2.transpose(1, 2, 0).reshape(160, D))
    xbias = np.asarray(L0_attn_x_bias, f32)
    xbiaspc = np.ascontiguousarray(
        xbias.reshape(5, NCH, 128).transpose(2, 0, 1))
    nc2 = _get_nc("L2")
    in2 = []
    for c in cores:
        in2.append(dict(
            x0=in1[c]["x0"],
            tern=r1[c]["tern"], alpha=r1[c]["alpha"], rows1=r1[c]["rows1"],
            lmax_all=lmax1_all,
            gammapc=gammapc, betapc=betapc, identf=identf,
            ident64d=ident64d, masku=masku,
            wx0T=np.ascontiguousarray(np.asarray(L0_attn_W_x0, f32).T),
            wx2T=wx2T,
            wrT=np.ascontiguousarray(np.asarray(L0_attn_W_r, f32).T),
            ww1T=np.ascontiguousarray(np.asarray(L0_attn_W_w1, f32).T),
            ww2T=np.ascontiguousarray(np.asarray(L0_attn_W_w2, f32).T),
            wkT=np.ascontiguousarray(np.asarray(L0_attn_W_k, f32).T),
            wvT=np.ascontiguousarray(np.asarray(L0_attn_W_v, f32).T),
            wgT=np.ascontiguousarray(np.asarray(L0_attn_W_g, f32).T),
            woT=np.ascontiguousarray(np.asarray(L0_attn_W_o, f32).T),
            mu0pc=_pc(L0_attn_mu0),
            xbiaspc=xbiaspc,
            bw2pc=_pc(L0_attn_b_w2, 4),
            bonusT=np.ascontiguousarray(np.asarray(L0_attn_bonus, f32).T),
            gnwrow=np.ascontiguousarray(np.stack(
                [np.asarray(L0_attn_gn_weight, f32),
                 np.asarray(L0_attn_gn_bias, f32)])),
            gncolw=_pc(L0_attn_gn_weight),
        ))
    r2 = run_bass_kernel_spmd(nc2, in2, core_ids=cores).results
    lmax2_all = np.ascontiguousarray(
        np.stack([r2[c]["lmax2"][0] for c in cores], axis=1))

    nc3 = _get_nc("L3")
    in3 = []
    for c in cores:
        in3.append(dict(
            x2=r2[c]["x2"], tern=r1[c]["tern"], alpha=r1[c]["alpha"],
            rows3=r2[c]["rows3"], lmax_all=lmax2_all,
            gammapc=gammapc, betapc=betapc))
    r3 = run_bass_kernel_spmd(nc3, in3, core_ids=cores).results

    out_fm = np.concatenate([r3[c]["out"] for c in cores], axis=0)
    return np.ascontiguousarray(out_fm.transpose(0, 2, 1))


# version 13
# speedup vs baseline: 1.0265x; 1.0070x over previous
import numpy as np
import ml_dtypes
import concourse.bass as bass
import concourse.mybir as mybir
from concourse.bass_utils import run_bass_kernel_spmd

F32 = mybir.dt.float32
BF16 = mybir.dt.bfloat16
U8 = mybir.dt.uint8
ALU = mybir.AluOpType
AF = mybir.ActivationFunctionType
AX = mybir.AxisListType

NCORES = 8
B_LOC = 2
T = 2048
TT = 512
NTPS = T // TT
NTILE = B_LOC * NTPS
D = 1024
NCH = 8
NH = 4
DK = 128
DV = 256
C = 64
NCT = TT // C
EPS = 1e-5
MAGIC = 12582912.0

ENGS = ("sy", "ve", "sc", "te", "gp")
PARANOID = [False]
AUDIT = [False]


class Prog:

    def __init__(self, nc):
        self.nc = nc
        self.ops = {e: [] for e in ENGS}
        self.nop = {e: 0 for e in ENGS}
        self.last_w = {}
        self.readers = {}
        self.waited = {}
        self.audited = set()

    def op(self, eng, fn, reads=(), writes=(), dma=False, tail=False,
           sreads=(), eager=False):
        idx = self.nop[eng]
        waits = set()

        def dep(peng, pidx, ptail=False, force=False):
            if peng == eng and not (ptail or force):
                return
            if self.waited.get((eng, peng), -1) >= pidx:
                return
            if peng == eng and pidx == idx:
                return
            waits.add((peng, pidx))

        para = PARANOID[0]
        if AUDIT[0]:
            for k in list(reads) + list(sreads):
                if k not in self.last_w and k not in self.audited:
                    self.audited.add(k)
                    print(f"AUDIT read-before-write: {k} at {eng}#{idx}")
        for k in sreads:
            if k in self.last_w:
                dep(*self.last_w[k], force=True)
        for k in reads:
            if k in self.last_w:
                dep(*self.last_w[k], force=eager or para)
        for k in writes:
            if k in self.last_w:
                dep(*self.last_w[k][:2], force=para)
            for (re_, ri) in self.readers.get(k, ()):
                dep(re_, ri, force=para)
        for (peng, pidx) in waits:
            if pidx > self.waited.get((eng, peng), -1):
                self.waited[(eng, peng)] = pidx
        for k in list(reads) + list(sreads):
            self.readers.setdefault(k, []).append((eng, idx))
        for k in writes:
            self.last_w[k] = (eng, idx, tail)
            self.readers[k] = []
        self.ops[eng].append([fn, waits, dma])
        self.nop[eng] += 1

    def emit(self):
        nc = self.nc
        needed = {e: set() for e in ENGS}
        for e in ENGS:
            for rec in self.ops[e]:
                for (pe, pi) in rec[1]:
                    needed[pe].add(pi)
        sems = {e: nc.alloc_semaphore("sem_" + e) for e in ENGS}
        NPOOL = 12
        pools = {e: [nc.alloc_semaphore(f"dp_{e}{i}") for i in range(NPOOL)]
                 for e in ("sy", "gp")}
        pool_val = {e: [0] * NPOOL for e in ("sy", "gp")}
        pool_i = {e: 0 for e in ("sy", "gp")}
        mile = {}
        cnt = {e: 0 for e in ENGS}
        prewait = {}
        for e in ENGS:
            for i, rec in enumerate(self.ops[e]):
                fn, waits, dma = rec
                if fn is None:
                    continue
                if dma:
                    j = pool_i[e] % NPOOL
                    pool_i[e] += 1
                    s = pools[e][j]
                    if pool_val[e][j] > 0:
                        prewait[(e, i)] = (s, pool_val[e][j])
                    pool_val[e][j] += 16
                    mile[(e, i)] = (s, pool_val[e][j])
                elif i in needed[e]:
                    cnt[e] += 1
                    mile[(e, i)] = (sems[e], cnt[e])
                else:
                    mile[(e, i)] = None

        with nc.Block() as blk:
            def body_for(e):
                def body(eng):
                    for i, rec in enumerate(self.ops[e]):
                        fn, waits, dma = rec
                        for (pe, pi) in sorted(waits):
                            ms = mile[(pe, pi)]
                            eng.wait_ge(ms[0], ms[1])
                        if fn is None:
                            continue
                        if (e, i) in prewait:
                            pw = prewait[(e, i)]
                            eng.wait_ge(pw[0], pw[1])
                        inst = fn(eng)
                        ms = mile[(e, i)]
                        if ms is not None:
                            inst.then_inc(ms[0], 16 if dma else 1)
                return body
            blk.sync(body_for("sy"))
            blk.vector(body_for("ve"))
            blk.scalar(body_for("sc"))
            blk.tensor(body_for("te"))
            blk.gpsimd(body_for("gp"))
        return sems

    def finish(self):
        waits = set()
        for pe in ENGS:
            if self.nop[pe] == 0:
                continue
            if pe != "sy":
                waits.add((pe, self.nop[pe] - 1))
        for e in ("sy", "gp"):
            for i in range(len(self.ops[e]) - 1, -1, -1):
                if self.ops[e][i][2]:
                    waits.add((e, i))
                    break
        self.ops["sy"].append([None, waits, False])


def flat(t):
    ap = t[:]
    if len(ap.shape) == 3:
        return ap.rearrange("p a b -> p (a b)")
    return ap


class G:
    pass


def alloc_generic(nc, g):
    g.PS = [nc.alloc_psum_tensor(f"ps{i}", [128, 512], F32) for i in range(7)]
    g.PS_KT = nc.alloc_psum_tensor("pskt", [128, 512], BF16)
    g.ones_b = nc.alloc_sbuf_tensor("ones_b", [128, 1], BF16)
    g.onesf = nc.alloc_sbuf_tensor("onesf", [1, 128], F32)
    g.zeros = nc.alloc_sbuf_tensor("zeros", [128, 512], BF16)
    g.row_s = nc.alloc_sbuf_tensor("row_s", [1, 512], F32)
    g.row_ms = nc.alloc_sbuf_tensor("row_ms", [1, 512], F32)
    g.rowb1 = nc.alloc_sbuf_tensor("rowb1", [1, 512], BF16)
    g.rowwork = nc.alloc_sbuf_tensor("rowwork", [1, 1536], F32)
    g.scal = nc.alloc_sbuf_tensor("scal", [1, 16], F32)
    g.cols = nc.alloc_sbuf_tensor("colsb", [128, 96], F32)
    g.sbc = nc.alloc_sbuf_tensor("sbc", [128, 512], F32)
    g.msbc = nc.alloc_sbuf_tensor("msbc", [128, 512], F32)
    g.u1 = nc.alloc_sbuf_tensor("u1", [128, 512], F32)
    g.u2 = nc.alloc_sbuf_tensor("u2", [128, 512], F32)


def init_consts(p, g):
    p.op("ve", lambda e: e.memset(g.ones_b[:], 1.0), writes=["ones_b"])
    p.op("ve", lambda e: e.memset(g.onesf[:], 1.0), writes=["onesf"])
    p.op("ve", lambda e: e.memset(g.zeros[:], 0.0), writes=["zeros"])


def ln_stats_mm(p, g, xb_key, xb_ap, sq_key, sq_ap):
    for k in range(NCH):
        p.op("te", lambda e, k=k: e.matmul(
            g.PS[4][0:1, :], g.ones_b[:], xb_ap(k),
            start=(k == 0), stop=(k == NCH - 1)),
            reads=[xb_key, "ones_b"], writes=["ps4"])
    for k in range(NCH):
        p.op("te", lambda e, k=k: e.matmul(
            g.PS[5][0:1, :], g.ones_b[:], sq_ap(k),
            start=(k == 0), stop=(k == NCH - 1)),
            reads=[sq_key, "ones_b"], writes=["ps5"])


def ln_rows(p, g, composite, rows_dram, tt):
    a = (1.0 + EPS) if composite else 1.0
    e2 = (EPS * EPS) if composite else EPS
    rw = g.rowwork
    p.op("sc", lambda e: e.activation(rw[0:1, 0:512], g.PS[4][0:1, :],
                                      AF.Square, scale=1.0 / D),
         reads=["ps4"], writes=["rw0"])
    p.op("ve", lambda e: e.tensor_scalar(rw[0:1, 512:1024], g.PS[5][0:1, :],
                                         a / D, e2, ALU.mult, ALU.add),
         reads=["ps5"], writes=["rw1"])
    p.op("ve", lambda e: e.scalar_tensor_tensor(
        rw[0:1, 1024:1536], rw[0:1, 0:512], -a, rw[0:1, 512:1024],
        ALU.mult, ALU.add), reads=["rw0", "rw1"], writes=["rw2"])
    p.op("sc", lambda e: e.activation(rw[0:1, 0:512], rw[0:1, 1024:1536],
                                      AF.Ln),
         reads=["rw2"], writes=["rw0"])
    p.op("sc", lambda e: e.activation(g.row_s[:], rw[0:1, 0:512], AF.Exp,
                                      scale=-0.5),
         reads=["rw0"], writes=["row_s"])
    p.op("ve", lambda e: e.scalar_tensor_tensor(
        g.row_ms[:], g.row_s[:], 1.0 / D, g.PS[4][0:1, :],
        ALU.mult, ALU.mult),
        reads=["row_s", "ps4"], writes=["row_ms"])
    if rows_dram is not None:
        sl = slice(tt * TT, (tt + 1) * TT)
        p.op("sy", lambda e, sl=sl: e.dma_start(rows_dram[0:1, sl],
                                                g.row_s[:]),
             reads=["row_s"], writes=["rows_dram_st"], dma=True)
        p.op("sy", lambda e, sl=sl: e.dma_start(rows_dram[1:2, sl],
                                                g.row_ms[:]),
             reads=["row_ms"], writes=["rows_dram_st"], dma=True)


def bcast_rows(p, g):
    p.op("te", lambda e: e.matmul(g.PS[5][:, 0:512], g.onesf[:], g.row_s[:],
                                  start=True, stop=True),
         reads=["row_s", "onesf"], writes=["ps5"])
    p.op("te", lambda e: e.matmul(g.PS[6][:, 0:512], g.onesf[:], g.row_ms[:],
                                  start=True, stop=True),
         reads=["row_ms", "onesf"], writes=["ps6"])
    p.op("sc", lambda e: e.activation(g.sbc[:], g.PS[5][:, 0:512], AF.Copy),
         reads=["ps5"], writes=["sbc"])
    p.op("sc", lambda e: e.activation(g.msbc[:], g.PS[6][:, 0:512], AF.Copy),
         reads=["ps6"], writes=["msbc"])


def reduce_scalar_max(p, g, mx_ap, mx_key, id128f, out_dram):
    p.op("ve", lambda e: e.memset(g.cols[:], 0.0), writes=["colsb"])
    p.op("ve", lambda e: e.reduce_max(g.cols[:, 0:1], mx_ap, axis=AX.X),
         reads=[mx_key], writes=["colsb"], tail=True)
    p.op("te", lambda e: e.transpose(g.PS[0][0:96, 0:128], g.cols[:, 0:96],
                                     id128f[:]),
         reads=["colsb", "id128f"], writes=["ps0"])
    p.op("ve", lambda e: e.reduce_max(g.scal[0:1, 0:1], g.PS[0][0:1, 0:128],
                                      axis=AX.X),
         reads=["ps0"], writes=["gmax"], tail=True)
    p.op("sy", lambda e: e.dma_start(out_dram, g.scal[0:1, 0:1]),
         reads=["gmax"], writes=["lmax_st"], dma=True)


def stats_tile(p, g, xt, xb, sqb, kx, kb, composite, rows_dram, tt,
               ksq="sqr"):
    kxl = [kx] if isinstance(kx, str) else list(kx)
    p.op("sc", lambda e: e.activation(flat(sqb), flat(xt), AF.Square),
         reads=kxl, writes=[ksq])
    ln_stats_mm(p, g, kb, lambda k: xb[:, k, :], ksq,
                lambda k: sqb[:, k, :])
    ln_rows(p, g, composite, rows_dram, tt)


def xn_max_tile(p, g, xt, kx, gcol, bcol, xnb, mxc, tt):
    bcast_rows(p, g)
    for k in range(NCH):
        p.op("ve", lambda e, k=k: e.tensor_tensor(
            g.u1[:], xt[:, k, :], g.sbc[:], ALU.mult),
            reads=[kx, "sbc"], writes=["u1"])
        p.op("ve", lambda e: e.tensor_tensor(g.u2[:], g.u1[:], g.msbc[:],
                                             ALU.subtract),
             reads=["u1", "msbc"], writes=["u2"])
        p.op("sc", lambda e, k=k: e.activation(
            xnb[:], g.u2[:], AF.Identity, scale=gcol[:, k:k + 1],
            bias=bcol[:, k:k + 1]),
            reads=["u2", "gbcols"], writes=["xnb"])
        p.op("ve", lambda e, tt=tt, k=k: e.reduce_max(
            mxc[:, tt * NCH + k:tt * NCH + k + 1], xnb[:], axis=AX.X,
            apply_absolute_value=True),
            reads=["xnb"], writes=["mxc"], tail=True)


def emit_stats_pass(p, g, xsrc, rows_dram, gcol, bcol, xt, xb, sqb, xnb, mxc,
                    id128f, lmax_o, composite=True, src_key="x0d",
                    kx="fxt", kb="gxb", ksq="sqr"):
    p.op("ve", lambda e: e.memset(mxc[:], 0.0), writes=["mxc"])
    for tt in range(NTILE):
        b, tau = divmod(tt, NTPS)
        sl = slice(tau * TT, (tau + 1) * TT)
        xv = xsrc[b].rearrange("(c p) t -> p c t", p=128)[:, :, sl]
        p.op("sy", lambda e, xv=xv: e.dma_start(xt[:], xv),
             reads=[src_key], writes=[kx], dma=True)
        p.op("gp", lambda e, xv=xv: e.dma_start(xb[:], xv),
             reads=[src_key], writes=[kb], dma=True)
        stats_tile(p, g, xt, xb, sqb, kx, kb, composite, rows_dram, tt,
                   ksq=ksq)
        xn_max_tile(p, g, xt, kx, gcol, bcol, xnb, mxc, tt)
    reduce_scalar_max(p, g, mxc[:], "mxc", id128f, lmax_o)


def emit_quant_prologue(p, g, lmax_all, alpha_in, lm_sb, al_sb, invsc_col,
                        epi_col):
    p.op("sy", lambda e: e.dma_start(lm_sb[:], lmax_all),
         writes=["lm_sb"], dma=True)
    p.op("sy", lambda e: e.dma_start(al_sb[:], alpha_in),
         writes=["al_sb"], dma=True)
    p.op("ve", lambda e: e.reduce_max(g.scal[0:1, 8:9], lm_sb[:], axis=AX.X),
         reads=["lm_sb"], writes=["q0"], tail=True)
    p.op("ve", lambda e: e.tensor_scalar(g.scal[0:1, 9:10], g.scal[0:1, 8:9],
                                         1e-8, None, ALU.max),
         reads=["q0"], writes=["q1"])
    p.op("ve", lambda e: e.reciprocal(g.scal[0:1, 10:11], g.scal[0:1, 9:10]),
         reads=["q1"], writes=["q2"], tail=True, eager=True)
    p.op("ve", lambda e: e.tensor_scalar(g.scal[0:1, 11:12],
                                         g.scal[0:1, 10:11], 127.0, None,
                                         ALU.mult),
         reads=["q2"], writes=["q3"])
    p.op("ve", lambda e: e.scalar_tensor_tensor(
        g.scal[0:1, 12:13], g.scal[0:1, 9:10], 0.5 / 127.0, al_sb[0:1, 0:1],
        ALU.mult, ALU.mult), reads=["q1", "al_sb"], writes=["q4"])
    p.op("te", lambda e: e.matmul(g.PS[5][:, 0:2], g.onesf[:],
                                  g.scal[0:1, 11:13], start=True, stop=True),
         reads=["q3", "q4", "onesf"], writes=["ps5"])
    p.op("ve", lambda e: e.tensor_copy(invsc_col[:], g.PS[5][:, 0:1]),
         reads=["ps5"], writes=["invsc"])
    p.op("ve", lambda e: e.tensor_copy(epi_col[:], g.PS[5][:, 1:2]),
         reads=["ps5"], writes=["epi"])


def emit_ffnc(p, g, xsrc, xdst, rows_in, tern_sb, gcol, bcol, invsc_col,
              epi_col, xt, qb, with_stats=False, rows2_dram=None,
              xb=None, sqb=None, src_key="x0d", dst_key="x1d",
              kx="fxt", kq="xiq", kb="gxb", ksq="sqr"):
    for tt in range(NTILE):
        b, tau = divmod(tt, NTPS)
        sl = slice(tau * TT, (tau + 1) * TT)
        xv = xsrc[b].rearrange("(c p) t -> p c t", p=128)[:, :, sl]
        for k in range(NCH):
            p.op("sy", lambda e, xv=xv, k=k: e.dma_start(xt[:, k, :],
                                                         xv[:, k, :]),
                 reads=[src_key], writes=[f"{kx}{k}"], dma=True)
        p.op("sy", lambda e, tt=tt: e.dma_start(
            g.row_s[:], rows_in[0:1, tt * TT:(tt + 1) * TT]),
            writes=["row_s"], dma=True)
        p.op("sy", lambda e, tt=tt: e.dma_start(
            g.row_ms[:], rows_in[1:2, tt * TT:(tt + 1) * TT]),
            writes=["row_ms"], dma=True)
        bcast_rows(p, g)
        for k in range(NCH):
            p.op("ve", lambda e, k=k: e.tensor_tensor(
                g.u1[:], xt[:, k, :], g.sbc[:], ALU.mult),
                reads=[f"{kx}{k}", "sbc"], writes=["u1"])
            p.op("ve", lambda e: e.tensor_tensor(g.u2[:], g.u1[:], g.msbc[:],
                                                 ALU.subtract),
                 reads=["u1", "msbc"], writes=["u2"])
            p.op("sc", lambda e, k=k: e.activation(
                g.u1[:], g.u2[:], AF.Identity, scale=gcol[:, k:k + 1],
                bias=bcol[:, k:k + 1]),
                reads=["u2", "gbcols"], writes=["u1"])
            p.op("ve", lambda e: e.tensor_scalar(
                g.u2[:], g.u1[:], invsc_col[:, 0:1], MAGIC, ALU.mult,
                ALU.add),
                reads=["u1"], sreads=["invsc"], writes=["u2"])
            p.op("ve", lambda e, k=k: e.tensor_scalar(
                qb[:, k, :], g.u2[:], MAGIC, None, ALU.subtract),
                reads=["u2"], writes=[f"{kq}{k}"])
        for m in range(NCH):
            pb = g.PS[m % 2]
            pk = f"ps{m % 2}"
            for k in range(NCH):
                p.op("te", lambda e, m=m, k=k, pb=pb: e.matmul(
                    pb[:, 0:512], tern_sb[:, k, m * 128:(m + 1) * 128],
                    qb[:, k, :], start=(k == 0), stop=(k == NCH - 1)),
                    reads=[f"{kq}{k}", "tern_res"], writes=[pk])
            p.op("sc", lambda e, pb=pb: e.activation(
                g.u1[:], pb[:, 0:512], AF.Identity, scale=epi_col[:, 0:1]),
                reads=[pk, "epi"], writes=["u1"])
            p.op("ve", lambda e, m=m: e.tensor_tensor(
                xt[:, m, :], g.u1[:], xt[:, m, :], ALU.add),
                reads=["u1", f"{kx}{m}"], writes=[f"{kx}{m}"])
            p.op("sy", lambda e, b=b, sl=sl, m=m: e.dma_start(
                xdst[b].rearrange("(c p) t -> p c t",
                                  p=128)[:, m, sl], xt[:, m, :]),
                reads=[f"{kx}{m}"], writes=[dst_key], dma=True)
        if with_stats:
            p.op("ve", lambda e: e.tensor_copy(flat(xb), flat(xt)),
                 reads=[f"{kx}{k}" for k in range(NCH)], writes=[kb])
            stats_tile(p, g, xt, xb, sqb,
                       [f"{kx}{k}" for k in range(NCH)], kb, False,
                       rows2_dram, tt, ksq=ksq)


def build_L1():
    nc = bass.Bass("TRN2", target_bir_lowering=False, debug=False)
    g = G()
    x0 = nc.dram_tensor("x0", [B_LOC, D, T], F32, kind="ExternalInput").ap()
    wffnT = nc.dram_tensor("wffnT", [D, D], F32, kind="ExternalInput").ap()
    gpc = nc.dram_tensor("gammapc", [128, NCH], F32, kind="ExternalInput").ap()
    bpc = nc.dram_tensor("betapc", [128, NCH], F32, kind="ExternalInput").ap()
    identf = nc.dram_tensor("identf", [128, 128], F32, kind="ExternalInput").ap()

    tern_o = nc.dram_tensor("tern", [D, D], BF16, kind="ExternalOutput").ap()
    alpha_o = nc.dram_tensor("alpha", [1, 1], F32, kind="ExternalOutput").ap()
    rows_o = nc.dram_tensor("rows1", [2, NTILE * TT], F32,
                            kind="ExternalOutput").ap()
    lmax_o = nc.dram_tensor("lmax", [1, 1], F32, kind="ExternalOutput").ap()

    alloc_generic(nc, g)
    wf = nc.alloc_sbuf_tensor("wf", [128, NCH, D], F32)
    absw = nc.alloc_sbuf_tensor("absw", [128, NCH * D], F32)
    m01 = nc.alloc_sbuf_tensor("m01", [128, NCH * D], BF16)
    sgn = nc.alloc_sbuf_tensor("sgn", [128, NCH * D], BF16)
    tern = nc.alloc_sbuf_tensor("tern_sb", [128, NCH, D], BF16)
    id128f = nc.alloc_sbuf_tensor("id128f", [128, 128], F32)
    onecol = nc.alloc_sbuf_tensor("onecol", [128, 1], F32)
    gcol = nc.alloc_sbuf_tensor("gcol", [128, NCH], F32)
    bcol = nc.alloc_sbuf_tensor("bcol", [128, NCH], F32)
    xt = nc.alloc_sbuf_tensor("sxt", [128, NCH, TT], F32)
    xb = nc.alloc_sbuf_tensor("sxb", [128, NCH, TT], BF16)
    sqb = nc.alloc_sbuf_tensor("sqb", [128, NCH, TT], BF16)
    xnb = nc.alloc_sbuf_tensor("xnb", [128, TT], F32)
    mxc = nc.alloc_sbuf_tensor("mxc", [128, NCH * NTILE], F32)

    p = Prog(nc)
    init_consts(p, g)
    p.op("sy", lambda e: e.dma_start(
        wf[:], wffnT.rearrange("(c p) o -> p c o", p=128)),
        writes=["wf"], dma=True)
    p.op("sy", lambda e: e.dma_start(id128f[:], identf),
         writes=["id128f"], dma=True)
    p.op("gp", lambda e: e.dma_start(gcol[:], gpc), writes=["gbcols"], dma=True)
    p.op("gp", lambda e: e.dma_start(bcol[:], bpc), writes=["gbcols"], dma=True)

    wfv = flat(wf)
    p.op("sc", lambda e: e.activation(absw[:], wfv, AF.Abs),
         reads=["wf"], writes=["absw"])
    p.op("sc", lambda e: e.activation(sgn[:], wfv, AF.Sign),
         reads=["wf"], writes=["sgn"])
    p.op("ve", lambda e: e.memset(g.cols[:], 0.0), writes=["colsb"])
    p.op("ve", lambda e: e.reduce_sum(g.cols[:, 0:1], absw[:], axis=AX.X),
         reads=["absw"], writes=["colsb"], tail=True)
    p.op("te", lambda e: e.transpose(g.PS[0][0:96, 0:128], g.cols[:, 0:96],
                                     id128f[:]),
         reads=["colsb", "id128f"], writes=["ps0"])
    p.op("ve", lambda e: e.reduce_sum(g.scal[0:1, 0:1], g.PS[0][0:1, 0:128],
                                      axis=AX.X), reads=["ps0"], writes=["s0"], tail=True)
    p.op("ve", lambda e: e.tensor_scalar(g.scal[0:1, 1:2], g.scal[0:1, 0:1],
                                         0.7 / (D * D), None, ALU.mult),
         reads=["s0"], writes=["s1"])
    p.op("te", lambda e: e.matmul(g.PS[1][:, 0:1], g.onesf[:],
                                  g.scal[0:1, 1:2], start=True, stop=True),
         reads=["s1", "onesf"], writes=["ps1"])
    p.op("ve", lambda e: e.tensor_copy(onecol[:], g.PS[1][:, 0:1]),
         reads=["ps1"], writes=["onecol"])
    p.op("ve", lambda e: e.tensor_scalar(m01[:], absw[:], onecol[:, 0:1],
                                         None, ALU.is_gt),
         reads=["absw"], sreads=["onecol"], writes=["m01"])
    p.op("ve", lambda e: e.tensor_tensor(absw[:], absw[:], m01[:], ALU.mult),
         reads=["absw", "m01"], writes=["absw"])
    p.op("ve", lambda e: e.reduce_sum(g.cols[:, 32:33], absw[:], axis=AX.X),
         reads=["absw"], writes=["colsb"], tail=True)
    p.op("ve", lambda e: e.reduce_sum(g.cols[:, 64:65], m01[:], axis=AX.X),
         reads=["m01"], writes=["colsb"], tail=True)
    p.op("te", lambda e: e.transpose(g.PS[0][0:96, 0:128], g.cols[:, 0:96],
                                     id128f[:]),
         reads=["colsb", "id128f"], writes=["ps0"])
    p.op("ve", lambda e: e.reduce_sum(g.scal[0:1, 2:3], g.PS[0][32:33, 0:128],
                                      axis=AX.X), reads=["ps0"], writes=["s2"], tail=True)
    p.op("ve", lambda e: e.reduce_sum(g.scal[0:1, 3:4], g.PS[0][64:65, 0:128],
                                      axis=AX.X), reads=["ps0"], writes=["s3"], tail=True)
    p.op("ve", lambda e: e.tensor_scalar(g.scal[0:1, 4:5], g.scal[0:1, 3:4],
                                         1.0, None, ALU.max),
         reads=["s3"], writes=["s4"])
    p.op("ve", lambda e: e.reciprocal(g.scal[0:1, 5:6], g.scal[0:1, 4:5]),
         reads=["s4"], writes=["s5"], tail=True, eager=True)
    p.op("ve", lambda e: e.tensor_tensor(g.scal[0:1, 6:7], g.scal[0:1, 2:3],
                                         g.scal[0:1, 5:6], ALU.mult),
         reads=["s2", "s5"], writes=["s6"])
    p.op("ve", lambda e: e.tensor_tensor(flat(tern), sgn[:], m01[:],
                                         ALU.mult),
         reads=["sgn", "m01"], writes=["tern"])
    p.op("sy", lambda e: e.dma_start(
        tern_o.rearrange("(c p) o -> p c o", p=128), tern[:]),
        reads=["tern"], writes=["tern_st"], dma=True)
    p.op("sy", lambda e: e.dma_start(alpha_o, g.scal[0:1, 6:7]),
         reads=["s6"], writes=["alpha_st"], dma=True)

    emit_stats_pass(p, g, x0, rows_o, gcol, bcol, xt, xb, sqb, xnb, mxc,
                    id128f, lmax_o, composite=True,
                    kx="sxt", kb="sxb", ksq="sqb")
    p.finish()
    p.emit()
    return nc


def build_L3():
    nc = bass.Bass("TRN2", target_bir_lowering=False, debug=False)
    g = G()
    x2 = nc.dram_tensor("x2", [B_LOC, D, T], F32, kind="ExternalInput").ap()
    tern_i = nc.dram_tensor("tern", [D, D], BF16, kind="ExternalInput").ap()
    alpha_i = nc.dram_tensor("alpha", [1, 1], F32, kind="ExternalInput").ap()
    rows_i = nc.dram_tensor("rows3", [2, NTILE * TT], F32,
                            kind="ExternalInput").ap()
    lmax_all = nc.dram_tensor("lmax_all", [1, 8], F32,
                              kind="ExternalInput").ap()
    gpc = nc.dram_tensor("gammapc", [128, NCH], F32, kind="ExternalInput").ap()
    bpc = nc.dram_tensor("betapc", [128, NCH], F32, kind="ExternalInput").ap()
    out = nc.dram_tensor("out", [B_LOC, D, T], F32, kind="ExternalOutput").ap()

    alloc_generic(nc, g)
    tern_sb = nc.alloc_sbuf_tensor("tern_res", [128, NCH, D], BF16)
    gcol = nc.alloc_sbuf_tensor("gcol", [128, NCH], F32)
    bcol = nc.alloc_sbuf_tensor("bcol", [128, NCH], F32)
    lm_sb = nc.alloc_sbuf_tensor("lm_sb", [1, 8], F32)
    al_sb = nc.alloc_sbuf_tensor("al_sb", [1, 1], F32)
    invsc_col = nc.alloc_sbuf_tensor("invsc_col", [128, 1], F32)
    epi_col = nc.alloc_sbuf_tensor("epi_col", [128, 1], F32)
    xt = nc.alloc_sbuf_tensor("fxt", [128, NCH, TT], F32)
    qb = nc.alloc_sbuf_tensor("qb", [128, NCH, TT], BF16)

    p = Prog(nc)
    init_consts(p, g)
    p.op("gp", lambda e: e.dma_start(
        tern_sb[:], tern_i.rearrange("(c p) o -> p c o", p=128)),
        writes=["tern_res"], dma=True)
    p.op("gp", lambda e: e.dma_start(gcol[:], gpc), writes=["gbcols"], dma=True)
    p.op("gp", lambda e: e.dma_start(bcol[:], bpc), writes=["gbcols"], dma=True)
    emit_quant_prologue(p, g, lmax_all, alpha_i, lm_sb, al_sb, invsc_col,
                        epi_col)
    emit_ffnc(p, g, x2, out, rows_i, tern_sb, gcol, bcol, invsc_col, epi_col,
              xt, qb, src_key="x2d", dst_key="outd")
    p.finish()
    p.emit()
    return nc


def build_L2():
    nc = bass.Bass("TRN2", target_bir_lowering=False, debug=False)
    g = G()

    def AP(n, s, d=F32, kk="ExternalInput"):
        return nc.dram_tensor(n, s, d, kind=kk).ap()

    x0 = AP("x0", [B_LOC, D, T])
    tern_i = AP("tern", [D, D], BF16)
    alpha_i = AP("alpha", [1, 1])
    rows1 = AP("rows1", [2, NTILE * TT])
    lmax_all = AP("lmax_all", [1, 8])
    gpc, bpc = AP("gammapc", [128, NCH]), AP("betapc", [128, NCH])
    identf = AP("identf", [128, 128])
    ident64d = AP("ident64d", [128, 64])
    masku = AP("masku", [128, 64], U8)
    wx0T = AP("wx0T", [D, 160])
    wx2T = AP("wx2T", [160, D])
    wrT = AP("wrT", [D, 512])
    ww1T = AP("ww1T", [D, 64])
    ww2T = AP("ww2T", [64, 512])
    wkT = AP("wkT", [D, 512])
    wvT = AP("wvT", [D, D])
    wgT = AP("wgT", [D, D])
    woT = AP("woT", [D, D])
    mu0pc = AP("mu0pc", [128, NCH])
    xbiaspc = AP("xbiaspc", [128, 5, NCH])
    bw2pc = AP("bw2pc", [128, 4])
    bonusT = AP("bonusT", [128, 4])
    gnwrow = AP("gnwrow", [2, D])
    gncolw = AP("gncolw", [128, NCH])
    x2_o = AP("x2", [B_LOC, D, T], F32, "ExternalOutput")
    rows3_o = AP("rows3", [2, NTILE * TT], F32, "ExternalOutput")
    lmax2_o = AP("lmax2", [1, 1], F32, "ExternalOutput")
    x1d = nc.dram_tensor("x1scr", [B_LOC, D, T], F32, kind="Internal").ap()
    rows2d = nc.dram_tensor("rows2scr", [2, NTILE * TT], F32,
                            kind="Internal").ap()

    alloc_generic(nc, g)
    tern_sb = nc.alloc_sbuf_tensor("tern_res", [128, NCH, D], BF16)
    wx0 = nc.alloc_sbuf_tensor("wx0", [128, NCH, 160], BF16)
    wx2 = nc.alloc_sbuf_tensor("wx2", [128, 2, D], BF16)
    wr = nc.alloc_sbuf_tensor("wr", [128, NCH, 512], BF16)
    ww1 = nc.alloc_sbuf_tensor("ww1", [128, NCH, 64], BF16)
    ww2 = nc.alloc_sbuf_tensor("ww2", [64, 512], BF16)
    wk = nc.alloc_sbuf_tensor("wk", [128, NCH, 512], BF16)
    wv = nc.alloc_sbuf_tensor("wv", [128, NCH, D], BF16)
    wg = nc.alloc_sbuf_tensor("wg", [128, NCH, D], BF16)
    wo = nc.alloc_sbuf_tensor("wo", [128, NCH, D], BF16)
    id128f = nc.alloc_sbuf_tensor("id128f", [128, 128], F32)
    id128b = nc.alloc_sbuf_tensor("id128b", [128, 128], BF16)
    id64b = nc.alloc_sbuf_tensor("id64b", [128, 64], BF16)
    msku = nc.alloc_sbuf_tensor("msku", [128, 64], U8)
    gcol = nc.alloc_sbuf_tensor("gcol", [128, NCH], F32)
    bcol = nc.alloc_sbuf_tensor("bcol", [128, NCH], F32)
    mu0c = nc.alloc_sbuf_tensor("mu0c", [128, NCH], F32)
    xbc = nc.alloc_sbuf_tensor("xbc", [128, 5, NCH], F32)
    bw2c = nc.alloc_sbuf_tensor("bw2c", [128, 4], F32)
    ubon = nc.alloc_sbuf_tensor("ubon", [128, 4], BF16)
    gnwr = nc.alloc_sbuf_tensor("gnwr", [2, D], BF16)
    gncol = nc.alloc_sbuf_tensor("gncol", [128, NCH], F32)
    lm_sb = nc.alloc_sbuf_tensor("lm_sb", [1, 8], F32)
    al_sb = nc.alloc_sbuf_tensor("al_sb", [1, 1], F32)
    invsc_col = nc.alloc_sbuf_tensor("invsc_col", [128, 1], F32)
    epi_col = nc.alloc_sbuf_tensor("epi_col", [128, 1], F32)
    xt = nc.alloc_sbuf_tensor("fxt", [128, NCH, TT], F32)
    xiq = nc.alloc_sbuf_tensor("xiq", [128, NCH, TT], BF16)
    gxb = nc.alloc_sbuf_tensor("gxb", [128, NCH, TT], BF16)
    sqr = nc.alloc_sbuf_tensor("sqr", [128, NCH, TT], BF16)
    dob = nc.alloc_sbuf_tensor("dob", [128, NCH, TT], BF16)
    xnb = nc.alloc_sbuf_tensor("xnb", [128, TT], F32)
    mxc = nc.alloc_sbuf_tensor("mxc", [128, NCH * NTILE], F32)
    xln = nc.alloc_sbuf_tensor("xln", [128, NCH, TT + 1], BF16)
    xp = nc.alloc_sbuf_tensor("xp", [128, 2, TT], BF16)
    rb = nc.alloc_sbuf_tensor("rb", [128, NH, TT], BF16)
    kbuf = nc.alloc_sbuf_tensor("kbuf", [128, NH, TT], BF16)
    w1b = nc.alloc_sbuf_tensor("w1b", [64, TT], BF16)
    posf = nc.alloc_sbuf_tensor("posf", [128, NH, TT], F32)
    vtm = nc.alloc_sbuf_tensor("vtm", [128, NTPS, D], BF16)
    tmpb = nc.alloc_sbuf_tensor("tmpb", [128, TT], BF16)
    enb2 = [nc.alloc_sbuf_tensor(f"enb{i}", [128, TT + 1], F32)
            for i in range(2)]
    encar = nc.alloc_sbuf_tensor("encar", [128, NH], F32)
    scar = nc.alloc_sbuf_tensor("scar", [128, NH, DV], BF16)
    spp = [nc.alloc_sbuf_tensor(f"spp{i}", [128, DV], BF16) for i in range(2)]
    dcolb = nc.alloc_sbuf_tensor("dcolb", [128, 4], F32)
    fcall2 = [nc.alloc_sbuf_tensor(f"fcall{i}", [128, 24], F32)
              for i in range(2)]
    grow = nc.alloc_sbuf_tensor("grow", [2, TT], BF16)
    rab2 = xp[:, 0, :]
    ksb2 = xp[:, 1, :]
    ktbq = tmpb
    u2bf = g.u2.bitcast(BF16)
    apb = u2bf[:, 0:256]
    dtb = u2bf[:, 256:512]

    p = Prog(nc)
    init_consts(p, g)
    p.op("ve", lambda e: e.memset(grow[:], -1.0), writes=["grow_c"])

    def gload(dst, src, key):
        p.op("gp", lambda e, dst=dst, src=src: e.dma_start(dst, src),
             writes=[key], dma=True)

    gload(tern_sb[:], tern_i.rearrange("(c p) o -> p c o", p=128), "tern_res")
    gload(wx0[:], wx0T.rearrange("(c p) o -> p c o", p=128), "wx0")
    gload(wx2[0:96, 0, :], wx2T[0:96, :], "wx2")
    gload(wx2[0:64, 1, :], wx2T[96:160, :], "wx2")
    gload(wr[:], wrT.rearrange("(c p) o -> p c o", p=128), "wr")
    gload(ww1[:], ww1T.rearrange("(c p) o -> p c o", p=128), "ww1")
    gload(ww2[:], ww2T, "ww2")
    gload(wk[:], wkT.rearrange("(c p) o -> p c o", p=128), "wk")
    gload(wv[:], wvT.rearrange("(c p) o -> p c o", p=128), "wv")
    gload(wg[:], wgT.rearrange("(c p) o -> p c o", p=128), "wg")
    gload(wo[:], woT.rearrange("(c p) o -> p c o", p=128), "wo")
    gload(id128b[:], identf, "id128b")
    gload(id64b[:], ident64d, "id64b")
    gload(ubon[:], bonusT, "ubon")
    gload(gnwr[:], gnwrow, "gnwr")
    p.op("sy", lambda e: e.dma_start(gncol[:], gncolw), writes=["gncol"],
         dma=True)
    p.op("sy", lambda e: e.dma_start(id128f[:], identf), writes=["id128f"],
         dma=True)
    p.op("sy", lambda e: e.dma_start(msku[:], masku), writes=["msku"],
         dma=True)
    p.op("sy", lambda e: e.dma_start(gcol[:], gpc), writes=["gbcols"],
         dma=True)
    p.op("sy", lambda e: e.dma_start(bcol[:], bpc), writes=["gbcols"],
         dma=True)
    p.op("sy", lambda e: e.dma_start(mu0c[:], mu0pc), writes=["mu0c"],
         dma=True)
    p.op("sy", lambda e: e.dma_start(xbc[:], xbiaspc), writes=["xbc"],
         dma=True)
    p.op("sy", lambda e: e.dma_start(bw2c[:], bw2pc), writes=["bw2c"],
         dma=True)

    p.op("ve", lambda e: e.memset(mxc[:], 0.0), writes=["mxc"])
    emit_quant_prologue(p, g, lmax_all, alpha_i, lm_sb, al_sb, invsc_col,
                        epi_col)
    emit_ffnc(p, g, x0, x1d, rows1, tern_sb, gcol, bcol, invsc_col, epi_col,
              xt, xiq, with_stats=True, rows2_dram=rows2d,
              xb=gxb, sqb=sqr, src_key="x0d", dst_key="x1d")

    delta = dob
    xi = xiq
    gsb = gxb
    ob = dob
    rkb = sqr
    xt2 = tern_sb.bitcast(F32)

    def xbuf(t):
        return (xt, "fxt", ["fxt"]) if t % 2 == 0 else                (xt2, "fxt1", ["fxt1", "tern_res"])

    def xload(t):
        bb, uu = divmod(t, NTPS)
        ssl = slice(uu * TT, (uu + 1) * TT)
        xv = x1d[bb].rearrange("(c p) t -> p c t", p=128)[:, :, ssl]
        dst, _, wk = xbuf(t)
        p.op("sy", lambda e, xv=xv, dst=dst: e.dma_start(dst[:], xv),
             reads=["x1d"], writes=wk, dma=True)

    xload(0)
    for tt in range(NTILE):
        b, tau = divmod(tt, NTPS)
        sl = slice(tau * TT, (tau + 1) * TT)
        xcur, kx, _ = xbuf(tt)
        if tt + 1 < NTILE:
            xload(tt + 1)
        p.op("sy", lambda e, tt=tt: e.dma_start(
            g.row_s[:], rows2d[0:1, tt * TT:(tt + 1) * TT]),
            reads=["rows_dram_st"], writes=["row_s"], dma=True)
        p.op("sy", lambda e, tt=tt: e.dma_start(
            g.row_ms[:], rows2d[1:2, tt * TT:(tt + 1) * TT]),
            reads=["rows_dram_st"], writes=["row_ms"], dma=True)
        bcast_rows(p, g)
        if tau == 0:
            p.op("ve", lambda e: e.memset(xln[:, :, 0:1], 0.0),
                 reads=["xln"], writes=["xlncar"])
            p.op("ve", lambda e: e.memset(flat(scar), 0.0),
                 reads=[f"scar{i}" for i in range(NH)],
                 writes=[f"scar{i}" for i in range(NH)])
            p.op("ve", lambda e: e.memset(encar[:], 0.0),
                 reads=["encar0", "encar1"], writes=["encar0", "encar1"])
        else:
            p.op("ve", lambda e: e.tensor_copy(xln[:, :, 0:1],
                                               xln[:, :, TT:TT + 1]),
                 reads=["xln"], writes=["xlncar"])
        for k in range(NCH):
            p.op("ve", lambda e, k=k, xcur=xcur: e.tensor_tensor(
                g.u1[:], xcur[:, k, :], g.sbc[:], ALU.mult),
                reads=[kx, "sbc"], writes=["u1"])
            p.op("ve", lambda e, k=k: e.tensor_tensor(
                xln[:, k, 1:TT + 1], g.u1[:], g.msbc[:], ALU.subtract),
                reads=["u1", "msbc"], writes=["xln"])
        p.op("ve", lambda e: e.tensor_tensor(
            delta[:], xln[:, :, 0:TT], xln[:, :, 1:TT + 1], ALU.subtract),
            reads=["xln", "xlncar"], writes=[f"dob{i}" for i in range(NCH)])
        for k in range(NCH):
            p.op("ve", lambda e, k=k: e.scalar_tensor_tensor(
                xi[:, k, :], delta[:, k, :], mu0c[:, k:k + 1],
                xln[:, k, 1:TT + 1], ALU.mult, ALU.add),
                reads=[f"dob{k}", "mu0c", "xln"], writes=[f"xiq{k}"])
        for mg, (mlo, mhi) in enumerate([(0, 96), (96, 160)]):
            pb, pk = g.PS[mg], f"ps{mg}"
            for k in range(NCH):
                p.op("te", lambda e, k=k, mlo=mlo, mhi=mhi, pb=pb: e.matmul(
                    pb[0:mhi - mlo, 0:512], wx0[:, k, mlo:mhi], xi[:, k, :],
                    start=(k == 0), stop=(k == NCH - 1)),
                    reads=[f"xiq{k}", "wx0"], writes=[pk])
            p.op("sc", lambda e, mg=mg, mlo=mlo, mhi=mhi, pb=pb: e.activation(
                xp[0:mhi - mlo, mg, :], pb[0:mhi - mlo, 0:512], AF.Tanh),
                reads=[pk], writes=["xp", "xpr", "xpk"])
        tmpb2 = xnb.bitcast(BF16)[:, 0:512]

        def emit_mu1(n, m):
            xpc, xpo = ((0, n * 32) if n < 3 else (1, (n - 3) * 32))
            pb, pk = g.PS[2 + m % 4], f"ps{2 + m % 4}"
            p.op("te", lambda e, m=m, xpc=xpc, xpo=xpo, pb=pb: e.matmul(
                pb[:, 0:512],
                wx2[xpo:xpo + 32, xpc, m * 128:(m + 1) * 128],
                xp[xpo:xpo + 32, xpc, :], start=True, stop=True),
                reads=["xp", "wx2"], writes=[pk])

        def emit_wave(n):
            for m in range(NCH):
                pb, pk = g.PS[2 + m % 4], f"ps{2 + m % 4}"
                t_b, tk = ((tmpb[:], "tmpb") if m % 2 == 0 else
                           (tmpb2, "xnb"))
                p.op("ve", lambda e, n=n, m=m, pb=pb, t_b=t_b:
                     e.scalar_tensor_tensor(
                         t_b, pb[:, 0:512], xbc[:, n, m:m + 1],
                         delta[:, m, :], ALU.add, ALU.mult),
                     reads=[pk, f"dob{m}"], sreads=["xbc"],
                     writes=[tk])
                p.op("gp" if m % 2 == 0 else "ve",
                     lambda e, m=m, t_b=t_b: e.tensor_tensor(
                         xi[:, m, :], t_b, xln[:, m, 1:TT + 1], ALU.add),
                     reads=[tk, "xln"], writes=[f"xiq{m}"])
                if m < 4:
                    emit_mu1(n, m + 4)
                elif n < 4:
                    emit_mu1(n + 1, m - 4)

        for m in range(4):
            emit_mu1(0, m)
        for n in range(5):
            emit_wave(n)
            if n in (0, 2):
                dst, dk = (rb, "rb") if n == 0 else (kbuf, "kbuf")
                wt, wkey = (wr, "wr") if n == 0 else (wk, "wk")
                for m in range(NH):
                    pb, pk = g.PS[m % 2], f"ps{m % 2}"
                    for k in range(NCH):
                        p.op("te", lambda e, m=m, k=k, pb=pb, wt=wt: e.matmul(
                            pb[:, 0:512], wt[:, k, m * 128:(m + 1) * 128],
                            xi[:, k, :], start=(k == 0), stop=(k == NCH - 1)),
                            reads=[f"xiq{k}", wkey], writes=[pk])
                    p.op("sc", lambda e, m=m, pb=pb, dst=dst, dk=dk:
                         e.activation(dst[:, m, :], pb[:, 0:512], AF.Copy),
                         reads=[pk], writes=[f"{dk}{m}"])
            elif n == 1:
                for k in range(NCH):
                    p.op("te", lambda e, k=k: e.matmul(
                        g.PS[0][0:64, 0:512], ww1[:, k, :], xi[:, k, :],
                        start=(k == 0), stop=(k == NCH - 1)),
                        reads=[f"xiq{k}", "ww1"], writes=["ps0"])
                p.op("sc", lambda e: e.activation(
                    w1b[:], g.PS[0][0:64, 0:512], AF.Tanh),
                    reads=["ps0"], writes=["w1b"])
                for m in range(NH):
                    pb, pk = g.PS[m % 2], f"ps{m % 2}"
                    p.op("te", lambda e, m=m, pb=pb: e.matmul(
                        pb[:, 0:512], ww2[:, m * 128:(m + 1) * 128], w1b[:],
                        start=True, stop=True),
                        reads=["w1b", "ww2"], writes=[pk])
                    p.op("sc", lambda e, m=m, pb=pb: e.activation(
                        posf[:, m, :], pb[:, 0:512], AF.Exp,
                        bias=bw2c[:, m:m + 1]),
                        reads=[pk, "bw2c"], writes=[f"posf{m}"])
            elif n == 3:
                for ts in range(4):
                    for nv in range(2):
                        pb, pk = g.PS[nv], f"ps{nv}"
                        for k in range(NCH):
                            p.op("te", lambda e, ts=ts, nv=nv, k=k, pb=pb:
                                 e.matmul(pb[:, 0:512],
                                          xi[:, k, ts * 128:(ts + 1) * 128],
                                          wv[:, k, nv * 512:(nv + 1) * 512],
                                          start=(k == 0),
                                          stop=(k == NCH - 1)),
                                 reads=[f"xiq{k}", "wv"], writes=[pk])
                        p.op("sc", lambda e, ts=ts, nv=nv, pb=pb: e.activation(
                            vtm[:, ts, nv * 512:(nv + 1) * 512],
                            pb[:, 0:512], AF.Copy),
                            reads=[pk], writes=["vtm"])
            else:
                for m in range(NCH):
                    pb, pk = g.PS[m % 2], f"ps{m % 2}"
                    for k in range(NCH):
                        p.op("te", lambda e, m=m, k=k, pb=pb: e.matmul(
                            pb[:, 0:512], wg[:, k, m * 128:(m + 1) * 128],
                            xi[:, k, :], start=(k == 0), stop=(k == NCH - 1)),
                            reads=[f"xiq{k}", "wg"], writes=[pk])
                    p.op("sc", lambda e, m=m, pb=pb: e.activation(
                        gsb[:, m, :], pb[:, 0:512], AF.Silu),
                        reads=[pk], writes=["gxb"])
        p.op("gp", lambda e: e.tensor_tensor(
            rkb[:, 0:NH, :].rearrange("p a b -> p (a b)"),
            flat(rb), flat(kbuf), ALU.mult),
            reads=[f"rb{i}" for i in range(NH)]
                  + [f"kbuf{i}" for i in range(NH)], writes=["sqr"])

        def scan_factors(h):
            hp = h % 2
            en, ek = enb2[hp], f"enb{hp}"
            eck = f"encar{hp}"
            fcall, fck, frk = fcall2[hp], f"fcall{hp}", f"fcraw{hp}"
            if tau == 0:
                p.op("ve", lambda e, en=en: e.memset(en[:, 0:1], 0.0),
                     reads=[ek], writes=[ek])
            else:
                p.op("ve", lambda e, h=h, en=en: e.tensor_copy(
                    en[:, 0:1], encar[:, h:h + 1]),
                    reads=[eck, ek], writes=[ek])
            p.op("ve", lambda e, h=h, tau=tau, en=en: e.tensor_tensor_scan(
                en[:, 1:TT + 1], posf[:, h, :], g.zeros[:],
                (encar[:, h:h + 1] if tau > 0 else 0.0), ALU.add, ALU.add),
                reads=[f"posf{h}", "zeros", eck, ek], writes=[ek], tail=True)
            p.op("ve", lambda e, h=h, en=en: e.tensor_copy(
                encar[:, h:h + 1], en[:, TT:TT + 1]),
                reads=[ek], writes=[eck])
            p.op("ve", lambda e, en=en: e.tensor_tensor(
                xnb[:], en[:, 0:512],
                en[:, 0:512:64].broadcast_to((128, 8, 64)), ALU.subtract),
                reads=[ek], writes=["xnb"])
            p.op("ve", lambda e, en=en, fcall=fcall: e.tensor_tensor(
                fcall[:, 0:8], en[:, 32:513:64], en[:, 0:512:64],
                ALU.subtract), reads=[ek], writes=[frk])
            p.op("ve", lambda e, en=en, fcall=fcall: e.tensor_tensor(
                fcall[:, 8:16], en[:, 32:513:64], en[:, 64:513:64],
                ALU.subtract), reads=[ek], writes=[frk])
            p.op("ve", lambda e, en=en, fcall=fcall: e.tensor_tensor(
                fcall[:, 16:24], en[:, 0:512:64], en[:, 64:513:64],
                ALU.subtract), reads=[ek], writes=[frk])
            p.op("sc", lambda e, fcall=fcall: e.activation(
                fcall[:, 0:24], fcall[:, 0:24], AF.Exp),
                reads=[frk], writes=[fck])
            p.op("sc", lambda e, h=h: e.activation(
                posf[:, h, :], xnb[:], AF.Exp, scale=-1.0),
                reads=["xnb"], writes=[f"posf{h}"])
            p.op("ve", lambda e, h=h: e.tensor_tensor(
                rb[:, h, :], rb[:, h, :], posf[:, h, :], ALU.mult),
                reads=[f"rb{h}", f"posf{h}"], writes=[f"rb{h}"])
            p.op("ve", lambda e, h=h, fcall=fcall: e.tensor_tensor(
                rab2, rb[:, h, :],
                fcall[:, 0:8].broadcast_to((128, 8, 64)), ALU.mult),
                reads=[f"rb{h}", fck], writes=["xpr"])
            p.op("ve", lambda e, en=en: e.tensor_tensor(
                xnb[:], en[:, 1:513],
                en[:, 32:513:64].broadcast_to((128, 8, 64)), ALU.subtract),
                reads=[ek, "xnb"], writes=["xnb"])
            p.op("sc", lambda e, h=h: e.activation(
                posf[:, h, :], xnb[:], AF.Exp),
                reads=["xnb", f"posf{h}"], writes=[f"posf{h}"])
            p.op("ve", lambda e, h=h: e.tensor_tensor(
                kbuf[:, h, :], kbuf[:, h, :], posf[:, h, :], ALU.mult),
                reads=[f"kbuf{h}", f"posf{h}"], writes=[f"kbuf{h}"])
            p.op("ve", lambda e, h=h, fcall=fcall: e.tensor_tensor(
                ksb2, kbuf[:, h, :],
                fcall[:, 8:16].broadcast_to((128, 8, 64)), ALU.mult),
                reads=[f"kbuf{h}", fck], writes=["xpk"])

        def scan_pre(h):
            for c in range(NCT):
                t0 = c * C
                pb_ = 64 * (c % 2)
                q = c // 2
                p.op("te", lambda e, t0=t0, pb_=pb_, q=q: e.transpose(
                    g.PS_KT[pb_:pb_ + 64, 128 * q:128 * q + 128],
                    ksb2[:, t0:t0 + C], id128b[:]),
                    reads=["xpk", "id128b"], writes=[f"pskt{c % 2}"])
            for pp in range(2):
                pb_ = 64 * pp
                p.op("sc", lambda e, pb_=pb_: e.activation(
                    ktbq[pb_:pb_ + 64, :], g.PS_KT[pb_:pb_ + 64, :],
                    AF.Copy),
                    reads=[f"pskt{pp}"], writes=["tmpb"])
            for c in range(NCT):
                t0 = c * C
                pb_ = 64 * (c % 2)
                q = c // 2
                bank, bk = g.PS[2 + c % 2], f"ps{2 + c % 2}"
                p.op("te", lambda e, h=h, t0=t0, pb_=pb_, q=q, bank=bank:
                     e.matmul(bank[pb_:pb_ + 64, 64 * q:64 * q + 64],
                              kbuf[:, h, t0:t0 + C], rab2[:, t0:t0 + C],
                              start=True, stop=True),
                     reads=[f"kbuf{h}", "xpr"], writes=[bk])
                p.op("te", lambda e, h=h, t0=t0, pb_=pb_, q=q, bank=bank:
                     e.matmul(bank[pb_:pb_ + 64, 384 + q:385 + q],
                              rkb[:, h, t0:t0 + C], ubon[:, h:h + 1],
                              start=True, stop=True),
                     reads=["sqr", "ubon"], writes=[bk])
            for pp in range(2):
                pb_ = 64 * pp
                bank, bk = g.PS[2 + pp], f"ps{2 + pp}"
                p.op("ve", lambda e, pb_=pb_, bank=bank: e.tensor_copy(
                    dcolb[pb_:pb_ + 64, 0:4], bank[pb_:pb_ + 64, 384:388]),
                    reads=[bk], writes=[f"dcolb{pp}"])
                for q in range(4):
                    p.op("ve", lambda e, pb_=pb_, q=q: e.tensor_scalar(
                        dtb[pb_:pb_ + 64, 64 * q:64 * q + 64],
                        id64b[pb_:pb_ + 64, :], dcolb[pb_:pb_ + 64, q:q + 1],
                        None, ALU.mult),
                        reads=["id64b"], sreads=[f"dcolb{pp}"],
                        writes=["u2"])
                p.op("ve", lambda e, pb_=pb_, bank=bank: e.select(
                    apb[pb_:pb_ + 64, 0:256],
                    msku[pb_:pb_ + 64, :][:, None, :].broadcast_to(
                        (64, 4, 64)),
                    bank[pb_:pb_ + 64, 0:256], dtb[pb_:pb_ + 64, 0:256]),
                    reads=[bk, "msku", "u2"], writes=["u2"],
                    eager=True)

        def scan_state(h):
            fcall, fck = fcall2[h % 2], f"fcall{h % 2}"

            def m_mm(c):
                pb_ = 64 * (c % 2)
                q = c // 2
                sl_ = 256 * ((c // 2) % 2)
                mbank, mk = g.PS[4 + c % 2], f"ps{4 + c % 2}"
                p.op("te", lambda e, h=h, q=q, pb_=pb_, sl_=sl_,
                     mbank=mbank: e.matmul(
                         mbank[:, sl_:sl_ + 256],
                         ktbq[pb_:pb_ + 64, 128 * q:128 * q + 128],
                         vtm[pb_:pb_ + 64, q, h * 256:(h + 1) * 256],
                         start=True, stop=True),
                     reads=["tmpb", "vtm"], writes=[mk])

            for c in range(4):
                m_mm(c)
            for c in range(NCT):
                t0 = c * C
                pb_ = 64 * (c % 2)
                q = c // 2
                st = c // 2
                sl_ = 256 * ((c // 2) % 2)
                bank, bk = g.PS[2 + c % 2], f"ps{2 + c % 2}"
                mbank, mk = g.PS[4 + c % 2], f"ps{4 + c % 2}"
                sprev_key = f"scar{h}" if c == 0 else f"spp{(c - 1) % 2}"
                sprev = (scar[:, h, :] if c == 0 else spp[(c - 1) % 2][:])
                sout_key = f"scar{h}" if c == NCT - 1 else f"spp{c % 2}"
                sout = (scar[:, h, :] if c == NCT - 1 else spp[c % 2][:])
                p.op("ve", lambda e, sprev=sprev, sout=sout, c=c, sl_=sl_,
                     mbank=mbank, fcall=fcall:
                     e.scalar_tensor_tensor(sout, sprev,
                                            fcall[:, 16 + c:17 + c],
                                            mbank[:, sl_:sl_ + 256],
                                            ALU.mult, ALU.add),
                     reads=[sprev_key, mk], sreads=[fck],
                     writes=[sout_key])
                if c + 4 < NCT:
                    m_mm(c + 4)
                for dvc in range(2):
                    p.op("te", lambda e, h=h, st=st, pb_=pb_, q=q, dvc=dvc,
                         bank=bank: e.matmul(
                             bank[:, 256 + dvc * 64:256 + dvc * 64 + 64],
                             vtm[pb_:pb_ + 64, st,
                                 h * 256 + dvc * 128:h * 256 + (dvc + 1) * 128],
                             apb[pb_:pb_ + 64, 64 * q:64 * q + 64],
                             start=True, stop=False),
                         reads=["vtm", "u2"], writes=[bk])
                    p.op("te", lambda e, sprev=sprev, dvc=dvc, h=h, t0=t0,
                         bank=bank: e.matmul(
                             bank[:, 256 + dvc * 64:256 + dvc * 64 + 64],
                             sprev[:, dvc * 128:(dvc + 1) * 128],
                             rb[:, h, t0:t0 + C], start=False, stop=True),
                         reads=[sprev_key, f"rb{h}"], writes=[bk])
                p.op("sc", lambda e, h=h, t0=t0, bank=bank: e.activation(
                    ob[:, 2 * h:2 * h + 2, t0:t0 + C],
                    bank[:, 256:384], AF.Copy),
                    reads=[bk], writes=[f"dob{2 * h}", f"dob{2 * h + 1}"])

        scan_factors(0)
        for h in range(NH):
            scan_pre(h)
            if h + 1 < NH:
                scan_factors(h + 1)
            scan_state(h)

        p.op("sc", lambda e: e.activation(flat(sqr), flat(ob), AF.Square),
             reads=[f"dob{i}" for i in range(NCH)], writes=["sqr"])
        def gn_banks(h):
            pbS, pkS = ((g.PS[5], "ps5") if h % 2 == 0 else (g.PS[2], "ps2"))
            pbQ, pkQ = ((g.PS[6], "ps6") if h % 2 == 0 else (g.PS[3], "ps3"))
            return pbS, pkS, pbQ, pkQ

        def gn_stats(h):
            pbS, pkS, pbQ, pkQ = gn_banks(h)
            for (src_t, key, pb, pk) in ((ob, "dob", pbS, pkS),
                                         (sqr, "sqr", pbQ, pkQ)):
                for cc in range(2):
                    p.op("te", lambda e, h=h, cc=cc, src_t=src_t, pb=pb,
                         key=key, pk=pk: e.matmul(
                             pb[0:1, 0:512], g.ones_b[:],
                             src_t[:, 2 * h + cc, :],
                             start=(cc == 0), stop=(cc == 1)),
                         reads=[(f"dob{2 * h + cc}" if key == "dob"
                                 else key), "ones_b"], writes=[pk])

        gn_stats(0)
        for h in range(NH):
            pbS, pkS, pbQ, pkQ = gn_banks(h)
            if h + 1 < NH:
                gn_stats(h + 1)
            rw = g.rowwork
            p.op("sc", lambda e, pbS=pbS: e.activation(
                rw[0:1, 0:512], pbS[0:1, 0:512], AF.Square,
                scale=1.0 / DV), reads=[pkS], writes=["rw0"])
            p.op("ve", lambda e, pbQ=pbQ: e.tensor_scalar(
                rw[0:1, 512:1024], pbQ[0:1, 0:512],
                1.0 / DV, EPS, ALU.mult, ALU.add),
                reads=[pkQ], writes=["rw1"])
            p.op("ve", lambda e: e.scalar_tensor_tensor(
                rw[0:1, 1024:1536], rw[0:1, 0:512], -1.0, rw[0:1, 512:1024],
                ALU.mult, ALU.add), reads=["rw0", "rw1"], writes=["rw2"])
            p.op("sc", lambda e: e.activation(rw[0:1, 0:512],
                                              rw[0:1, 1024:1536], AF.Ln),
                 reads=["rw2"], writes=["rw0"])
            p.op("sc", lambda e: e.activation(g.row_s[:], rw[0:1, 0:512],
                                              AF.Exp, scale=-0.5),
                 reads=["rw0"], writes=["row_s"])
            p.op("ve", lambda e, pbS=pbS: e.scalar_tensor_tensor(
                g.row_ms[:], g.row_s[:], 1.0 / DV, pbS[0:1, 0:512],
                ALU.mult, ALU.mult),
                reads=["row_s", pkS], writes=["row_ms"])
            p.op("ve", lambda e: e.tensor_copy(grow[0:1, :], g.row_ms[:]),
                 reads=["row_ms"], writes=["grow"])
            p.op("te", lambda e: e.matmul(
                g.PS[0][:, 0:512], g.onesf[:], g.row_s[:],
                start=True, stop=True),
                reads=["onesf", "row_s"], writes=["ps0"])
            for dvc in range(2):
                ch = 2 * h + dvc
                pbB, pkB = ((g.PS[1], "ps1") if dvc == 0 else
                            (g.PS[4], "ps4"))
                t_b, tk = ((tmpb[:], "tmpb") if dvc == 0 else
                           (posf[:, 2, :], "posf2"))
                x_b, xk = ((xnb[:], "xnb") if dvc == 0 else
                           (posf[:, 3, :], "posf3"))
                p.op("te", lambda e, ch=ch, pbB=pbB: e.matmul(
                    pbB[:, 0:512], gnwr[0:2, ch * 128:(ch + 1) * 128],
                    grow[:], start=True, stop=True),
                    reads=["gnwr", "grow", "grow_c"], writes=[pkB])
                p.op("ve", lambda e, ch=ch, t_b=t_b: e.tensor_tensor(
                    t_b, ob[:, ch, :], g.PS[0][:, 0:512], ALU.mult),
                    reads=[f"dob{ch}", "ps0"], writes=[tk])
                p.op("ve", lambda e, ch=ch, pbB=pbB, t_b=t_b, x_b=x_b:
                     e.scalar_tensor_tensor(
                         x_b, t_b, gncol[:, ch:ch + 1], pbB[:, 0:512],
                         ALU.mult, ALU.subtract),
                    reads=[tk, pkB], sreads=["gncol"], writes=[xk])
                p.op("gp", lambda e, ch=ch, x_b=x_b: e.tensor_tensor(
                    ob[:, ch, :], x_b, gsb[:, ch, :], ALU.mult),
                    reads=[xk, "gxb"], writes=[f"dob{ch}"])
        for m in range(NCH):
            pb, pk = g.PS[m % 2], f"ps{m % 2}"
            for k in range(NCH):
                p.op("te", lambda e, m=m, k=k, pb=pb: e.matmul(
                    pb[:, 0:512], wo[:, k, m * 128:(m + 1) * 128],
                    ob[:, k, :], start=(k == 0), stop=(k == NCH - 1)),
                    reads=[f"dob{k}", "wo"], writes=[pk])
            p.op("ve", lambda e, m=m, pb=pb, xcur=xcur: e.tensor_tensor(
                xcur[:, m, :], pb[:, 0:512], xcur[:, m, :], ALU.add),
                reads=[pk, kx], writes=[kx])
        p.op("sy", lambda e, b=b, sl=sl, xcur=xcur: e.dma_start(
            x2_o[b].rearrange("(c p) t -> p c t", p=128)[:, :, sl],
            xcur[:]),
            reads=[kx], writes=["x2d"], dma=True)
        ep_u1, ep_u2 = posf[:, 0, :], posf[:, 1, :]
        ep_sb, ep_msb = posf[:, 2, :], posf[:, 3, :]
        p.op("sc", lambda e, xcur=xcur: e.activation(flat(gxb), flat(xcur),
                                                     AF.Copy),
             reads=[kx], writes=["gxb"])
        p.op("sc", lambda e, xcur=xcur: e.activation(flat(sqr), flat(xcur),
                                                     AF.Square),
             reads=[kx], writes=["sqr"])
        ln_stats_mm(p, g, "gxb", lambda k: gxb[:, k, :], "sqr",
                    lambda k: sqr[:, k, :])
        ln_rows(p, g, True, rows3_o, tt)
        p.op("te", lambda e: e.matmul(g.PS[5][:, 0:512], g.onesf[:],
                                      g.row_s[:], start=True, stop=True),
             reads=["row_s", "onesf"], writes=["ps5"])
        p.op("te", lambda e: e.matmul(g.PS[6][:, 0:512], g.onesf[:],
                                      g.row_ms[:], start=True, stop=True),
             reads=["row_ms", "onesf"], writes=["ps6"])
        p.op("sc", lambda e, ep_sb=ep_sb: e.activation(
            ep_sb, g.PS[5][:, 0:512], AF.Copy),
            reads=["ps5"], writes=["posf2"])
        p.op("sc", lambda e, ep_msb=ep_msb: e.activation(
            ep_msb, g.PS[6][:, 0:512], AF.Copy),
            reads=["ps6"], writes=["posf3"])
        for k in range(NCH):
            p.op("ve", lambda e, k=k, xcur=xcur, ep_u1=ep_u1, ep_sb=ep_sb:
                 e.tensor_tensor(ep_u1, xcur[:, k, :], ep_sb, ALU.mult),
                 reads=[kx, "posf2"], writes=["posf0"])
            p.op("ve", lambda e, ep_u1=ep_u1, ep_u2=ep_u2, ep_msb=ep_msb:
                 e.tensor_tensor(ep_u2, ep_u1, ep_msb, ALU.subtract),
                 reads=["posf0", "posf3"], writes=["posf1"])
            p.op("sc", lambda e, k=k, ep_u2=ep_u2: e.activation(
                xnb[:], ep_u2, AF.Identity, scale=gcol[:, k:k + 1],
                bias=bcol[:, k:k + 1]),
                reads=["posf1", "gbcols"], writes=["xnb"])
            p.op("ve", lambda e, tt=tt, k=k: e.reduce_max(
                mxc[:, tt * NCH + k:tt * NCH + k + 1], xnb[:], axis=AX.X,
                apply_absolute_value=True),
                reads=["xnb"], writes=["mxc"], tail=True)

    reduce_scalar_max(p, g, mxc[:], "mxc", id128f, lmax2_o)
    p.finish()
    p.emit()
    return nc


_NC_CACHE = {}


def _get_nc(which):
    if which not in _NC_CACHE:
        _NC_CACHE[which] = {"L1": build_L1, "L2": build_L2, "L3": build_L3}[which]()
    return _NC_CACHE[which]


def _pc(v, nch=NCH):
    return np.ascontiguousarray(np.asarray(v, np.float32).reshape(nch, 128).T)


def kernel(L0_attn_hidden_states, L0_attn_mu0, L0_attn_W_x0, L0_attn_W_x2,
           L0_attn_x_bias, L0_attn_W_r, L0_attn_W_w1, L0_attn_W_w2,
           L0_attn_b_w2, L0_attn_W_k, L0_attn_W_v, L0_attn_W_g,
           L0_attn_bonus, L0_attn_gn_weight, L0_attn_gn_bias, L0_attn_W_o,
           L0_ffn_weight, L0_ffn_gamma, L0_ffn_beta):
    f32 = np.float32
    x = np.asarray(L0_attn_hidden_states, f32)
    xfm = np.ascontiguousarray(x.transpose(0, 2, 1))
    cores = list(range(NCORES))

    identf = np.eye(128, dtype=f32)
    ident64d = np.ascontiguousarray(
        np.concatenate([np.eye(64, dtype=f32)] * 2, axis=0))
    jj = np.arange(128)[:, None] % 64
    ii = np.arange(64)[None, :]
    masku = (jj < ii).astype(np.uint8)

    wffnT = np.ascontiguousarray(np.asarray(L0_ffn_weight, f32).T)
    gammapc = _pc(L0_ffn_gamma)
    betapc = _pc(L0_ffn_beta)

    nc1 = _get_nc("L1")
    in1 = []
    for c in cores:
        in1.append(dict(
            x0=np.ascontiguousarray(xfm[2 * c:2 * c + 2]),
            wffnT=wffnT, gammapc=gammapc, betapc=betapc, identf=identf))
    r1 = run_bass_kernel_spmd(nc1, in1, core_ids=cores).results
    lmax1_all = np.ascontiguousarray(
        np.stack([r1[c]["lmax"][0] for c in cores], axis=1))

    wx2 = np.asarray(L0_attn_W_x2, f32).reshape(D, 5, 32)
    wx2T = np.ascontiguousarray(wx2.transpose(1, 2, 0).reshape(160, D))
    xbias = np.asarray(L0_attn_x_bias, f32)
    xbiaspc = np.ascontiguousarray(
        xbias.reshape(5, NCH, 128).transpose(2, 0, 1))
    nc2 = _get_nc("L2")
    in2 = []
    for c in cores:
        in2.append(dict(
            x0=in1[c]["x0"],
            tern=r1[c]["tern"], alpha=r1[c]["alpha"], rows1=r1[c]["rows1"],
            lmax_all=lmax1_all,
            gammapc=gammapc, betapc=betapc, identf=identf,
            ident64d=ident64d, masku=masku,
            wx0T=np.ascontiguousarray(np.asarray(L0_attn_W_x0, f32).T),
            wx2T=wx2T,
            wrT=np.ascontiguousarray(np.asarray(L0_attn_W_r, f32).T),
            ww1T=np.ascontiguousarray(np.asarray(L0_attn_W_w1, f32).T),
            ww2T=np.ascontiguousarray(np.asarray(L0_attn_W_w2, f32).T),
            wkT=np.ascontiguousarray(np.asarray(L0_attn_W_k, f32).T),
            wvT=np.ascontiguousarray(np.asarray(L0_attn_W_v, f32).T),
            wgT=np.ascontiguousarray(np.asarray(L0_attn_W_g, f32).T),
            woT=np.ascontiguousarray(np.asarray(L0_attn_W_o, f32).T),
            mu0pc=_pc(L0_attn_mu0),
            xbiaspc=xbiaspc,
            bw2pc=_pc(L0_attn_b_w2, 4),
            bonusT=np.ascontiguousarray(np.asarray(L0_attn_bonus, f32).T),
            gnwrow=np.ascontiguousarray(np.stack(
                [np.asarray(L0_attn_gn_weight, f32),
                 np.asarray(L0_attn_gn_bias, f32)])),
            gncolw=_pc(L0_attn_gn_weight),
        ))
    r2 = run_bass_kernel_spmd(nc2, in2, core_ids=cores).results
    lmax2_all = np.ascontiguousarray(
        np.stack([r2[c]["lmax2"][0] for c in cores], axis=1))

    nc3 = _get_nc("L3")
    in3 = []
    for c in cores:
        in3.append(dict(
            x2=r2[c]["x2"], tern=r1[c]["tern"], alpha=r1[c]["alpha"],
            rows3=r2[c]["rows3"], lmax_all=lmax2_all,
            gammapc=gammapc, betapc=betapc))
    r3 = run_bass_kernel_spmd(nc3, in3, core_ids=cores).results

    out_fm = np.concatenate([r3[c]["out"] for c in cores], axis=0)
    return np.ascontiguousarray(out_fm.transpose(0, 2, 1))


# version 14
# speedup vs baseline: 1.0399x; 1.0131x over previous
import numpy as np
import ml_dtypes
import concourse.bass as bass
import concourse.mybir as mybir
from concourse.bass_utils import run_bass_kernel_spmd

F32 = mybir.dt.float32
BF16 = mybir.dt.bfloat16
U8 = mybir.dt.uint8
ALU = mybir.AluOpType
AF = mybir.ActivationFunctionType
AX = mybir.AxisListType

NCORES = 8
B_LOC = 2
T = 2048
TT = 512
NTPS = T // TT
NTILE = B_LOC * NTPS
D = 1024
NCH = 8
NH = 4
DK = 128
DV = 256
C = 64
NCT = TT // C
EPS = 1e-5
MAGIC = 12582912.0

ENGS = ("sy", "ve", "sc", "te", "gp")
PARANOID = [False]
AUDIT = [False]


class Prog:

    def __init__(self, nc):
        self.nc = nc
        self.ops = {e: [] for e in ENGS}
        self.nop = {e: 0 for e in ENGS}
        self.last_w = {}
        self.readers = {}
        self.waited = {}
        self.audited = set()

    def op(self, eng, fn, reads=(), writes=(), dma=False, tail=False,
           sreads=(), eager=False):
        idx = self.nop[eng]
        waits = set()

        def dep(peng, pidx, ptail=False, force=False):
            if peng == eng and not (ptail or force):
                return
            if self.waited.get((eng, peng), -1) >= pidx:
                return
            if peng == eng and pidx == idx:
                return
            waits.add((peng, pidx))

        para = PARANOID[0]
        if AUDIT[0]:
            for k in list(reads) + list(sreads):
                if k not in self.last_w and k not in self.audited:
                    self.audited.add(k)
                    print(f"AUDIT read-before-write: {k} at {eng}#{idx}")
        for k in sreads:
            if k in self.last_w:
                dep(*self.last_w[k], force=True)
        for k in reads:
            if k in self.last_w:
                dep(*self.last_w[k], force=eager or para)
        for k in writes:
            if k in self.last_w:
                dep(*self.last_w[k][:2], force=para)
            for (re_, ri) in self.readers.get(k, ()):
                dep(re_, ri, force=para)
        for (peng, pidx) in waits:
            if pidx > self.waited.get((eng, peng), -1):
                self.waited[(eng, peng)] = pidx
        for k in list(reads) + list(sreads):
            self.readers.setdefault(k, []).append((eng, idx))
        for k in writes:
            self.last_w[k] = (eng, idx, tail)
            self.readers[k] = []
        self.ops[eng].append([fn, waits, dma])
        self.nop[eng] += 1

    def emit(self):
        nc = self.nc
        needed = {e: set() for e in ENGS}
        for e in ENGS:
            for rec in self.ops[e]:
                for (pe, pi) in rec[1]:
                    needed[pe].add(pi)
        sems = {e: nc.alloc_semaphore("sem_" + e) for e in ENGS}
        NPOOL = 12
        pools = {e: [nc.alloc_semaphore(f"dp_{e}{i}") for i in range(NPOOL)]
                 for e in ("sy", "gp")}
        pool_val = {e: [0] * NPOOL for e in ("sy", "gp")}
        pool_i = {e: 0 for e in ("sy", "gp")}
        mile = {}
        cnt = {e: 0 for e in ENGS}
        prewait = {}
        for e in ENGS:
            for i, rec in enumerate(self.ops[e]):
                fn, waits, dma = rec
                if fn is None:
                    continue
                if dma:
                    j = pool_i[e] % NPOOL
                    pool_i[e] += 1
                    s = pools[e][j]
                    if pool_val[e][j] > 0:
                        prewait[(e, i)] = (s, pool_val[e][j])
                    pool_val[e][j] += 16
                    mile[(e, i)] = (s, pool_val[e][j])
                elif i in needed[e]:
                    cnt[e] += 1
                    mile[(e, i)] = (sems[e], cnt[e])
                else:
                    mile[(e, i)] = None

        with nc.Block() as blk:
            def body_for(e):
                def body(eng):
                    for i, rec in enumerate(self.ops[e]):
                        fn, waits, dma = rec
                        for (pe, pi) in sorted(waits):
                            ms = mile[(pe, pi)]
                            eng.wait_ge(ms[0], ms[1])
                        if fn is None:
                            continue
                        if (e, i) in prewait:
                            pw = prewait[(e, i)]
                            eng.wait_ge(pw[0], pw[1])
                        inst = fn(eng)
                        ms = mile[(e, i)]
                        if ms is not None:
                            inst.then_inc(ms[0], 16 if dma else 1)
                return body
            blk.sync(body_for("sy"))
            blk.vector(body_for("ve"))
            blk.scalar(body_for("sc"))
            blk.tensor(body_for("te"))
            blk.gpsimd(body_for("gp"))
        return sems

    def finish(self):
        waits = set()
        for pe in ENGS:
            if self.nop[pe] == 0:
                continue
            if pe != "sy":
                waits.add((pe, self.nop[pe] - 1))
        for e in ("sy", "gp"):
            for i in range(len(self.ops[e]) - 1, -1, -1):
                if self.ops[e][i][2]:
                    waits.add((e, i))
                    break
        self.ops["sy"].append([None, waits, False])


def flat(t):
    ap = t[:]
    if len(ap.shape) == 3:
        return ap.rearrange("p a b -> p (a b)")
    return ap


class G:
    pass


def alloc_generic(nc, g):
    g.PS = [nc.alloc_psum_tensor(f"ps{i}", [128, 512], F32) for i in range(7)]
    g.PS_KT = nc.alloc_psum_tensor("pskt", [128, 512], BF16)
    g.ones_b = nc.alloc_sbuf_tensor("ones_b", [128, 1], BF16)
    g.onesf = nc.alloc_sbuf_tensor("onesf", [1, 128], F32)
    g.zeros = nc.alloc_sbuf_tensor("zeros", [128, 512], BF16)
    g.row_s = nc.alloc_sbuf_tensor("row_s", [1, 512], F32)
    g.row_ms = nc.alloc_sbuf_tensor("row_ms", [1, 512], F32)
    g.rowb1 = nc.alloc_sbuf_tensor("rowb1", [1, 512], BF16)
    g.rowwork = nc.alloc_sbuf_tensor("rowwork", [1, 1536], F32)
    g.scal = nc.alloc_sbuf_tensor("scal", [1, 16], F32)
    g.cols = nc.alloc_sbuf_tensor("colsb", [128, 96], F32)
    g.sbc = nc.alloc_sbuf_tensor("sbc", [128, 512], F32)
    g.msbc = nc.alloc_sbuf_tensor("msbc", [128, 512], F32)
    g.u1 = nc.alloc_sbuf_tensor("u1", [128, 512], F32)
    g.u2 = nc.alloc_sbuf_tensor("u2", [128, 512], F32)


def init_consts(p, g):
    p.op("ve", lambda e: e.memset(g.ones_b[:], 1.0), writes=["ones_b"])
    p.op("ve", lambda e: e.memset(g.onesf[:], 1.0), writes=["onesf"])
    p.op("ve", lambda e: e.memset(g.zeros[:], 0.0), writes=["zeros"])


def ln_stats_mm(p, g, xb_key, xb_ap, sq_key, sq_ap):
    for k in range(NCH):
        p.op("te", lambda e, k=k: e.matmul(
            g.PS[4][0:1, :], g.ones_b[:], xb_ap(k),
            start=(k == 0), stop=(k == NCH - 1)),
            reads=[xb_key, "ones_b"], writes=["ps4"])
    for k in range(NCH):
        p.op("te", lambda e, k=k: e.matmul(
            g.PS[5][0:1, :], g.ones_b[:], sq_ap(k),
            start=(k == 0), stop=(k == NCH - 1)),
            reads=[sq_key, "ones_b"], writes=["ps5"])


def ln_rows(p, g, composite, rows_dram, tt):
    a = (1.0 + EPS) if composite else 1.0
    e2 = (EPS * EPS) if composite else EPS
    rw = g.rowwork
    p.op("sc", lambda e: e.activation(rw[0:1, 0:512], g.PS[4][0:1, :],
                                      AF.Square, scale=1.0 / D),
         reads=["ps4"], writes=["rw0"])
    p.op("ve", lambda e: e.tensor_scalar(rw[0:1, 512:1024], g.PS[5][0:1, :],
                                         a / D, e2, ALU.mult, ALU.add),
         reads=["ps5"], writes=["rw1"])
    p.op("ve", lambda e: e.scalar_tensor_tensor(
        rw[0:1, 1024:1536], rw[0:1, 0:512], -a, rw[0:1, 512:1024],
        ALU.mult, ALU.add), reads=["rw0", "rw1"], writes=["rw2"])
    p.op("sc", lambda e: e.activation(rw[0:1, 0:512], rw[0:1, 1024:1536],
                                      AF.Ln),
         reads=["rw2"], writes=["rw0"])
    p.op("sc", lambda e: e.activation(g.row_s[:], rw[0:1, 0:512], AF.Exp,
                                      scale=-0.5),
         reads=["rw0"], writes=["row_s"])
    p.op("ve", lambda e: e.scalar_tensor_tensor(
        g.row_ms[:], g.row_s[:], 1.0 / D, g.PS[4][0:1, :],
        ALU.mult, ALU.mult),
        reads=["row_s", "ps4"], writes=["row_ms"])
    if rows_dram is not None:
        sl = slice(tt * TT, (tt + 1) * TT)
        p.op("sy", lambda e, sl=sl: e.dma_start(rows_dram[0:1, sl],
                                                g.row_s[:]),
             reads=["row_s"], writes=["rows_dram_st"], dma=True)
        p.op("sy", lambda e, sl=sl: e.dma_start(rows_dram[1:2, sl],
                                                g.row_ms[:]),
             reads=["row_ms"], writes=["rows_dram_st"], dma=True)


def bcast_rows(p, g):
    p.op("te", lambda e: e.matmul(g.PS[5][:, 0:512], g.onesf[:], g.row_s[:],
                                  start=True, stop=True),
         reads=["row_s", "onesf"], writes=["ps5"])
    p.op("te", lambda e: e.matmul(g.PS[6][:, 0:512], g.onesf[:], g.row_ms[:],
                                  start=True, stop=True),
         reads=["row_ms", "onesf"], writes=["ps6"])
    p.op("sc", lambda e: e.activation(g.sbc[:], g.PS[5][:, 0:512], AF.Copy),
         reads=["ps5"], writes=["sbc"])
    p.op("sc", lambda e: e.activation(g.msbc[:], g.PS[6][:, 0:512], AF.Copy),
         reads=["ps6"], writes=["msbc"])


def reduce_scalar_max(p, g, mx_ap, mx_key, id128f, out_dram):
    p.op("ve", lambda e: e.memset(g.cols[:], 0.0), writes=["colsb"])
    p.op("ve", lambda e: e.reduce_max(g.cols[:, 0:1], mx_ap, axis=AX.X),
         reads=[mx_key], writes=["colsb"], tail=True)
    p.op("te", lambda e: e.transpose(g.PS[0][0:96, 0:128], g.cols[:, 0:96],
                                     id128f[:]),
         reads=["colsb", "id128f"], writes=["ps0"])
    p.op("ve", lambda e: e.reduce_max(g.scal[0:1, 0:1], g.PS[0][0:1, 0:128],
                                      axis=AX.X),
         reads=["ps0"], writes=["gmax"], tail=True)
    p.op("sy", lambda e: e.dma_start(out_dram, g.scal[0:1, 0:1]),
         reads=["gmax"], writes=["lmax_st"], dma=True)


def stats_tile(p, g, xt, xb, sqb, kx, kb, composite, rows_dram, tt,
               ksq="sqr"):
    kxl = [kx] if isinstance(kx, str) else list(kx)
    p.op("sc", lambda e: e.activation(flat(sqb), flat(xt), AF.Square),
         reads=kxl, writes=[ksq])
    ln_stats_mm(p, g, kb, lambda k: xb[:, k, :], ksq,
                lambda k: sqb[:, k, :])
    ln_rows(p, g, composite, rows_dram, tt)


def xn_max_tile(p, g, xt, kx, gcol, bcol, xnb, mxc, tt):
    bcast_rows(p, g)
    for k in range(NCH):
        p.op("ve", lambda e, k=k: e.tensor_tensor(
            g.u1[:], xt[:, k, :], g.sbc[:], ALU.mult),
            reads=[(kx if isinstance(kx, str) else kx[k]), "sbc"],
            writes=["u1"])
        p.op("ve", lambda e: e.tensor_tensor(g.u2[:], g.u1[:], g.msbc[:],
                                             ALU.subtract),
             reads=["u1", "msbc"], writes=["u2"])
        p.op("sc", lambda e, k=k: e.activation(
            xnb[:], g.u2[:], AF.Identity, scale=gcol[:, k:k + 1],
            bias=bcol[:, k:k + 1]),
            reads=["u2", "gbcols"], writes=["xnb"])
        p.op("ve", lambda e, tt=tt, k=k: e.reduce_max(
            mxc[:, tt * NCH + k:tt * NCH + k + 1], xnb[:], axis=AX.X,
            apply_absolute_value=True),
            reads=["xnb"], writes=["mxc"], tail=True)


def emit_stats_pass(p, g, xsrc, rows_dram, gcol, bcol, xt, xb, sqb, xnb, mxc,
                    id128f, lmax_o, composite=True, src_key="x0d",
                    kx="fxt", kb="gxb", ksq="sqr"):
    p.op("ve", lambda e: e.memset(mxc[:], 0.0), writes=["mxc"])
    kxl = [f"{kx}{k}" for k in range(NCH)]
    for tt in range(NTILE):
        b, tau = divmod(tt, NTPS)
        sl = slice(tau * TT, (tau + 1) * TT)
        xv = xsrc[b].rearrange("(c p) t -> p c t", p=128)[:, :, sl]
        for k in range(NCH):
            p.op("sy", lambda e, xv=xv, k=k: e.dma_start(xt[:, k, :],
                                                         xv[:, k, :]),
                 reads=[src_key], writes=[kxl[k]], dma=True)
            p.op("gp", lambda e, xv=xv, k=k: e.dma_start(xb[:, k, :],
                                                         xv[:, k, :]),
                 reads=[src_key], writes=[kb], dma=True)
        stats_tile(p, g, xt, xb, sqb, kxl, kb, composite, rows_dram, tt,
                   ksq=ksq)
        xn_max_tile(p, g, xt, kxl, gcol, bcol, xnb, mxc, tt)
    reduce_scalar_max(p, g, mxc[:], "mxc", id128f, lmax_o)


def emit_quant_prologue(p, g, lmax_all, alpha_in, lm_sb, al_sb, invsc_col,
                        epi_col):
    p.op("sy", lambda e: e.dma_start(lm_sb[:], lmax_all),
         writes=["lm_sb"], dma=True)
    p.op("sy", lambda e: e.dma_start(al_sb[:], alpha_in),
         writes=["al_sb"], dma=True)
    p.op("ve", lambda e: e.reduce_max(g.scal[0:1, 8:9], lm_sb[:], axis=AX.X),
         reads=["lm_sb"], writes=["q0"], tail=True)
    p.op("ve", lambda e: e.tensor_scalar(g.scal[0:1, 9:10], g.scal[0:1, 8:9],
                                         1e-8, None, ALU.max),
         reads=["q0"], writes=["q1"])
    p.op("ve", lambda e: e.reciprocal(g.scal[0:1, 10:11], g.scal[0:1, 9:10]),
         reads=["q1"], writes=["q2"], tail=True, eager=True)
    p.op("ve", lambda e: e.tensor_scalar(g.scal[0:1, 11:12],
                                         g.scal[0:1, 10:11], 127.0, None,
                                         ALU.mult),
         reads=["q2"], writes=["q3"])
    p.op("ve", lambda e: e.scalar_tensor_tensor(
        g.scal[0:1, 12:13], g.scal[0:1, 9:10], 0.5 / 127.0, al_sb[0:1, 0:1],
        ALU.mult, ALU.mult), reads=["q1", "al_sb"], writes=["q4"])
    p.op("te", lambda e: e.matmul(g.PS[5][:, 0:2], g.onesf[:],
                                  g.scal[0:1, 11:13], start=True, stop=True),
         reads=["q3", "q4", "onesf"], writes=["ps5"])
    p.op("ve", lambda e: e.tensor_copy(invsc_col[:], g.PS[5][:, 0:1]),
         reads=["ps5"], writes=["invsc"])
    p.op("ve", lambda e: e.tensor_copy(epi_col[:], g.PS[5][:, 1:2]),
         reads=["ps5"], writes=["epi"])


def emit_ffnc(p, g, xsrc, xdst, rows_in, tern_sb, gcol, bcol, invsc_col,
              epi_col, xt, qb, with_stats=False, rows2_dram=None,
              xb=None, sqb=None, src_key="x0d", dst_key="x1d",
              kx="fxt", kq="xiq", kb="gxb", ksq="sqr"):
    for tt in range(NTILE):
        b, tau = divmod(tt, NTPS)
        sl = slice(tau * TT, (tau + 1) * TT)
        xv = xsrc[b].rearrange("(c p) t -> p c t", p=128)[:, :, sl]
        for k in range(NCH):
            p.op("sy", lambda e, xv=xv, k=k: e.dma_start(xt[:, k, :],
                                                         xv[:, k, :]),
                 reads=[src_key], writes=[f"{kx}{k}"], dma=True)
        p.op("sy", lambda e, tt=tt: e.dma_start(
            g.row_s[:], rows_in[0:1, tt * TT:(tt + 1) * TT]),
            writes=["row_s"], dma=True)
        p.op("sy", lambda e, tt=tt: e.dma_start(
            g.row_ms[:], rows_in[1:2, tt * TT:(tt + 1) * TT]),
            writes=["row_ms"], dma=True)
        bcast_rows(p, g)
        for k in range(NCH):
            p.op("ve", lambda e, k=k: e.tensor_tensor(
                g.u1[:], xt[:, k, :], g.sbc[:], ALU.mult),
                reads=[f"{kx}{k}", "sbc"], writes=["u1"])
            p.op("ve", lambda e: e.tensor_tensor(g.u2[:], g.u1[:], g.msbc[:],
                                                 ALU.subtract),
                 reads=["u1", "msbc"], writes=["u2"])
            p.op("sc", lambda e, k=k: e.activation(
                g.u1[:], g.u2[:], AF.Identity, scale=gcol[:, k:k + 1],
                bias=bcol[:, k:k + 1]),
                reads=["u2", "gbcols"], writes=["u1"])
            p.op("ve", lambda e: e.tensor_scalar(
                g.u2[:], g.u1[:], invsc_col[:, 0:1], MAGIC, ALU.mult,
                ALU.add),
                reads=["u1"], sreads=["invsc"], writes=["u2"])
            p.op("ve", lambda e, k=k: e.tensor_scalar(
                qb[:, k, :], g.u2[:], MAGIC, None, ALU.subtract),
                reads=["u2"], writes=[f"{kq}{k}"])
        for m in range(NCH):
            pb = g.PS[m % 2]
            pk = f"ps{m % 2}"
            for k in range(NCH):
                p.op("te", lambda e, m=m, k=k, pb=pb: e.matmul(
                    pb[:, 0:512], tern_sb[:, k, m * 128:(m + 1) * 128],
                    qb[:, k, :], start=(k == 0), stop=(k == NCH - 1)),
                    reads=[f"{kq}{k}", "tern_res"], writes=[pk])
            p.op("sc", lambda e, pb=pb: e.activation(
                g.u1[:], pb[:, 0:512], AF.Identity, scale=epi_col[:, 0:1]),
                reads=[pk, "epi"], writes=["u1"])
            p.op("ve", lambda e, m=m: e.tensor_tensor(
                xt[:, m, :], g.u1[:], xt[:, m, :], ALU.add),
                reads=["u1", f"{kx}{m}"], writes=[f"{kx}{m}"])
            p.op("sy", lambda e, b=b, sl=sl, m=m: e.dma_start(
                xdst[b].rearrange("(c p) t -> p c t",
                                  p=128)[:, m, sl], xt[:, m, :]),
                reads=[f"{kx}{m}"], writes=[dst_key], dma=True)
        if with_stats:
            p.op("ve", lambda e: e.tensor_copy(flat(xb), flat(xt)),
                 reads=[f"{kx}{k}" for k in range(NCH)], writes=[kb])
            stats_tile(p, g, xt, xb, sqb,
                       [f"{kx}{k}" for k in range(NCH)], kb, False,
                       rows2_dram, tt, ksq=ksq)


def build_L1():
    nc = bass.Bass("TRN2", target_bir_lowering=False, debug=False)
    g = G()
    x0 = nc.dram_tensor("x0", [B_LOC, D, T], F32, kind="ExternalInput").ap()
    wffnT = nc.dram_tensor("wffnT", [D, D], F32, kind="ExternalInput").ap()
    gpc = nc.dram_tensor("gammapc", [128, NCH], F32, kind="ExternalInput").ap()
    bpc = nc.dram_tensor("betapc", [128, NCH], F32, kind="ExternalInput").ap()
    identf = nc.dram_tensor("identf", [128, 128], F32, kind="ExternalInput").ap()

    tern_o = nc.dram_tensor("tern", [D, D], BF16, kind="ExternalOutput").ap()
    alpha_o = nc.dram_tensor("alpha", [1, 1], F32, kind="ExternalOutput").ap()
    rows_o = nc.dram_tensor("rows1", [2, NTILE * TT], F32,
                            kind="ExternalOutput").ap()
    lmax_o = nc.dram_tensor("lmax", [1, 1], F32, kind="ExternalOutput").ap()

    alloc_generic(nc, g)
    wf = nc.alloc_sbuf_tensor("wf", [128, NCH, D], F32)
    absw = nc.alloc_sbuf_tensor("absw", [128, NCH * D], F32)
    m01 = nc.alloc_sbuf_tensor("m01", [128, NCH * D], BF16)
    sgn = nc.alloc_sbuf_tensor("sgn", [128, NCH * D], BF16)
    tern = nc.alloc_sbuf_tensor("tern_sb", [128, NCH, D], BF16)
    id128f = nc.alloc_sbuf_tensor("id128f", [128, 128], F32)
    onecol = nc.alloc_sbuf_tensor("onecol", [128, 1], F32)
    gcol = nc.alloc_sbuf_tensor("gcol", [128, NCH], F32)
    bcol = nc.alloc_sbuf_tensor("bcol", [128, NCH], F32)
    xt = nc.alloc_sbuf_tensor("sxt", [128, NCH, TT], F32)
    xb = nc.alloc_sbuf_tensor("sxb", [128, NCH, TT], BF16)
    sqb = nc.alloc_sbuf_tensor("sqb", [128, NCH, TT], BF16)
    xnb = nc.alloc_sbuf_tensor("xnb", [128, TT], F32)
    mxc = nc.alloc_sbuf_tensor("mxc", [128, NCH * NTILE], F32)

    p = Prog(nc)
    init_consts(p, g)
    p.op("sy", lambda e: e.dma_start(
        wf[:], wffnT.rearrange("(c p) o -> p c o", p=128)),
        writes=["wf"], dma=True)
    p.op("sy", lambda e: e.dma_start(id128f[:], identf),
         writes=["id128f"], dma=True)
    p.op("gp", lambda e: e.dma_start(gcol[:], gpc), writes=["gbcols"], dma=True)
    p.op("gp", lambda e: e.dma_start(bcol[:], bpc), writes=["gbcols"], dma=True)

    wfv = flat(wf)
    p.op("sc", lambda e: e.activation(absw[:], wfv, AF.Abs),
         reads=["wf"], writes=["absw"])
    p.op("sc", lambda e: e.activation(sgn[:], wfv, AF.Sign),
         reads=["wf"], writes=["sgn"])
    p.op("ve", lambda e: e.memset(g.cols[:], 0.0), writes=["colsb"])
    p.op("ve", lambda e: e.reduce_sum(g.cols[:, 0:1], absw[:], axis=AX.X),
         reads=["absw"], writes=["colsb"], tail=True)
    p.op("te", lambda e: e.transpose(g.PS[0][0:96, 0:128], g.cols[:, 0:96],
                                     id128f[:]),
         reads=["colsb", "id128f"], writes=["ps0"])
    p.op("ve", lambda e: e.reduce_sum(g.scal[0:1, 0:1], g.PS[0][0:1, 0:128],
                                      axis=AX.X), reads=["ps0"], writes=["s0"], tail=True)
    p.op("ve", lambda e: e.tensor_scalar(g.scal[0:1, 1:2], g.scal[0:1, 0:1],
                                         0.7 / (D * D), None, ALU.mult),
         reads=["s0"], writes=["s1"])
    p.op("te", lambda e: e.matmul(g.PS[1][:, 0:1], g.onesf[:],
                                  g.scal[0:1, 1:2], start=True, stop=True),
         reads=["s1", "onesf"], writes=["ps1"])
    p.op("ve", lambda e: e.tensor_copy(onecol[:], g.PS[1][:, 0:1]),
         reads=["ps1"], writes=["onecol"])
    p.op("ve", lambda e: e.tensor_scalar(m01[:], absw[:], onecol[:, 0:1],
                                         None, ALU.is_gt),
         reads=["absw"], sreads=["onecol"], writes=["m01"])
    p.op("ve", lambda e: e.tensor_tensor(absw[:], absw[:], m01[:], ALU.mult),
         reads=["absw", "m01"], writes=["absw"])
    p.op("ve", lambda e: e.reduce_sum(g.cols[:, 32:33], absw[:], axis=AX.X),
         reads=["absw"], writes=["colsb"], tail=True)
    p.op("ve", lambda e: e.reduce_sum(g.cols[:, 64:65], m01[:], axis=AX.X),
         reads=["m01"], writes=["colsb"], tail=True)
    p.op("te", lambda e: e.transpose(g.PS[0][0:96, 0:128], g.cols[:, 0:96],
                                     id128f[:]),
         reads=["colsb", "id128f"], writes=["ps0"])
    p.op("ve", lambda e: e.reduce_sum(g.scal[0:1, 2:3], g.PS[0][32:33, 0:128],
                                      axis=AX.X), reads=["ps0"], writes=["s2"], tail=True)
    p.op("ve", lambda e: e.reduce_sum(g.scal[0:1, 3:4], g.PS[0][64:65, 0:128],
                                      axis=AX.X), reads=["ps0"], writes=["s3"], tail=True)
    p.op("ve", lambda e: e.tensor_scalar(g.scal[0:1, 4:5], g.scal[0:1, 3:4],
                                         1.0, None, ALU.max),
         reads=["s3"], writes=["s4"])
    p.op("ve", lambda e: e.reciprocal(g.scal[0:1, 5:6], g.scal[0:1, 4:5]),
         reads=["s4"], writes=["s5"], tail=True, eager=True)
    p.op("ve", lambda e: e.tensor_tensor(g.scal[0:1, 6:7], g.scal[0:1, 2:3],
                                         g.scal[0:1, 5:6], ALU.mult),
         reads=["s2", "s5"], writes=["s6"])
    p.op("ve", lambda e: e.tensor_tensor(flat(tern), sgn[:], m01[:],
                                         ALU.mult),
         reads=["sgn", "m01"], writes=["tern"])
    p.op("sy", lambda e: e.dma_start(
        tern_o.rearrange("(c p) o -> p c o", p=128), tern[:]),
        reads=["tern"], writes=["tern_st"], dma=True)
    p.op("sy", lambda e: e.dma_start(alpha_o, g.scal[0:1, 6:7]),
         reads=["s6"], writes=["alpha_st"], dma=True)

    emit_stats_pass(p, g, x0, rows_o, gcol, bcol, xt, xb, sqb, xnb, mxc,
                    id128f, lmax_o, composite=True,
                    kx="sxt", kb="sxb", ksq="sqb")
    p.finish()
    p.emit()
    return nc


def build_L3():
    nc = bass.Bass("TRN2", target_bir_lowering=False, debug=False)
    g = G()
    x2 = nc.dram_tensor("x2", [B_LOC, D, T], F32, kind="ExternalInput").ap()
    tern_i = nc.dram_tensor("tern", [D, D], BF16, kind="ExternalInput").ap()
    alpha_i = nc.dram_tensor("alpha", [1, 1], F32, kind="ExternalInput").ap()
    rows_i = nc.dram_tensor("rows3", [2, NTILE * TT], F32,
                            kind="ExternalInput").ap()
    lmax_all = nc.dram_tensor("lmax_all", [1, 8], F32,
                              kind="ExternalInput").ap()
    gpc = nc.dram_tensor("gammapc", [128, NCH], F32, kind="ExternalInput").ap()
    bpc = nc.dram_tensor("betapc", [128, NCH], F32, kind="ExternalInput").ap()
    out = nc.dram_tensor("out", [B_LOC, D, T], F32, kind="ExternalOutput").ap()

    alloc_generic(nc, g)
    tern_sb = nc.alloc_sbuf_tensor("tern_res", [128, NCH, D], BF16)
    gcol = nc.alloc_sbuf_tensor("gcol", [128, NCH], F32)
    bcol = nc.alloc_sbuf_tensor("bcol", [128, NCH], F32)
    lm_sb = nc.alloc_sbuf_tensor("lm_sb", [1, 8], F32)
    al_sb = nc.alloc_sbuf_tensor("al_sb", [1, 1], F32)
    invsc_col = nc.alloc_sbuf_tensor("invsc_col", [128, 1], F32)
    epi_col = nc.alloc_sbuf_tensor("epi_col", [128, 1], F32)
    xt = nc.alloc_sbuf_tensor("fxt", [128, NCH, TT], F32)
    qb = nc.alloc_sbuf_tensor("qb", [128, NCH, TT], BF16)

    p = Prog(nc)
    init_consts(p, g)
    p.op("gp", lambda e: e.dma_start(
        tern_sb[:], tern_i.rearrange("(c p) o -> p c o", p=128)),
        writes=["tern_res"], dma=True)
    p.op("gp", lambda e: e.dma_start(gcol[:], gpc), writes=["gbcols"], dma=True)
    p.op("gp", lambda e: e.dma_start(bcol[:], bpc), writes=["gbcols"], dma=True)
    emit_quant_prologue(p, g, lmax_all, alpha_i, lm_sb, al_sb, invsc_col,
                        epi_col)
    emit_ffnc(p, g, x2, out, rows_i, tern_sb, gcol, bcol, invsc_col, epi_col,
              xt, qb, src_key="x2d", dst_key="outd")
    p.finish()
    p.emit()
    return nc


def build_L2():
    nc = bass.Bass("TRN2", target_bir_lowering=False, debug=False)
    g = G()

    def AP(n, s, d=F32, kk="ExternalInput"):
        return nc.dram_tensor(n, s, d, kind=kk).ap()

    x0 = AP("x0", [B_LOC, D, T])
    tern_i = AP("tern", [D, D], BF16)
    alpha_i = AP("alpha", [1, 1])
    rows1 = AP("rows1", [2, NTILE * TT])
    lmax_all = AP("lmax_all", [1, 8])
    gpc, bpc = AP("gammapc", [128, NCH]), AP("betapc", [128, NCH])
    identf = AP("identf", [128, 128])
    ident64d = AP("ident64d", [128, 64])
    masku = AP("masku", [128, 64], U8)
    wx0T = AP("wx0T", [D, 160])
    wx2T = AP("wx2T", [160, D])
    wrT = AP("wrT", [D, 512])
    ww1T = AP("ww1T", [D, 64])
    ww2T = AP("ww2T", [64, 512])
    wkT = AP("wkT", [D, 512])
    wvT = AP("wvT", [D, D])
    wgT = AP("wgT", [D, D])
    woT = AP("woT", [D, D])
    mu0pc = AP("mu0pc", [128, NCH])
    xbiaspc = AP("xbiaspc", [128, 5, NCH])
    bw2pc = AP("bw2pc", [128, 4])
    bonusT = AP("bonusT", [128, 4])
    gnwrow = AP("gnwrow", [2, D])
    gncolw = AP("gncolw", [128, NCH])
    x2_o = AP("x2", [B_LOC, D, T], F32, "ExternalOutput")
    rows3_o = AP("rows3", [2, NTILE * TT], F32, "ExternalOutput")
    lmax2_o = AP("lmax2", [1, 1], F32, "ExternalOutput")
    x1d = nc.dram_tensor("x1scr", [B_LOC, D, T], F32, kind="Internal").ap()
    rows2d = nc.dram_tensor("rows2scr", [2, NTILE * TT], F32,
                            kind="Internal").ap()

    alloc_generic(nc, g)
    tern_sb = nc.alloc_sbuf_tensor("tern_res", [128, NCH, D], BF16)
    wx0 = nc.alloc_sbuf_tensor("wx0", [128, NCH, 160], BF16)
    wx2 = nc.alloc_sbuf_tensor("wx2", [128, 2, D], BF16)
    wr = nc.alloc_sbuf_tensor("wr", [128, NCH, 512], BF16)
    ww1 = nc.alloc_sbuf_tensor("ww1", [128, NCH, 64], BF16)
    ww2 = nc.alloc_sbuf_tensor("ww2", [64, 512], BF16)
    wk = nc.alloc_sbuf_tensor("wk", [128, NCH, 512], BF16)
    wv = nc.alloc_sbuf_tensor("wv", [128, NCH, D], BF16)
    wg = nc.alloc_sbuf_tensor("wg", [128, NCH, D], BF16)
    wo = nc.alloc_sbuf_tensor("wo", [128, NCH, D], BF16)
    id128f = nc.alloc_sbuf_tensor("id128f", [128, 128], F32)
    id128b = nc.alloc_sbuf_tensor("id128b", [128, 128], BF16)
    id64b = nc.alloc_sbuf_tensor("id64b", [128, 64], BF16)
    msku = nc.alloc_sbuf_tensor("msku", [128, 64], U8)
    gcol = nc.alloc_sbuf_tensor("gcol", [128, NCH], F32)
    bcol = nc.alloc_sbuf_tensor("bcol", [128, NCH], F32)
    mu0c = nc.alloc_sbuf_tensor("mu0c", [128, NCH], F32)
    xbc = nc.alloc_sbuf_tensor("xbc", [128, 5, NCH], F32)
    bw2c = nc.alloc_sbuf_tensor("bw2c", [128, 4], F32)
    ubon = nc.alloc_sbuf_tensor("ubon", [128, 4], BF16)
    gnwr = nc.alloc_sbuf_tensor("gnwr", [2, D], BF16)
    gncol = nc.alloc_sbuf_tensor("gncol", [128, NCH], F32)
    lm_sb = nc.alloc_sbuf_tensor("lm_sb", [1, 8], F32)
    al_sb = nc.alloc_sbuf_tensor("al_sb", [1, 1], F32)
    invsc_col = nc.alloc_sbuf_tensor("invsc_col", [128, 1], F32)
    epi_col = nc.alloc_sbuf_tensor("epi_col", [128, 1], F32)
    xt = nc.alloc_sbuf_tensor("fxt", [128, NCH, TT], F32)
    xiq = nc.alloc_sbuf_tensor("xiq", [128, NCH, TT], BF16)
    gxb = nc.alloc_sbuf_tensor("gxb", [128, NCH, TT], BF16)
    sqr = nc.alloc_sbuf_tensor("sqr", [128, NCH, TT], BF16)
    dob = nc.alloc_sbuf_tensor("dob", [128, NCH, TT], BF16)
    xnb = nc.alloc_sbuf_tensor("xnb", [128, TT], F32)
    mxc = nc.alloc_sbuf_tensor("mxc", [128, NCH * NTILE], F32)
    xln = nc.alloc_sbuf_tensor("xln", [128, NCH, TT + 1], BF16)
    xp = nc.alloc_sbuf_tensor("xp", [128, 2, TT], BF16)
    rb = nc.alloc_sbuf_tensor("rb", [128, NH, TT], BF16)
    kbuf = nc.alloc_sbuf_tensor("kbuf", [128, NH, TT], BF16)
    w1b = nc.alloc_sbuf_tensor("w1b", [64, TT], BF16)
    posf = nc.alloc_sbuf_tensor("posf", [128, NH, TT], F32)
    vtm = nc.alloc_sbuf_tensor("vtm", [128, NTPS, D], BF16)
    tmpb = nc.alloc_sbuf_tensor("tmpb", [128, TT], BF16)
    enb2 = [nc.alloc_sbuf_tensor(f"enb{i}", [128, TT + 1], F32)
            for i in range(2)]
    encar = nc.alloc_sbuf_tensor("encar", [128, NH], F32)
    scar = nc.alloc_sbuf_tensor("scar", [128, NH, DV], BF16)
    spp = [nc.alloc_sbuf_tensor(f"spp{i}", [128, DV], BF16) for i in range(2)]
    dcolb = nc.alloc_sbuf_tensor("dcolb", [128, 4], F32)
    fcall2 = [nc.alloc_sbuf_tensor(f"fcall{i}", [128, 24], F32)
              for i in range(2)]
    grow = nc.alloc_sbuf_tensor("grow", [2, TT], BF16)
    rab2 = xp[:, 0, :]
    ksb2 = xp[:, 1, :]
    ktbq = tmpb
    u2bf = g.u2.bitcast(BF16)
    apb = u2bf[:, 0:256]
    dtb = u2bf[:, 256:512]

    p = Prog(nc)
    init_consts(p, g)
    p.op("ve", lambda e: e.memset(grow[:], -1.0), writes=["grow_c"])

    def gload(dst, src, key):
        p.op("gp", lambda e, dst=dst, src=src: e.dma_start(dst, src),
             writes=[key], dma=True)

    gload(tern_sb[:], tern_i.rearrange("(c p) o -> p c o", p=128), "tern_res")
    gload(wx0[:], wx0T.rearrange("(c p) o -> p c o", p=128), "wx0")
    gload(wx2[0:96, 0, :], wx2T[0:96, :], "wx2")
    gload(wx2[0:64, 1, :], wx2T[96:160, :], "wx2")
    gload(wr[:], wrT.rearrange("(c p) o -> p c o", p=128), "wr")
    gload(ww1[:], ww1T.rearrange("(c p) o -> p c o", p=128), "ww1")
    gload(ww2[:], ww2T, "ww2")
    gload(wk[:], wkT.rearrange("(c p) o -> p c o", p=128), "wk")
    gload(wv[:], wvT.rearrange("(c p) o -> p c o", p=128), "wv")
    gload(wg[:], wgT.rearrange("(c p) o -> p c o", p=128), "wg")
    gload(wo[:], woT.rearrange("(c p) o -> p c o", p=128), "wo")
    gload(id128b[:], identf, "id128b")
    gload(id64b[:], ident64d, "id64b")
    gload(ubon[:], bonusT, "ubon")
    gload(gnwr[:], gnwrow, "gnwr")
    p.op("sy", lambda e: e.dma_start(gncol[:], gncolw), writes=["gncol"],
         dma=True)
    p.op("sy", lambda e: e.dma_start(id128f[:], identf), writes=["id128f"],
         dma=True)
    p.op("sy", lambda e: e.dma_start(msku[:], masku), writes=["msku"],
         dma=True)
    p.op("sy", lambda e: e.dma_start(gcol[:], gpc), writes=["gbcols"],
         dma=True)
    p.op("sy", lambda e: e.dma_start(bcol[:], bpc), writes=["gbcols"],
         dma=True)
    p.op("sy", lambda e: e.dma_start(mu0c[:], mu0pc), writes=["mu0c"],
         dma=True)
    p.op("sy", lambda e: e.dma_start(xbc[:], xbiaspc), writes=["xbc"],
         dma=True)
    p.op("sy", lambda e: e.dma_start(bw2c[:], bw2pc), writes=["bw2c"],
         dma=True)

    p.op("ve", lambda e: e.memset(mxc[:], 0.0), writes=["mxc"])
    emit_quant_prologue(p, g, lmax_all, alpha_i, lm_sb, al_sb, invsc_col,
                        epi_col)
    emit_ffnc(p, g, x0, x1d, rows1, tern_sb, gcol, bcol, invsc_col, epi_col,
              xt, xiq, with_stats=True, rows2_dram=rows2d,
              xb=gxb, sqb=sqr, src_key="x0d", dst_key="x1d")

    delta = dob
    xi = xiq
    gsb = gxb
    ob = dob
    rkb = sqr
    xt2 = tern_sb.bitcast(F32)

    def xbuf(t):
        return (xt, "fxt", ["fxt"]) if t % 2 == 0 else                (xt2, "fxt1", ["fxt1", "tern_res"])

    def xload(t):
        bb, uu = divmod(t, NTPS)
        ssl = slice(uu * TT, (uu + 1) * TT)
        xv = x1d[bb].rearrange("(c p) t -> p c t", p=128)[:, :, ssl]
        dst, _, wk = xbuf(t)
        p.op("sy", lambda e, xv=xv, dst=dst: e.dma_start(dst[:], xv),
             reads=["x1d"], writes=wk, dma=True)

    xload(0)
    for tt in range(NTILE):
        b, tau = divmod(tt, NTPS)
        sl = slice(tau * TT, (tau + 1) * TT)
        xcur, kx, _ = xbuf(tt)
        if tt + 1 < NTILE:
            xload(tt + 1)
        p.op("sy", lambda e, tt=tt: e.dma_start(
            g.row_s[:], rows2d[0:1, tt * TT:(tt + 1) * TT]),
            reads=["rows_dram_st"], writes=["row_s"], dma=True)
        p.op("sy", lambda e, tt=tt: e.dma_start(
            g.row_ms[:], rows2d[1:2, tt * TT:(tt + 1) * TT]),
            reads=["rows_dram_st"], writes=["row_ms"], dma=True)
        bcast_rows(p, g)
        if tau == 0:
            p.op("ve", lambda e: e.memset(xln[:, :, 0:1], 0.0),
                 reads=["xln"], writes=["xlncar"])
            p.op("ve", lambda e: e.memset(flat(scar), 0.0),
                 reads=[f"scar{i}" for i in range(NH)],
                 writes=[f"scar{i}" for i in range(NH)])
            p.op("ve", lambda e: e.memset(encar[:], 0.0),
                 reads=["encar0", "encar1"], writes=["encar0", "encar1"])
        else:
            p.op("ve", lambda e: e.tensor_copy(xln[:, :, 0:1],
                                               xln[:, :, TT:TT + 1]),
                 reads=["xln"], writes=["xlncar"])
        for k in range(NCH):
            p.op("ve", lambda e, k=k, xcur=xcur: e.tensor_tensor(
                g.u1[:], xcur[:, k, :], g.sbc[:], ALU.mult),
                reads=[kx, "sbc"], writes=["u1"])
            p.op("ve", lambda e, k=k: e.tensor_tensor(
                xln[:, k, 1:TT + 1], g.u1[:], g.msbc[:], ALU.subtract),
                reads=["u1", "msbc"], writes=["xln"])
        p.op("ve", lambda e: e.tensor_tensor(
            delta[:], xln[:, :, 0:TT], xln[:, :, 1:TT + 1], ALU.subtract),
            reads=["xln", "xlncar"], writes=[f"dob{i}" for i in range(NCH)])
        for k in range(NCH):
            p.op("ve", lambda e, k=k: e.scalar_tensor_tensor(
                xi[:, k, :], delta[:, k, :], mu0c[:, k:k + 1],
                xln[:, k, 1:TT + 1], ALU.mult, ALU.add),
                reads=[f"dob{k}", "mu0c", "xln"], writes=[f"xiq{k}"])
        for mg, (mlo, mhi) in enumerate([(0, 96), (96, 160)]):
            pb, pk = g.PS[mg], f"ps{mg}"
            for k in range(NCH):
                p.op("te", lambda e, k=k, mlo=mlo, mhi=mhi, pb=pb: e.matmul(
                    pb[0:mhi - mlo, 0:512], wx0[:, k, mlo:mhi], xi[:, k, :],
                    start=(k == 0), stop=(k == NCH - 1)),
                    reads=[f"xiq{k}", "wx0"], writes=[pk])
            p.op("sc", lambda e, mg=mg, mlo=mlo, mhi=mhi, pb=pb: e.activation(
                xp[0:mhi - mlo, mg, :], pb[0:mhi - mlo, 0:512], AF.Tanh),
                reads=[pk], writes=["xp", "xpr", "xpk"])
        tmpb2 = xnb.bitcast(BF16)[:, 0:512]

        def emit_mu1(n, m):
            xpc, xpo = ((0, n * 32) if n < 3 else (1, (n - 3) * 32))
            pb, pk = g.PS[2 + m % 4], f"ps{2 + m % 4}"
            p.op("te", lambda e, m=m, xpc=xpc, xpo=xpo, pb=pb: e.matmul(
                pb[:, 0:512],
                wx2[xpo:xpo + 32, xpc, m * 128:(m + 1) * 128],
                xp[xpo:xpo + 32, xpc, :], start=True, stop=True),
                reads=["xp", "wx2"], writes=[pk])

        def emit_wave(n):
            for m in range(NCH):
                pb, pk = g.PS[2 + m % 4], f"ps{2 + m % 4}"
                t_b, tk = ((tmpb[:], "tmpb") if m % 2 == 0 else
                           (tmpb2, "xnb"))
                p.op("ve", lambda e, n=n, m=m, pb=pb, t_b=t_b:
                     e.scalar_tensor_tensor(
                         t_b, pb[:, 0:512], xbc[:, n, m:m + 1],
                         delta[:, m, :], ALU.add, ALU.mult),
                     reads=[pk, f"dob{m}"], sreads=["xbc"],
                     writes=[tk])
                p.op("gp" if m % 2 == 0 else "ve",
                     lambda e, m=m, t_b=t_b: e.tensor_tensor(
                         xi[:, m, :], t_b, xln[:, m, 1:TT + 1], ALU.add),
                     reads=[tk, "xln"], writes=[f"xiq{m}"])
                if m < 4:
                    emit_mu1(n, m + 4)
                elif n < 4:
                    emit_mu1(n + 1, m - 4)

        for m in range(4):
            emit_mu1(0, m)
        for n in range(5):
            emit_wave(n)
            if n in (0, 2):
                dst, dk = (rb, "rb") if n == 0 else (kbuf, "kbuf")
                wt, wkey = (wr, "wr") if n == 0 else (wk, "wk")
                for m in range(NH):
                    pb, pk = g.PS[m % 2], f"ps{m % 2}"
                    for k in range(NCH):
                        p.op("te", lambda e, m=m, k=k, pb=pb, wt=wt: e.matmul(
                            pb[:, 0:512], wt[:, k, m * 128:(m + 1) * 128],
                            xi[:, k, :], start=(k == 0), stop=(k == NCH - 1)),
                            reads=[f"xiq{k}", wkey], writes=[pk])
                    p.op("sc", lambda e, m=m, pb=pb, dst=dst, dk=dk:
                         e.activation(dst[:, m, :], pb[:, 0:512], AF.Copy),
                         reads=[pk], writes=[f"{dk}{m}"])
            elif n == 1:
                for k in range(NCH):
                    p.op("te", lambda e, k=k: e.matmul(
                        g.PS[0][0:64, 0:512], ww1[:, k, :], xi[:, k, :],
                        start=(k == 0), stop=(k == NCH - 1)),
                        reads=[f"xiq{k}", "ww1"], writes=["ps0"])
                p.op("sc", lambda e: e.activation(
                    w1b[:], g.PS[0][0:64, 0:512], AF.Tanh),
                    reads=["ps0"], writes=["w1b"])
                for m in range(NH):
                    pb, pk = g.PS[m % 2], f"ps{m % 2}"
                    p.op("te", lambda e, m=m, pb=pb: e.matmul(
                        pb[:, 0:512], ww2[:, m * 128:(m + 1) * 128], w1b[:],
                        start=True, stop=True),
                        reads=["w1b", "ww2"], writes=[pk])
                    p.op("sc", lambda e, m=m, pb=pb: e.activation(
                        posf[:, m, :], pb[:, 0:512], AF.Exp,
                        bias=bw2c[:, m:m + 1]),
                        reads=[pk, "bw2c"], writes=[f"posf{m}"])
            elif n == 3:
                for ts in range(4):
                    for nv in range(2):
                        pb, pk = g.PS[nv], f"ps{nv}"
                        for k in range(NCH):
                            p.op("te", lambda e, ts=ts, nv=nv, k=k, pb=pb:
                                 e.matmul(pb[:, 0:512],
                                          xi[:, k, ts * 128:(ts + 1) * 128],
                                          wv[:, k, nv * 512:(nv + 1) * 512],
                                          start=(k == 0),
                                          stop=(k == NCH - 1)),
                                 reads=[f"xiq{k}", "wv"], writes=[pk])
                        p.op("sc", lambda e, ts=ts, nv=nv, pb=pb: e.activation(
                            vtm[:, ts, nv * 512:(nv + 1) * 512],
                            pb[:, 0:512], AF.Copy),
                            reads=[pk], writes=["vtm"])
            else:
                for m in range(NCH):
                    pb, pk = g.PS[m % 2], f"ps{m % 2}"
                    for k in range(NCH):
                        p.op("te", lambda e, m=m, k=k, pb=pb: e.matmul(
                            pb[:, 0:512], wg[:, k, m * 128:(m + 1) * 128],
                            xi[:, k, :], start=(k == 0), stop=(k == NCH - 1)),
                            reads=[f"xiq{k}", "wg"], writes=[pk])
                    p.op("sc", lambda e, m=m, pb=pb: e.activation(
                        gsb[:, m, :], pb[:, 0:512], AF.Silu),
                        reads=[pk], writes=["gxb"])
        p.op("gp", lambda e: e.tensor_tensor(
            rkb[:, 0:NH, :].rearrange("p a b -> p (a b)"),
            flat(rb), flat(kbuf), ALU.mult),
            reads=[f"rb{i}" for i in range(NH)]
                  + [f"kbuf{i}" for i in range(NH)], writes=["sqr"])

        def scan_factors(h):
            hp = h % 2
            en, ek = enb2[hp], f"enb{hp}"
            eck = f"encar{hp}"
            fcall, fck, frk = fcall2[hp], f"fcall{hp}", f"fcraw{hp}"
            if tau == 0:
                p.op("ve", lambda e, en=en: e.memset(en[:, 0:1], 0.0),
                     reads=[ek], writes=[ek])
            else:
                p.op("ve", lambda e, h=h, en=en: e.tensor_copy(
                    en[:, 0:1], encar[:, h:h + 1]),
                    reads=[eck, ek], writes=[ek])
            p.op("ve", lambda e, h=h, tau=tau, en=en: e.tensor_tensor_scan(
                en[:, 1:TT + 1], posf[:, h, :], g.zeros[:],
                (encar[:, h:h + 1] if tau > 0 else 0.0), ALU.add, ALU.add),
                reads=[f"posf{h}", "zeros", eck, ek], writes=[ek], tail=True)
            p.op("ve", lambda e, h=h, en=en: e.tensor_copy(
                encar[:, h:h + 1], en[:, TT:TT + 1]),
                reads=[ek], writes=[eck])
            p.op("ve", lambda e, en=en: e.tensor_tensor(
                xnb[:], en[:, 0:512],
                en[:, 0:512:64].broadcast_to((128, 8, 64)), ALU.subtract),
                reads=[ek], writes=["xnb"])
            p.op("ve", lambda e, en=en, fcall=fcall: e.tensor_tensor(
                fcall[:, 0:8], en[:, 32:513:64], en[:, 0:512:64],
                ALU.subtract), reads=[ek], writes=[frk])
            p.op("ve", lambda e, en=en, fcall=fcall: e.tensor_tensor(
                fcall[:, 8:16], en[:, 32:513:64], en[:, 64:513:64],
                ALU.subtract), reads=[ek], writes=[frk])
            p.op("ve", lambda e, en=en, fcall=fcall: e.tensor_tensor(
                fcall[:, 16:24], en[:, 0:512:64], en[:, 64:513:64],
                ALU.subtract), reads=[ek], writes=[frk])
            p.op("sc", lambda e, fcall=fcall: e.activation(
                fcall[:, 0:24], fcall[:, 0:24], AF.Exp),
                reads=[frk], writes=[fck])
            p.op("sc", lambda e, h=h: e.activation(
                posf[:, h, :], xnb[:], AF.Exp, scale=-1.0),
                reads=["xnb"], writes=[f"posf{h}"])
            p.op("ve", lambda e, h=h: e.tensor_tensor(
                rb[:, h, :], rb[:, h, :], posf[:, h, :], ALU.mult),
                reads=[f"rb{h}", f"posf{h}"], writes=[f"rb{h}"])
            p.op("ve", lambda e, h=h, fcall=fcall: e.tensor_tensor(
                rab2, rb[:, h, :],
                fcall[:, 0:8].broadcast_to((128, 8, 64)), ALU.mult),
                reads=[f"rb{h}", fck], writes=["xpr"])
            p.op("ve", lambda e, en=en: e.tensor_tensor(
                xnb[:], en[:, 1:513],
                en[:, 32:513:64].broadcast_to((128, 8, 64)), ALU.subtract),
                reads=[ek, "xnb"], writes=["xnb"])
            p.op("sc", lambda e, h=h: e.activation(
                posf[:, h, :], xnb[:], AF.Exp),
                reads=["xnb", f"posf{h}"], writes=[f"posf{h}"])
            p.op("ve", lambda e, h=h: e.tensor_tensor(
                kbuf[:, h, :], kbuf[:, h, :], posf[:, h, :], ALU.mult),
                reads=[f"kbuf{h}", f"posf{h}"], writes=[f"kbuf{h}"])
            p.op("ve", lambda e, h=h, fcall=fcall: e.tensor_tensor(
                ksb2, kbuf[:, h, :],
                fcall[:, 8:16].broadcast_to((128, 8, 64)), ALU.mult),
                reads=[f"kbuf{h}", fck], writes=["xpk"])

        def scan_pre(h):
            for c in range(NCT):
                t0 = c * C
                pb_ = 64 * (c % 2)
                q = c // 2
                p.op("te", lambda e, t0=t0, pb_=pb_, q=q: e.transpose(
                    g.PS_KT[pb_:pb_ + 64, 128 * q:128 * q + 128],
                    ksb2[:, t0:t0 + C], id128b[:]),
                    reads=["xpk", "id128b"], writes=[f"pskt{c % 2}"])
            for pp in range(2):
                pb_ = 64 * pp
                p.op("sc", lambda e, pb_=pb_: e.activation(
                    ktbq[pb_:pb_ + 64, :], g.PS_KT[pb_:pb_ + 64, :],
                    AF.Copy),
                    reads=[f"pskt{pp}"], writes=["tmpb"])
            for c in range(NCT):
                t0 = c * C
                pb_ = 64 * (c % 2)
                q = c // 2
                bank, bk = g.PS[2 + c % 2], f"ps{2 + c % 2}"
                p.op("te", lambda e, h=h, t0=t0, pb_=pb_, q=q, bank=bank:
                     e.matmul(bank[pb_:pb_ + 64, 64 * q:64 * q + 64],
                              kbuf[:, h, t0:t0 + C], rab2[:, t0:t0 + C],
                              start=True, stop=True),
                     reads=[f"kbuf{h}", "xpr"], writes=[bk])
                p.op("te", lambda e, h=h, t0=t0, pb_=pb_, q=q, bank=bank:
                     e.matmul(bank[pb_:pb_ + 64, 384 + q:385 + q],
                              rkb[:, h, t0:t0 + C], ubon[:, h:h + 1],
                              start=True, stop=True),
                     reads=["sqr", "ubon"], writes=[bk])
            for pp in range(2):
                pb_ = 64 * pp
                bank, bk = g.PS[2 + pp], f"ps{2 + pp}"
                p.op("ve", lambda e, pb_=pb_, bank=bank: e.tensor_copy(
                    dcolb[pb_:pb_ + 64, 0:4], bank[pb_:pb_ + 64, 384:388]),
                    reads=[bk], writes=[f"dcolb{pp}"])
                for q in range(4):
                    p.op("ve", lambda e, pb_=pb_, q=q: e.tensor_scalar(
                        dtb[pb_:pb_ + 64, 64 * q:64 * q + 64],
                        id64b[pb_:pb_ + 64, :], dcolb[pb_:pb_ + 64, q:q + 1],
                        None, ALU.mult),
                        reads=["id64b"], sreads=[f"dcolb{pp}"],
                        writes=["u2"])
                p.op("ve", lambda e, pb_=pb_, bank=bank: e.select(
                    apb[pb_:pb_ + 64, 0:256],
                    msku[pb_:pb_ + 64, :][:, None, :].broadcast_to(
                        (64, 4, 64)),
                    bank[pb_:pb_ + 64, 0:256], dtb[pb_:pb_ + 64, 0:256]),
                    reads=[bk, "msku", "u2"], writes=["u2"],
                    eager=True)

        def scan_state(h):
            fcall, fck = fcall2[h % 2], f"fcall{h % 2}"

            def m_mm(c):
                pb_ = 64 * (c % 2)
                q = c // 2
                sl_ = 256 * ((c // 2) % 2)
                mbank, mk = g.PS[4 + c % 2], f"ps{4 + c % 2}"
                p.op("te", lambda e, h=h, q=q, pb_=pb_, sl_=sl_,
                     mbank=mbank: e.matmul(
                         mbank[:, sl_:sl_ + 256],
                         ktbq[pb_:pb_ + 64, 128 * q:128 * q + 128],
                         vtm[pb_:pb_ + 64, q, h * 256:(h + 1) * 256],
                         start=True, stop=True),
                     reads=["tmpb", "vtm"], writes=[mk])

            for c in range(4):
                m_mm(c)
            for c in range(NCT):
                t0 = c * C
                pb_ = 64 * (c % 2)
                q = c // 2
                st = c // 2
                sl_ = 256 * ((c // 2) % 2)
                bank, bk = g.PS[2 + c % 2], f"ps{2 + c % 2}"
                mbank, mk = g.PS[4 + c % 2], f"ps{4 + c % 2}"
                sprev_key = f"scar{h}" if c == 0 else f"spp{(c - 1) % 2}"
                sprev = (scar[:, h, :] if c == 0 else spp[(c - 1) % 2][:])
                sout_key = f"scar{h}" if c == NCT - 1 else f"spp{c % 2}"
                sout = (scar[:, h, :] if c == NCT - 1 else spp[c % 2][:])
                p.op("ve", lambda e, sprev=sprev, sout=sout, c=c, sl_=sl_,
                     mbank=mbank, fcall=fcall:
                     e.scalar_tensor_tensor(sout, sprev,
                                            fcall[:, 16 + c:17 + c],
                                            mbank[:, sl_:sl_ + 256],
                                            ALU.mult, ALU.add),
                     reads=[sprev_key, mk], sreads=[fck],
                     writes=[sout_key])
                if c + 4 < NCT:
                    m_mm(c + 4)
                for dvc in range(2):
                    p.op("te", lambda e, h=h, st=st, pb_=pb_, q=q, dvc=dvc,
                         bank=bank: e.matmul(
                             bank[:, 256 + dvc * 64:256 + dvc * 64 + 64],
                             vtm[pb_:pb_ + 64, st,
                                 h * 256 + dvc * 128:h * 256 + (dvc + 1) * 128],
                             apb[pb_:pb_ + 64, 64 * q:64 * q + 64],
                             start=True, stop=False),
                         reads=["vtm", "u2"], writes=[bk])
                    p.op("te", lambda e, sprev=sprev, dvc=dvc, h=h, t0=t0,
                         bank=bank: e.matmul(
                             bank[:, 256 + dvc * 64:256 + dvc * 64 + 64],
                             sprev[:, dvc * 128:(dvc + 1) * 128],
                             rb[:, h, t0:t0 + C], start=False, stop=True),
                         reads=[sprev_key, f"rb{h}"], writes=[bk])
                p.op("sc", lambda e, h=h, t0=t0, bank=bank: e.activation(
                    ob[:, 2 * h:2 * h + 2, t0:t0 + C],
                    bank[:, 256:384], AF.Copy),
                    reads=[bk], writes=[f"dob{2 * h}", f"dob{2 * h + 1}"])

        scan_factors(0)
        for h in range(NH):
            scan_pre(h)
            if h + 1 < NH:
                scan_factors(h + 1)
            scan_state(h)

        p.op("sc", lambda e: e.activation(flat(sqr), flat(ob), AF.Square),
             reads=[f"dob{i}" for i in range(NCH)], writes=["sqr"])
        def gn_banks(h):
            pbS, pkS = ((g.PS[5], "ps5") if h % 2 == 0 else (g.PS[2], "ps2"))
            pbQ, pkQ = ((g.PS[6], "ps6") if h % 2 == 0 else (g.PS[3], "ps3"))
            return pbS, pkS, pbQ, pkQ

        def gn_stats(h):
            pbS, pkS, pbQ, pkQ = gn_banks(h)
            for (src_t, key, pb, pk) in ((ob, "dob", pbS, pkS),
                                         (sqr, "sqr", pbQ, pkQ)):
                for cc in range(2):
                    p.op("te", lambda e, h=h, cc=cc, src_t=src_t, pb=pb,
                         key=key, pk=pk: e.matmul(
                             pb[0:1, 0:512], g.ones_b[:],
                             src_t[:, 2 * h + cc, :],
                             start=(cc == 0), stop=(cc == 1)),
                         reads=[(f"dob{2 * h + cc}" if key == "dob"
                                 else key), "ones_b"], writes=[pk])

        gn_stats(0)
        for h in range(NH):
            pbS, pkS, pbQ, pkQ = gn_banks(h)
            if h + 1 < NH:
                gn_stats(h + 1)
            rw = g.rowwork
            p.op("sc", lambda e, pbS=pbS: e.activation(
                rw[0:1, 0:512], pbS[0:1, 0:512], AF.Square,
                scale=1.0 / DV), reads=[pkS], writes=["rw0"])
            p.op("ve", lambda e, pbQ=pbQ: e.tensor_scalar(
                rw[0:1, 512:1024], pbQ[0:1, 0:512],
                1.0 / DV, EPS, ALU.mult, ALU.add),
                reads=[pkQ], writes=["rw1"])
            p.op("ve", lambda e: e.scalar_tensor_tensor(
                rw[0:1, 1024:1536], rw[0:1, 0:512], -1.0, rw[0:1, 512:1024],
                ALU.mult, ALU.add), reads=["rw0", "rw1"], writes=["rw2"])
            p.op("sc", lambda e: e.activation(rw[0:1, 0:512],
                                              rw[0:1, 1024:1536], AF.Ln),
                 reads=["rw2"], writes=["rw0"])
            p.op("sc", lambda e: e.activation(g.row_s[:], rw[0:1, 0:512],
                                              AF.Exp, scale=-0.5),
                 reads=["rw0"], writes=["row_s"])
            p.op("ve", lambda e, pbS=pbS: e.scalar_tensor_tensor(
                g.row_ms[:], g.row_s[:], 1.0 / DV, pbS[0:1, 0:512],
                ALU.mult, ALU.mult),
                reads=["row_s", pkS], writes=["row_ms"])
            p.op("ve", lambda e: e.tensor_copy(grow[0:1, :], g.row_ms[:]),
                 reads=["row_ms"], writes=["grow"])
            p.op("te", lambda e: e.matmul(
                g.PS[0][:, 0:512], g.onesf[:], g.row_s[:],
                start=True, stop=True),
                reads=["onesf", "row_s"], writes=["ps0"])
            for dvc in range(2):
                ch = 2 * h + dvc
                pbB, pkB = ((g.PS[1], "ps1") if dvc == 0 else
                            (g.PS[4], "ps4"))
                t_b, tk = ((tmpb[:], "tmpb") if dvc == 0 else
                           (posf[:, 2, :], "posf2"))
                x_b, xk = ((xnb[:], "xnb") if dvc == 0 else
                           (posf[:, 3, :], "posf3"))
                p.op("te", lambda e, ch=ch, pbB=pbB: e.matmul(
                    pbB[:, 0:512], gnwr[0:2, ch * 128:(ch + 1) * 128],
                    grow[:], start=True, stop=True),
                    reads=["gnwr", "grow", "grow_c"], writes=[pkB])
                p.op("ve", lambda e, ch=ch, t_b=t_b: e.tensor_tensor(
                    t_b, ob[:, ch, :], g.PS[0][:, 0:512], ALU.mult),
                    reads=[f"dob{ch}", "ps0"], writes=[tk])
                p.op("ve", lambda e, ch=ch, pbB=pbB, t_b=t_b, x_b=x_b:
                     e.scalar_tensor_tensor(
                         x_b, t_b, gncol[:, ch:ch + 1], pbB[:, 0:512],
                         ALU.mult, ALU.subtract),
                    reads=[tk, pkB], sreads=["gncol"], writes=[xk])
                p.op("gp", lambda e, ch=ch, x_b=x_b: e.tensor_tensor(
                    ob[:, ch, :], x_b, gsb[:, ch, :], ALU.mult),
                    reads=[xk, "gxb"], writes=[f"dob{ch}"])
        for m in range(NCH):
            pb, pk = g.PS[m % 2], f"ps{m % 2}"
            for k in range(NCH):
                p.op("te", lambda e, m=m, k=k, pb=pb: e.matmul(
                    pb[:, 0:512], wo[:, k, m * 128:(m + 1) * 128],
                    ob[:, k, :], start=(k == 0), stop=(k == NCH - 1)),
                    reads=[f"dob{k}", "wo"], writes=[pk])
            p.op("ve", lambda e, m=m, pb=pb, xcur=xcur: e.tensor_tensor(
                xcur[:, m, :], pb[:, 0:512], xcur[:, m, :], ALU.add),
                reads=[pk, kx], writes=[kx])
        p.op("sy", lambda e, b=b, sl=sl, xcur=xcur: e.dma_start(
            x2_o[b].rearrange("(c p) t -> p c t", p=128)[:, :, sl],
            xcur[:]),
            reads=[kx], writes=["x2d"], dma=True)
        ep_u1, ep_u2 = posf[:, 0, :], posf[:, 1, :]
        ep_sb, ep_msb = posf[:, 2, :], posf[:, 3, :]
        p.op("sc", lambda e, xcur=xcur: e.activation(flat(gxb), flat(xcur),
                                                     AF.Copy),
             reads=[kx], writes=["gxb"])
        p.op("sc", lambda e, xcur=xcur: e.activation(flat(sqr), flat(xcur),
                                                     AF.Square),
             reads=[kx], writes=["sqr"])
        ln_stats_mm(p, g, "gxb", lambda k: gxb[:, k, :], "sqr",
                    lambda k: sqr[:, k, :])
        ln_rows(p, g, True, rows3_o, tt)
        p.op("te", lambda e: e.matmul(g.PS[5][:, 0:512], g.onesf[:],
                                      g.row_s[:], start=True, stop=True),
             reads=["row_s", "onesf"], writes=["ps5"])
        p.op("te", lambda e: e.matmul(g.PS[6][:, 0:512], g.onesf[:],
                                      g.row_ms[:], start=True, stop=True),
             reads=["row_ms", "onesf"], writes=["ps6"])
        p.op("sc", lambda e, ep_sb=ep_sb: e.activation(
            ep_sb, g.PS[5][:, 0:512], AF.Copy),
            reads=["ps5"], writes=["posf2"])
        p.op("sc", lambda e, ep_msb=ep_msb: e.activation(
            ep_msb, g.PS[6][:, 0:512], AF.Copy),
            reads=["ps6"], writes=["posf3"])
        for k in range(NCH):
            p.op("ve", lambda e, k=k, xcur=xcur, ep_u1=ep_u1, ep_sb=ep_sb:
                 e.tensor_tensor(ep_u1, xcur[:, k, :], ep_sb, ALU.mult),
                 reads=[kx, "posf2"], writes=["posf0"])
            p.op("ve", lambda e, ep_u1=ep_u1, ep_u2=ep_u2, ep_msb=ep_msb:
                 e.tensor_tensor(ep_u2, ep_u1, ep_msb, ALU.subtract),
                 reads=["posf0", "posf3"], writes=["posf1"])
            p.op("sc", lambda e, k=k, ep_u2=ep_u2: e.activation(
                xnb[:], ep_u2, AF.Identity, scale=gcol[:, k:k + 1],
                bias=bcol[:, k:k + 1]),
                reads=["posf1", "gbcols"], writes=["xnb"])
            p.op("ve", lambda e, tt=tt, k=k: e.reduce_max(
                mxc[:, tt * NCH + k:tt * NCH + k + 1], xnb[:], axis=AX.X,
                apply_absolute_value=True),
                reads=["xnb"], writes=["mxc"], tail=True)

    reduce_scalar_max(p, g, mxc[:], "mxc", id128f, lmax2_o)
    p.finish()
    p.emit()
    return nc


_NC_CACHE = {}


def _get_nc(which):
    if which not in _NC_CACHE:
        _NC_CACHE[which] = {"L1": build_L1, "L2": build_L2, "L3": build_L3}[which]()
    return _NC_CACHE[which]


def _pc(v, nch=NCH):
    return np.ascontiguousarray(np.asarray(v, np.float32).reshape(nch, 128).T)


def kernel(L0_attn_hidden_states, L0_attn_mu0, L0_attn_W_x0, L0_attn_W_x2,
           L0_attn_x_bias, L0_attn_W_r, L0_attn_W_w1, L0_attn_W_w2,
           L0_attn_b_w2, L0_attn_W_k, L0_attn_W_v, L0_attn_W_g,
           L0_attn_bonus, L0_attn_gn_weight, L0_attn_gn_bias, L0_attn_W_o,
           L0_ffn_weight, L0_ffn_gamma, L0_ffn_beta):
    f32 = np.float32
    x = np.asarray(L0_attn_hidden_states, f32)
    xfm = np.ascontiguousarray(x.transpose(0, 2, 1))
    cores = list(range(NCORES))

    identf = np.eye(128, dtype=f32)
    ident64d = np.ascontiguousarray(
        np.concatenate([np.eye(64, dtype=f32)] * 2, axis=0))
    jj = np.arange(128)[:, None] % 64
    ii = np.arange(64)[None, :]
    masku = (jj < ii).astype(np.uint8)

    wffnT = np.ascontiguousarray(np.asarray(L0_ffn_weight, f32).T)
    gammapc = _pc(L0_ffn_gamma)
    betapc = _pc(L0_ffn_beta)

    nc1 = _get_nc("L1")
    in1 = []
    for c in cores:
        in1.append(dict(
            x0=np.ascontiguousarray(xfm[2 * c:2 * c + 2]),
            wffnT=wffnT, gammapc=gammapc, betapc=betapc, identf=identf))
    r1 = run_bass_kernel_spmd(nc1, in1, core_ids=cores).results
    lmax1_all = np.ascontiguousarray(
        np.stack([r1[c]["lmax"][0] for c in cores], axis=1))

    wx2 = np.asarray(L0_attn_W_x2, f32).reshape(D, 5, 32)
    wx2T = np.ascontiguousarray(wx2.transpose(1, 2, 0).reshape(160, D))
    xbias = np.asarray(L0_attn_x_bias, f32)
    xbiaspc = np.ascontiguousarray(
        xbias.reshape(5, NCH, 128).transpose(2, 0, 1))
    nc2 = _get_nc("L2")
    in2 = []
    for c in cores:
        in2.append(dict(
            x0=in1[c]["x0"],
            tern=r1[c]["tern"], alpha=r1[c]["alpha"], rows1=r1[c]["rows1"],
            lmax_all=lmax1_all,
            gammapc=gammapc, betapc=betapc, identf=identf,
            ident64d=ident64d, masku=masku,
            wx0T=np.ascontiguousarray(np.asarray(L0_attn_W_x0, f32).T),
            wx2T=wx2T,
            wrT=np.ascontiguousarray(np.asarray(L0_attn_W_r, f32).T),
            ww1T=np.ascontiguousarray(np.asarray(L0_attn_W_w1, f32).T),
            ww2T=np.ascontiguousarray(np.asarray(L0_attn_W_w2, f32).T),
            wkT=np.ascontiguousarray(np.asarray(L0_attn_W_k, f32).T),
            wvT=np.ascontiguousarray(np.asarray(L0_attn_W_v, f32).T),
            wgT=np.ascontiguousarray(np.asarray(L0_attn_W_g, f32).T),
            woT=np.ascontiguousarray(np.asarray(L0_attn_W_o, f32).T),
            mu0pc=_pc(L0_attn_mu0),
            xbiaspc=xbiaspc,
            bw2pc=_pc(L0_attn_b_w2, 4),
            bonusT=np.ascontiguousarray(np.asarray(L0_attn_bonus, f32).T),
            gnwrow=np.ascontiguousarray(np.stack(
                [np.asarray(L0_attn_gn_weight, f32),
                 np.asarray(L0_attn_gn_bias, f32)])),
            gncolw=_pc(L0_attn_gn_weight),
        ))
    r2 = run_bass_kernel_spmd(nc2, in2, core_ids=cores).results
    lmax2_all = np.ascontiguousarray(
        np.stack([r2[c]["lmax2"][0] for c in cores], axis=1))

    nc3 = _get_nc("L3")
    in3 = []
    for c in cores:
        in3.append(dict(
            x2=r2[c]["x2"], tern=r1[c]["tern"], alpha=r1[c]["alpha"],
            rows3=r2[c]["rows3"], lmax_all=lmax2_all,
            gammapc=gammapc, betapc=betapc))
    r3 = run_bass_kernel_spmd(nc3, in3, core_ids=cores).results

    out_fm = np.concatenate([r3[c]["out"] for c in cores], axis=0)
    return np.ascontiguousarray(out_fm.transpose(0, 2, 1))
